# version 01
# baseline (speedup 1.0000x reference)
import math
import numpy as np
import ml_dtypes
import concourse.bass as bass
import concourse.mybir as mybir
from concourse.bass_utils import run_bass_kernel_spmd
from contextlib import ExitStack

F32 = mybir.dt.float32
BF16 = mybir.dt.bfloat16
AF = mybir.ActivationFunctionType
ALU = mybir.AluOpType
AX = mybir.AxisListType

D = 1024
S = 2048
NH = 16
NT = 16
ND = 8
NG = 4
GH = 4
LN_EPS = 1e-5
RMS_EPS = 1e-6

_CONSTS = None


def _host_consts():
    global _CONSTS
    if _CONSTS is None:
        n = np.arange(D)
        nk = (np.outer(n, n) % D).astype(np.float64)
        theta = 2.0 * np.pi * nk / D
        Cm = (np.cos(theta) / 32.0).astype(ml_dtypes.bfloat16)
        Sm = (np.sin(theta) / 32.0).astype(ml_dtypes.bfloat16)
        ident = np.eye(128, dtype=ml_dtypes.bfloat16)
        tri = (np.arange(128)[None, :] >= np.arange(128)[:, None]).astype(ml_dtypes.bfloat16)
        e2 = np.zeros((128, 2), dtype=ml_dtypes.bfloat16)
        e2[:64, 0] = 1
        e2[64:, 1] = 1
        e2t = np.ascontiguousarray(e2.T)
        e16 = np.zeros((16, ND * 128), dtype=ml_dtypes.bfloat16)
        for c in range(ND):
            for p in range(128):
                e16[2 * c + p // 64, c * 128 + p] = 1
        ones1 = np.ones((1, 128), dtype=ml_dtypes.bfloat16)
        _CONSTS = dict(Cm=Cm, Sm=Sm, ident=ident, tri=tri, e2=e2, e2t=e2t,
                       e16=e16, ones1=ones1)
    return _CONSTS


DMA_KEYS = ("gpd", "spd")
ALL_KEYS = ("pe", "act", "dve", "gpc", "gpd", "spd")


class Ref:
    __slots__ = ("key", "idx")

    def __init__(self, key, idx):
        self.key = key
        self.idx = idx


class Prog:
    def __init__(self):
        self.ops = {k: [] for k in ALL_KEYS}
        self.needed = {k: set() for k in ALL_KEYS}
        self.hazard = {k: set() for k in ALL_KEYS}

    def add(self, key, emit_fn, deps=(), war=(), hazard=False):
        kept = []
        for d in deps:
            if d is None:
                continue
            if d.key == key:
                assert d.idx < len(self.ops[key]), "same-engine forward dep"
            self.needed[d.key].add(d.idx)
            kept.append(d)
        for d in war:
            if d is None:
                continue
            if d.key == key:
                assert d.idx < len(self.ops[key]), "same-engine forward dep"
                continue
            self.needed[d.key].add(d.idx)
            kept.append(d)
        ref = Ref(key, len(self.ops[key]))
        self.ops[key].append((emit_fn, kept))
        return ref

    def finalize(self):
        self.val_after = {}
        for k in ALL_KEYS:
            vals, tot = [], 0
            for i in range(len(self.ops[k])):
                if k in DMA_KEYS:
                    tot += 16
                elif i in self.needed[k]:
                    tot += 1
                vals.append(tot)
            self.val_after[k] = vals

    def emit(self, key, eng, sems):
        last_wait = {k: 0 for k in ALL_KEYS}
        for i, (emit_fn, deps) in enumerate(self.ops[key]):
            for d in deps:
                v = self.val_after[d.key][d.idx]
                if v > last_wait[d.key]:
                    eng.wait_ge(sems[d.key], v)
                    last_wait[d.key] = v
            inst = emit_fn(eng)
            if key in DMA_KEYS:
                inst.then_inc(sems[key], 16)
            elif i in self.needed[key]:
                inst.then_inc(sems[key], 1)

    def last_refs(self):
        return [Ref(k, len(self.ops[k]) - 1) for k in ALL_KEYS if self.ops[k]]


class Slot:

    def __init__(self):
        self.writer = None
        self.readers = []

    def rdeps(self):
        return [self.writer] if self.writer else []

    def wdeps(self):
        return self.readers + ([self.writer] if self.writer else [])

    def read(self, ref):
        self.readers.append(ref)

    def wrote(self, ref):
        self.writer = ref
        self.readers = []

    def wrote_cont(self, ref):
        self.writer = ref


def build_program(taps=(), stop_after=None):
    nc = bass.Bass("TRN2", target_bir_lowering=False)

    d_x = nc.dram_tensor("x", [S, D], F32, kind="ExternalInput").ap()
    d_w = {w: nc.dram_tensor(w, [D, D], F32, kind="ExternalInput").ap()
           for w in ("wqa", "wka", "wqp", "wkp", "wv", "wo")}
    d_sls = nc.dram_tensor("sls", [NH], F32, kind="ExternalInput").ap()
    d_wfr = nc.dram_tensor("wfr", [D, D], F32, kind="ExternalInput").ap()
    d_wfi = nc.dram_tensor("wfi", [D, D], F32, kind="ExternalInput").ap()
    d_bias = nc.dram_tensor("bias", [D], F32, kind="ExternalInput").ap()
    d_C = nc.dram_tensor("Cm", [D, D], BF16, kind="ExternalInput").ap()
    d_S = nc.dram_tensor("Sm", [D, D], BF16, kind="ExternalInput").ap()
    d_ident = nc.dram_tensor("identc", [128, 128], BF16, kind="ExternalInput").ap()
    d_tri = nc.dram_tensor("tric", [128, 128], BF16, kind="ExternalInput").ap()
    d_e2 = nc.dram_tensor("e2c", [128, 2], BF16, kind="ExternalInput").ap()
    d_e2t = nc.dram_tensor("e2tc", [2, 128], BF16, kind="ExternalInput").ap()
    d_e16 = nc.dram_tensor("e16c", [16, ND * 128], BF16, kind="ExternalInput").ap()
    d_ones1 = nc.dram_tensor("ones1c", [1, 128], BF16, kind="ExternalInput").ap()
    d_out = nc.dram_tensor("out", [S, D], F32, kind="ExternalOutput").ap()
    d_cpsc = nc.dram_tensor("cpsc", [NH], F32, kind="Internal").ap()

    P = Prog()
    tap_done = set()

    with ExitStack() as ctx:
        sb = lambda n, s, d: ctx.enter_context(nc.sbuf_tensor(n, s, d))
        psa = lambda n, s: ctx.enter_context(nc.psum_tensor(n, s, F32))

        x_bf = sb("x_bf", [128, NT * D], BF16)
        hfm = sb("hfm", [128, ND * S], BF16)
        identb = sb("identb", [128, 128], BF16)
        trib = sb("trib", [128, 128], BF16)
        e2b = sb("e2b", [128, 2], BF16)
        e2tb = sb("e2tb", [2, 128], BF16)
        e16b = sb("e16b", [16, ND * 128], BF16)
        ones1b = sb("ones1b", [1, 128], BF16)
        slsrow = sb("slsrow", [1, NH], F32)
        crow8 = sb("crow8", [1, NH], F32)
        mnegrow = sb("mnegrow", [1, NH], BF16)
        mneg_bc = sb("mneg_bc", [128, NH], F32)
        cpair = sb("cpair", [2, ND], F32)
        ccol = sb("ccol", [128, 4], F32)
        stg = sb("stg", [65, S], BF16)
        io_stage = sb("io_stage", [128, D], F32)
        ln_s = sb("ln_s", [128, D], F32)
        sq_s = sb("sq_s", [128, D], BF16)
        htm_s = sb("htm_s", [128, D], BF16)
        stat4 = sb("stat4", [128, 4], F32)
        biasrow = sb("biasrow", [1, D], BF16)

        pA = psa("pA", [128, 2048])
        pB = psa("pB", [128, 512])
        pC = psa("pC", [128, 512])
        pD = psa("pD", [128, 1024])

        sems = {k: ctx.enter_context(nc.semaphore("sem_" + k)) for k in ALL_KEYS}

        sl = dict(pA=Slot(), pB=Slot(), pC=Slot(), pD=Slot(),
                  x_bf=Slot(), hfm=Slot(), o_fm=Slot(), stg=Slot(),
                  io=Slot(), ln=Slot(), sq=Slot(), htm=Slot(), stat=Slot(),
                  Lb=Slot(), rlb=Slot())

        def op(key, fn, reads=(), writes=(), writes_cont=(), also=(), hazard=False):
            deps = list(also)
            war = []
            for s in reads:
                deps += s.rdeps()
            for s in writes:
                war += s.wdeps()
            ref = P.add(key, fn, deps, war=war)
            for s in reads:
                s.read(ref)
            for s in writes:
                s.wrote(ref)
            for s in writes_cont:
                s.wrote_cont(ref)
            return ref

        dbg_outs = {}

        def tap(name, ap, also=()):
            if name not in taps or name in tap_done:
                return
            tap_done.add(name)
            shp = [d[1] for d in ap.ap]
            dt = ap.tensor.dtype if hasattr(ap.tensor, "dtype") else F32
            d_dbg = nc.dram_tensor("dbg_" + name, shp, dt, kind="ExternalOutput").ap()
            dbg_outs[name] = d_dbg
            P.add("spd", lambda e, _a=ap, _d=d_dbg: e.dma_start(_d, _a), list(also))

        r_ident = P.add("spd", lambda e: e.dma_start(identb[:, :], d_ident))
        r_tri = P.add("spd", lambda e: e.dma_start(trib[:, :], d_tri))
        r_e2 = P.add("spd", lambda e: e.dma_start(e2b[:, :], d_e2))
        r_e2t = P.add("spd", lambda e: e.dma_start(e2tb[:, :], d_e2t))
        r_e16 = P.add("spd", lambda e: e.dma_start(e16b[:, :], d_e16))
        r_ones1 = P.add("spd", lambda e: e.dma_start(ones1b[:, :], d_ones1))
        r_sls = P.add("spd", lambda e: e.dma_start(slsrow[:, :], d_sls.rearrange("(a b) -> a b", a=1)))
        r_biasrow = P.add("gpd", lambda e: e.dma_start(biasrow[:, :], d_bias.rearrange("(a b) -> a b", a=1)))

        r_cc = P.add("dve", lambda e: e.memset(ccol[:, 0:1], 1.0))
        r_cc = P.add("dve", lambda e: e.memset(ccol[:, 1:2], RMS_EPS))
        r_cc = P.add("dve", lambda e: e.memset(ccol[:, 2:3], LN_EPS))
        r_ccol = P.add("dve", lambda e: e.memset(ccol[:, 3:4], -math.pi / 2))

        r_crow = P.add("act", lambda e: e.activation(crow8[:, :], slsrow[:, :], AF.Exp), [r_sls])
        r_crow = P.add("dve", lambda e: e.tensor_scalar_mul(crow8[:, :], crow8[:, :], 0.125), [r_crow])
        r_mneg = P.add("dve", lambda e: e.tensor_scalar_mul(mnegrow[:, :], crow8[:, :], -64.0), [r_crow])
        r_mm = op("pe", lambda e: e.matmul(pB[:, 0:NH], lhsT=ones1b[:, :], rhs=mnegrow[:, :],
                                           start=True, stop=True),
                  writes=[sl["pB"]], also=[r_mneg, r_ones1])
        r_mnegbc = op("dve", lambda e: e.tensor_copy(mneg_bc[:, :], pB[:, 0:NH]),
                      reads=[sl["pB"]])
        r_cpd = P.add("gpd", lambda e: e.dma_start(d_cpsc.rearrange("(a b) -> a b", a=1), crow8[:, :]), [r_crow])
        r_cpair = P.add("gpd", lambda e: e.dma_start(
            cpair[:, :], d_cpsc.rearrange("(cc p) -> p cc", p=2),
            allow_slow_non_contiguous=True), [r_cpd])

        r_xload = [P.add("gpd", lambda e, t=t: e.dma_start(
            x_bf[:, t * D:(t + 1) * D], d_x[t * 128:(t + 1) * 128, :])) for t in range(NT)]

        def emit_ln_stats_apply(src_ap, dst_ap, src_slot, dst_slot, extra=()):
            r1 = op("dve", lambda e: e.reduce_sum(stat4[:, 0:1], src_ap, axis=AX.X),
                    reads=[src_slot], writes=[sl["stat"]], also=list(extra), hazard=True)
            r2 = op("dve", lambda e: e.tensor_scalar_mul(stat4[:, 0:1], stat4[:, 0:1], 1.0 / D),
                    writes=[sl["stat"]], also=[r1])
            r3 = op("act", lambda e: e.activation(
                sq_s[:, :], src_ap, AF.Square, accum_out=stat4[:, 1:2]),
                reads=[src_slot], writes=[sl["sq"], sl["stat"]], hazard=True)
            r4 = op("dve", lambda e: e.tensor_tensor(stat4[:, 3:4], stat4[:, 0:1], stat4[:, 0:1], ALU.mult),
                    writes=[sl["stat"]], also=[r2])
            r5 = op("dve", lambda e: e.scalar_tensor_tensor(
                stat4[:, 2:3], stat4[:, 1:2], 1.0 / D, stat4[:, 3:4], ALU.mult, ALU.subtract),
                writes=[sl["stat"]], also=[r3, r4])
            r6 = op("act", lambda e: e.activation(stat4[:, 2:3], stat4[:, 2:3], AF.Ln, bias=ccol[:, 2:3]),
                    writes=[sl["stat"]], also=[r5, r_ccol])
            r7 = op("act", lambda e: e.activation(stat4[:, 2:3], stat4[:, 2:3], AF.Exp, scale=-0.5),
                    writes=[sl["stat"]], also=[r6])
            r8 = op("dve", lambda e: e.tensor_scalar(
                dst_ap, src_ap, stat4[:, 0:1], stat4[:, 2:3], ALU.subtract, ALU.mult),
                reads=[src_slot, sl["stat"]], writes=[dst_slot], also=[r7])
            return r8

        def emit_transpose_tm_to_fm(src_tile, dst_tensor, t, src_slot, dst_slot, extra=()):
            refs = []
            for half in range(2):
                pslot = sl["pB"] if half == 0 else sl["pC"]
                ptile = pB if half == 0 else pC
                for q in range(4):
                    dc = half * 4 + q
                    op("pe", lambda e, dc=dc, q=q, ptile=ptile: e.matmul(
                        ptile[:, q * 128:(q + 1) * 128],
                        lhsT=src_tile[:, dc * 128:(dc + 1) * 128],
                        rhs=identb[:, :], start=True, stop=True),
                        reads=[src_slot],
                        writes=[pslot] if q == 0 else [],
                        writes_cont=[pslot] if q > 0 else [],
                        also=list(extra) + ([r_ident] if q == 0 else []))
                dst = bass.AP(dst_tensor[:, :].tensor, (half * 4) * S + t * 128,
                              [dst_tensor[:, :].ap[0], [S, 4], [1, 128]])
                rev = op("act", lambda e, dst=dst, ptile=ptile: e.activation(dst, ptile[:, :], AF.Copy),
                         reads=[pslot], writes=[dst_slot])
                refs.append(rev)
            return refs

        for t in range(NT):
            r_app = emit_ln_stats_apply(x_bf[:, t * D:(t + 1) * D], htm_s[:, :],
                                        sl["x_bf"], sl["htm"], extra=[r_xload[t]])
            emit_transpose_tm_to_fm(htm_s, hfm, t, sl["htm"], sl["hfm"])

        tap("h0fm", hfm[:, :], also=P.last_refs())
        if stop_after == "phase0":
            return _finish(nc, P, sems, dbg_outs)

        attn_ctx = ctx.enter_context(ExitStack())
        sba = lambda n, s, d: attn_ctx.enter_context(nc.sbuf_tensor(n, s, d))
        o_fm = sba("o_fm", [128, ND * S], BF16)
        Lb = sba("Lb", [NH, S], BF16)
        rlb = sba("rlb", [NH, S], BF16)
        Qcs = sba("Qcs", [128, GH * S], BF16)
        Kcs = sba("Kcs", [128, GH * S], BF16)
        V_aug = sba("V_aug", [128, NT * GH * 65], BF16)
        WT = {w: sba("WT_" + w, [128, ND * 256], BF16)
              for w in ("wqa", "wka", "wqp", "wkp", "wv")}
        Wstage = sba("Wstage", [128, 2 * D], BF16)
        qx = sba("qx", [128, S], BF16)
        Pt = sba("Pt", [128, 1024], BF16)
        sp_e = sba("sp_e", [128, 512], F32)
        qa_s = sba("qa_s", [128, 512], BF16)
        q2_s = sba("q2_s", [128, 512], BF16)
        t_s = sba("t_s", [128, 512], BF16)
        tn_s = sba("tn_s", [128, 512], BF16)
        ta_s = tn_s
        sinv_s = sba("sinv_s", [128, 512], BF16)
        cosn_s = sba("cosn_s", [128, 512], BF16)
        t1_s = sba("t1_s", [128, 512], BF16)
        rstd2_s = sba("rstd2_s", [2, 512], F32)
        rstd2b_s = sba("rstd2b_s", [2, 512], BF16)

        for nm in ("Qcs", "Kcs", "V_aug", "Wstage", "qx", "Pt0", "Pt1",
                   "sp_e", "qa_s", "q2_s", "t_s", "tn_s",
                   "sinv_s", "cosn_s", "t1_s", "rstd2"):
            sl[nm] = Slot()
        for w in WT:
            sl["WT_" + w] = Slot()

        va = V_aug[:, :]
        va_ones = bass.AP(va.tensor, 64, [va.ap[0], [260, NT], [65, GH], [0, 1]])
        op("dve", lambda e: e.memset(va_ones, 1.0), writes=[sl["V_aug"]])

        def emit_weight_slices(g):
            for w in ("wqa", "wka", "wqp", "wkp", "wv"):
                rl_ = []
                for fc in range(2):
                    rl_.append(op("gpd", lambda e, w=w, fc=fc, g=g: e.dma_start(
                        Wstage[:, fc * D:(fc + 1) * D],
                        d_w[w][g * 256 + fc * 128: g * 256 + (fc + 1) * 128, :]),
                        writes=[sl["Wstage"]]))
                for grp4 in range(4):
                    pslot = sl["pB"] if grp4 % 2 == 0 else sl["pC"]
                    ptile = pB if grp4 % 2 == 0 else pC
                    for pos in range(4):
                        idx = grp4 * 4 + pos
                        dc, fc = idx // 2, idx % 2
                        op("pe", lambda e, dc=dc, fc=fc, pos=pos, ptile=ptile: e.matmul(
                            ptile[:, pos * 128:(pos + 1) * 128],
                            lhsT=Wstage[:, fc * D + dc * 128: fc * D + (dc + 1) * 128],
                            rhs=identb[:, :], start=True, stop=True),
                            reads=[sl["Wstage"]],
                            writes=[pslot] if pos == 0 else [],
                            writes_cont=[pslot] if pos > 0 else [],
                            also=rl_ if idx == 0 else [])
                    op("dve", lambda e, w=w, grp4=grp4, ptile=ptile: e.tensor_copy(
                        WT[w][:, grp4 * 512:(grp4 + 1) * 512], ptile[:, :]),
                        reads=[pslot], writes=[sl["WT_" + w]])

        def emit_proj_pair(g, wamp, wphi, dst, dst_name, scale_c):
            for c in range(2):
                for js in range(4):
                    for dc in range(ND):
                        op("pe", lambda e, dc=dc, c=c, js=js, wamp=wamp: e.matmul(
                            pB[:, :], lhsT=WT[wamp][:, dc * 256 + c * 128: dc * 256 + (c + 1) * 128],
                            rhs=hfm[:, dc * S + js * 512: dc * S + (js + 1) * 512],
                            start=(dc == 0), stop=(dc == ND - 1)),
                            reads=[sl["WT_" + wamp], sl["hfm"]],
                            writes=[sl["pB"]] if dc == 0 else [],
                            writes_cont=[sl["pB"]] if dc > 0 else [])
                    for dc in range(ND):
                        op("pe", lambda e, dc=dc, c=c, js=js, wphi=wphi: e.matmul(
                            pC[:, :], lhsT=WT[wphi][:, dc * 256 + c * 128: dc * 256 + (c + 1) * 128],
                            rhs=hfm[:, dc * S + js * 512: dc * S + (js + 1) * 512],
                            start=(dc == 0), stop=(dc == ND - 1)),
                            reads=[sl["WT_" + wphi], sl["hfm"]],
                            writes=[sl["pC"]] if dc == 0 else [],
                            writes_cont=[sl["pC"]] if dc > 0 else [])
                    op("act", lambda e: e.activation(sp_e[:, :], pB[:, :], AF.Exp),
                       reads=[sl["pB"]], writes=[sl["sp_e"]])
                    op("act", lambda e: e.activation(qa_s[:, :], sp_e[:, :], AF.Ln, bias=ccol[:, 0:1]),
                       reads=[sl["sp_e"]], writes=[sl["qa_s"]], also=[r_ccol])
                    op("dve", lambda e: e.tensor_mul(q2_s[:, :], qa_s[:, :], qa_s[:, :]),
                       reads=[sl["qa_s"]], writes=[sl["q2_s"]])
                    op("pe", lambda e: e.matmul(pD[0:2, 0:512], lhsT=e2b[:, :], rhs=q2_s[:, :],
                                                start=True, stop=True),
                       reads=[sl["q2_s"]], writes=[sl["pD"]], also=[r_e2])
                    op("act", lambda e: e.activation(rstd2_s[:, :], pD[0:2, 0:512], AF.Ln,
                                                     scale=1.0 / 64.0, bias=ccol[0:2, 1:2]),
                       reads=[sl["pD"]], writes=[sl["rstd2"]])
                    op("act", lambda e: e.activation(rstd2b_s[:, :], rstd2_s[:, :], AF.Exp, scale=-0.5),
                       writes=[sl["rstd2"]])
                    if scale_c:
                        op("dve", lambda e, g=g, c=c: e.tensor_scalar_mul(
                            rstd2b_s[:, :], rstd2b_s[:, :], cpair[:, g * 2 + c: g * 2 + c + 1]),
                           writes=[sl["rstd2"]], also=[r_cpair])
                    op("pe", lambda e: e.matmul(pD[:, 512:1024], lhsT=e2tb[:, :], rhs=rstd2b_s[:, :],
                                                start=True, stop=True),
                       reads=[sl["rstd2"]], writes=[sl["pD"]], also=[r_e2t])
                    op("act", lambda e: e.activation(t_s[:, :], pC[:, :], AF.Tanh),
                       reads=[sl["pC"]], writes=[sl["t_s"]])
                    op("dve", lambda e: e.tensor_scalar_mul(tn_s[:, :], t_s[:, :], -1.0),
                       reads=[sl["t_s"]], writes=[sl["tn_s"]])
                    op("dve", lambda e: e.tensor_max(tn_s[:, :], t_s[:, :], tn_s[:, :]),
                       reads=[sl["t_s"]], writes=[sl["tn_s"]])
                    op("act", lambda e: e.activation(sinv_s[:, :], t_s[:, :], AF.Sin, scale=math.pi),
                       reads=[sl["t_s"]], writes=[sl["sinv_s"]])
                    op("act", lambda e: e.activation(cosn_s[:, :], tn_s[:, :], AF.Sin,
                                                     scale=math.pi, bias=ccol[:, 3:4]),
                       reads=[sl["tn_s"]], writes=[sl["cosn_s"]])
                    op("dve", lambda e: e.tensor_tensor(t1_s[:, :], qa_s[:, :], pD[:, 512:1024], ALU.mult),
                       reads=[sl["qa_s"], sl["pD"]], writes=[sl["t1_s"]])
                    lo0 = (2 * c) * S + js * 512
                    lo1 = (2 * c + 1) * S + js * 512
                    op("dve", lambda e, lo0=lo0: e.scalar_tensor_tensor(
                        dst[0:64, lo0:lo0 + 512], cosn_s[0:64, :], -1.0, t1_s[0:64, :],
                        ALU.mult, ALU.mult),
                       reads=[sl["cosn_s"], sl["t1_s"]], writes=[sl[dst_name]])
                    op("dve", lambda e, js=js: e.scalar_tensor_tensor(
                        qx[64:128, js * 512:(js + 1) * 512], cosn_s[64:128, :], -1.0,
                        t1_s[64:128, :], ALU.mult, ALU.mult),
                       reads=[sl["cosn_s"], sl["t1_s"]], writes=[sl["qx"]])
                    op("dve", lambda e, js=js: e.tensor_mul(
                        qx[0:64, js * 512:(js + 1) * 512], t1_s[0:64, :], sinv_s[0:64, :]),
                       reads=[sl["sinv_s"], sl["t1_s"]], writes=[sl["qx"]])
                    op("dve", lambda e, lo1=lo1: e.tensor_mul(
                        dst[64:128, lo1:lo1 + 512], t1_s[64:128, :], sinv_s[64:128, :]),
                       reads=[sl["sinv_s"], sl["t1_s"]], writes=[sl[dst_name]])
                op("gpd", lambda e, c=c: e.dma_start(dst[0:64, (2 * c + 1) * S:(2 * c + 2) * S],
                                                     qx[64:128, :]),
                   reads=[sl["qx"]], writes=[sl[dst_name]])
                op("gpd", lambda e, c=c: e.dma_start(dst[64:128, (2 * c) * S:(2 * c) * S + S],
                                                     qx[0:64, :]),
                   reads=[sl["qx"]], writes=[sl[dst_name]])

        def emit_vproj(g):
            for t in range(NT):
                for dc in range(ND):
                    op("pe", lambda e, dc=dc, t=t: e.matmul(
                        pB[:, 0:256], lhsT=hfm[:, dc * S + t * 128: dc * S + (t + 1) * 128],
                        rhs=WT["wv"][:, dc * 256:(dc + 1) * 256],
                        start=(dc == 0), stop=(dc == ND - 1)),
                        reads=[sl["hfm"], sl["WT_wv"]],
                        writes=[sl["pB"]] if dc == 0 else [],
                        writes_cont=[sl["pB"]] if dc > 0 else [])
                dstp = bass.AP(va.tensor, t * 260, [va.ap[0], [65, GH], [1, 64]])
                srcp = pB[:, 0:256].rearrange("p (a b) -> p a b", a=GH)
                op("dve", lambda e, dstp=dstp, srcp=srcp: e.tensor_copy(dstp, srcp),
                   reads=[sl["pB"]], writes=[sl["V_aug"]])

        def emit_attention(g):
            for hl in range(GH):
                h = g * GH + hl
                it = 0
                for ki in range(NT):
                    blocks = [(128 * ki, 512 - 128 * (ki % 4))]
                    blocks += [(512 * b, 512) for b in range(ki // 4 + 1, 4)]
                    for bi, (qs, w) in enumerate(blocks):
                        stile = pB if it % 2 == 0 else pC
                        sslot = sl["pB"] if it % 2 == 0 else sl["pC"]
                        ptslot = sl["Pt0"] if it % 2 == 0 else sl["Pt1"]
                        ptoff = 0 if it % 2 == 0 else 512
                        r_st = op("pe", lambda e, hl=hl, ki=ki, qs=qs, w=w, stile=stile: e.matmul(
                            stile[:, 0:w], lhsT=Kcs[:, hl * S + ki * 128: hl * S + (ki + 1) * 128],
                            rhs=Qcs[:, hl * S + qs: hl * S + qs + w], start=True, stop=True),
                            reads=[sl["Kcs"], sl["Qcs"]], writes=[sslot])
                        r_exp = op("act", lambda e, w=w, h=h, ptoff=ptoff, stile=stile: e.activation(
                            Pt[:, ptoff:ptoff + w], stile[:, 0:w], AF.Exp, bias=mneg_bc[:, h:h + 1]),
                            reads=[sslot], writes=[ptslot], also=[r_mnegbc])
                        r_last = r_exp
                        if bi == 0:
                            r_last = op("dve", lambda e, ptoff=ptoff: e.tensor_mul(
                                Pt[:, ptoff:ptoff + 128], Pt[:, ptoff:ptoff + 128], trib[:, :]),
                                writes=[ptslot], also=[r_tri])
                        bank_last = min(15, 4 * ((qs + w - 1) // 512) + 3)
                        op("pe", lambda e, ki=ki, hl=hl, qs=qs, w=w, ptoff=ptoff,
                           bank_last=bank_last: e.matmul(
                            pA[0:65, qs:qs + w],
                            lhsT=V_aug[:, ki * 260 + hl * 65: ki * 260 + (hl + 1) * 65],
                            rhs=Pt[:, ptoff:ptoff + w],
                            start=(ki == 0), stop=(ki == bank_last), skip_group_check=True),
                            reads=[ptslot, sl["V_aug"]],
                            writes=[sl["pA"]] if (ki == 0 and bi == 0) else [],
                            writes_cont=[sl["pA"]] if not (ki == 0 and bi == 0) else [],
                            also=[r_last])
                        it += 1
                r_ev = op("act", lambda e: e.activation(stg[:, :], pA[0:65, :], AF.Copy),
                          reads=[sl["pA"]], writes=[sl["stg"]])
                c, half = h // 2, h % 2
                op("gpd", lambda e, c=c, half=half: e.dma_start(
                    o_fm[64 * half:64 * half + 64, c * S:(c + 1) * S], stg[0:64, :]),
                   reads=[sl["stg"]], writes=[sl["o_fm"]])
                op("gpd", lambda e, h=h: e.dma_start(Lb[h:h + 1, :], stg[64:65, :]),
                   reads=[sl["stg"]], writes=[sl["Lb"]])

        for g in range(NG):
            emit_weight_slices(g)
            emit_proj_pair(g, "wqa", "wqp", Qcs, "Qcs", True)
            emit_proj_pair(g, "wka", "wkp", Kcs, "Kcs", False)
            emit_vproj(g)
            if g == 0:
                tap("Qcs0", Qcs[:, :], also=P.last_refs())
                tap("Kcs0", Kcs[:, :], also=P.last_refs())
                tap("Vaug0", V_aug[:, :], also=P.last_refs())
            emit_attention(g)
            if g == 0:
                tap("stg0", stg[:, :], also=P.last_refs())
            if stop_after == "group%d" % g:
                tap("o_fm", o_fm[:, :], also=P.last_refs())
                tap("Lb", Lb[:, :], also=P.last_refs())
                return _finish(nc, P, sems, dbg_outs)

        tap("o_fm", o_fm[:, :], also=P.last_refs())
        tap("Lb", Lb[:, :], also=P.last_refs())

        def _emit_recip(e):
            with nc.allow_low_precision(reason="1/l stored bf16; same scale as bf16 matmul rounding"):
                return e.reciprocal(rlb[:, :], Lb[:, :])
        op("dve", _emit_recip, reads=[sl["Lb"]], writes=[sl["rlb"]], hazard=True)
        for c in range(ND):
            for js in range(4):
                pslot = sl["pB"] if (c * 4 + js) % 2 == 0 else sl["pC"]
                ptile = pB if (c * 4 + js) % 2 == 0 else pC
                op("pe", lambda e, c=c, js=js, ptile=ptile: e.matmul(
                    ptile[:, :], lhsT=e16b[:, c * 128:(c + 1) * 128],
                    rhs=rlb[:, js * 512:(js + 1) * 512], start=True, stop=True),
                   reads=[sl["rlb"]], writes=[pslot], also=[r_e16])
                op("dve", lambda e, c=c, js=js, ptile=ptile: e.tensor_tensor(
                    o_fm[:, c * S + js * 512: c * S + (js + 1) * 512],
                    o_fm[:, c * S + js * 512: c * S + (js + 1) * 512],
                    ptile[:, :], ALU.mult),
                   reads=[pslot], writes=[sl["o_fm"]])

        tap("o_norm", o_fm[:, :], also=P.last_refs())

        WoT = Qcs
        sl["WoT"] = sl["Qcs"]
        for fcpair in range(4):
            rl_ = []
            for k in range(2):
                foc = fcpair * 2 + k
                rl_.append(op("gpd", lambda e, foc=foc, k=k: e.dma_start(
                    Wstage[:, k * D:(k + 1) * D], d_w["wo"][foc * 128:(foc + 1) * 128, :]),
                    writes=[sl["Wstage"]]))
            for grp4 in range(4):
                pslot = sl["pB"] if grp4 % 2 == 0 else sl["pC"]
                ptile = pB if grp4 % 2 == 0 else pC
                evs = []
                for pos in range(4):
                    idx = grp4 * 4 + pos
                    k, fc = idx // 8, idx % 8
                    op("pe", lambda e, k=k, fc=fc, pos=pos, ptile=ptile: e.matmul(
                        ptile[:, pos * 128:(pos + 1) * 128],
                        lhsT=Wstage[:, k * D + fc * 128: k * D + (fc + 1) * 128],
                        rhs=identb[:, :], start=True, stop=True),
                        reads=[sl["Wstage"]],
                        writes=[pslot] if pos == 0 else [],
                        writes_cont=[pslot] if pos > 0 else [],
                        also=rl_ if idx == 0 else [])
                    evs.append((k, fc, pos))
                for (k, fc, pos) in evs:
                    foc = fcpair * 2 + k
                    op("dve", lambda e, fc=fc, foc=foc, pos=pos, ptile=ptile: e.tensor_copy(
                        WoT[:, fc * D + foc * 128: fc * D + (foc + 1) * 128],
                        ptile[:, pos * 128:(pos + 1) * 128]),
                        reads=[pslot], writes=[sl["WoT"]])

        for t in range(NT):
            use_pD = (t % 2 == 0)
            ptile = pD if use_pD else pA
            pslot = sl["pD"] if use_pD else sl["pA"]
            for js in range(2):
                for c in range(ND):
                    first_op = (js == 0 and c == 0)
                    op("pe", lambda e, c=c, js=js, t=t, ptile=ptile: e.matmul(
                        ptile[:, js * 512:(js + 1) * 512],
                        lhsT=o_fm[:, c * S + t * 128: c * S + (t + 1) * 128],
                        rhs=WoT[:, c * D + js * 512: c * D + (js + 1) * 512],
                        start=(c == 0), stop=(c == ND - 1)),
                        reads=[sl["o_fm"], sl["WoT"]],
                        writes=[pslot] if first_op else [],
                        writes_cont=[pslot] if not first_op else [])
            r_ev = op("act", lambda e, ptile=ptile: e.activation(
                ln_s[:, :], ptile[:, 0:1024], AF.Copy, accum_out=stat4[:, 0:1]),
                reads=[pslot], writes=[sl["ln"], sl["stat"]], hazard=True)
            r2 = op("dve", lambda e: e.tensor_scalar_mul(stat4[:, 0:1], stat4[:, 0:1], 1.0 / D),
                    writes=[sl["stat"]], also=[r_ev])
            r3 = op("act", lambda e: e.activation(
                sq_s[:, :], ln_s[:, :], AF.Square, accum_out=stat4[:, 1:2]),
                reads=[sl["ln"]], writes=[sl["sq"], sl["stat"]], hazard=True)
            r4 = op("dve", lambda e: e.tensor_tensor(stat4[:, 3:4], stat4[:, 0:1], stat4[:, 0:1], ALU.mult),
                    writes=[sl["stat"]], also=[r2])
            r5 = op("dve", lambda e: e.scalar_tensor_tensor(
                stat4[:, 2:3], stat4[:, 1:2], 1.0 / D, stat4[:, 3:4], ALU.mult, ALU.subtract),
                writes=[sl["stat"]], also=[r3, r4])
            r6 = op("act", lambda e: e.activation(stat4[:, 2:3], stat4[:, 2:3], AF.Ln, bias=ccol[:, 2:3]),
                    writes=[sl["stat"]], also=[r5])
            r7 = op("act", lambda e: e.activation(stat4[:, 2:3], stat4[:, 2:3], AF.Exp, scale=-0.5),
                    writes=[sl["stat"]], also=[r6])
            r8 = op("dve", lambda e: e.tensor_scalar(
                ln_s[:, :], ln_s[:, :], stat4[:, 0:1], stat4[:, 2:3], ALU.subtract, ALU.mult),
                reads=[sl["stat"]], writes=[sl["ln"]], also=[r7])
            op("dve", lambda e, t=t: e.tensor_tensor(
                x_bf[:, t * D:(t + 1) * D], x_bf[:, t * D:(t + 1) * D], ln_s[:, :], ALU.add),
               reads=[sl["ln"]], writes=[sl["x_bf"]])
            emit_ln_stats_apply(x_bf[:, t * D:(t + 1) * D], htm_s[:, :], sl["x_bf"], sl["htm"])
            emit_transpose_tm_to_fm(htm_s, hfm, t, sl["htm"], sl["hfm"])

        tap("x1", x_bf[:, :], also=P.last_refs())
        tap("h1fm", hfm[:, :], also=P.last_refs())
        if stop_after == "phaseC":
            return _finish(nc, P, sems, dbg_outs)

        barrier = P.last_refs()
        attn_ctx.close()
        sbm = lambda n, s, d: ctx.enter_context(nc.sbuf_tensor(n, s, d))
        Wr = sbm("Wr", [128, ND * D], BF16)
        Wi = sbm("Wi", [128, ND * D], BF16)
        Csb = sbm("Csb", [128, ND * D], BF16)
        Ssb = sbm("Ssb", [128, ND * D], BF16)
        A_T = sbm("A_T", [128, ND * D], BF16)
        B_Tn = sbm("B_Tn", [128, ND * D], BF16)
        Mb = sbm("Mb", [128, ND * D], BF16)
        for nm in ("Wr", "Wi", "Csb", "Ssb", "A_T", "B_Tn", "Mb"):
            sl[nm] = Slot()

        r_wr, r_wi, r_cs, r_ss = [], [], [], []
        for dc in range(ND):
            r_wr.append(P.add("gpd", lambda e, dc=dc: e.dma_start(
                Wr[:, dc * D:(dc + 1) * D], d_wfr[dc * 128:(dc + 1) * 128, :]), barrier))
            r_wi.append(P.add("gpd", lambda e, dc=dc: e.dma_start(
                Wi[:, dc * D:(dc + 1) * D], d_wfi[dc * 128:(dc + 1) * 128, :]), barrier))
            r_cs.append(P.add("spd", lambda e, dc=dc: e.dma_start(
                Csb[:, dc * D:(dc + 1) * D], d_C[dc * 128:(dc + 1) * 128, :]), barrier))
            r_ss.append(P.add("spd", lambda e, dc=dc: e.dma_start(
                Ssb[:, dc * D:(dc + 1) * D], d_S[dc * 128:(dc + 1) * 128, :]), barrier))
        for r in r_wr + r_wi:
            sl["Wr" if r in r_wr else "Wi"].wrote(r)
        for r in r_cs:
            sl["Csb"].wrote(r)
        for r in r_ss:
            sl["Ssb"].wrote(r)

        def emit_MM_accum(dst, dst_name, lhs_list):
            nterm = len(lhs_list)
            for outc in range(ND):
                first_ref = None
                for js in range(2):
                    for ti, (lt, lslot, rt, rslot) in enumerate(lhs_list):
                        for dc in range(ND):
                            first = (ti == 0 and dc == 0)
                            last = (ti == nterm - 1 and dc == ND - 1)
                            is_first_op = (js == 0 and first)
                            op("pe", lambda e, lt=lt, rt=rt, dc=dc, outc=outc, js=js,
                               first=first, last=last: e.matmul(
                                   pD[:, js * 512:(js + 1) * 512],
                                   lhsT=lt[:, dc * D + outc * 128: dc * D + (outc + 1) * 128],
                                   rhs=rt[:, dc * D + js * 512: dc * D + (js + 1) * 512],
                                   start=first, stop=last, skip_group_check=True),
                               reads=[lslot, rslot],
                               writes=[sl["pD"]] if is_first_op else [],
                               writes_cont=[sl["pD"]] if not is_first_op else [])
                op("dve", lambda e, outc=outc: e.tensor_copy(
                    dst[:, outc * D:(outc + 1) * D], pD[:, :]),
                    reads=[sl["pD"]], writes=[sl[dst_name]])

        emit_MM_accum(A_T, "A_T", [(Wr, sl["Wr"], Csb, sl["Csb"]),
                                   (Wi, sl["Wi"], Ssb, sl["Ssb"])])
        op("dve", lambda e: e.tensor_scalar_mul(Wi[:, :], Wi[:, :], -1.0),
           writes=[sl["Wi"]])
        emit_MM_accum(B_Tn, "B_Tn", [(Wr, sl["Wr"], Ssb, sl["Ssb"]),
                                     (Wi, sl["Wi"], Csb, sl["Csb"])])
        emit_MM_accum(Mb, "Mb", [(A_T, sl["A_T"], Csb, sl["Csb"]),
                                 (B_Tn, sl["B_Tn"], Ssb, sl["Ssb"])])
        tap("Mb", Mb[:, :], also=P.last_refs())

        for t in range(NT):
            use_pD = (t % 2 == 0)
            ptile = pD if use_pD else pA
            pslot = sl["pD"] if use_pD else sl["pA"]
            for js in range(2):
                for dc in range(ND):
                    first_op = (js == 0 and dc == 0)
                    op("pe", lambda e, dc=dc, js=js, t=t, ptile=ptile: e.matmul(
                        ptile[:, js * 512:(js + 1) * 512],
                        lhsT=hfm[:, dc * S + t * 128: dc * S + (t + 1) * 128],
                        rhs=Mb[:, dc * D + js * 512: dc * D + (js + 1) * 512],
                        start=(dc == 0), stop=False, skip_group_check=True),
                        reads=[sl["hfm"], sl["Mb"]],
                        writes=[pslot] if first_op else [],
                        writes_cont=[pslot] if not first_op else [])
                op("pe", lambda e, js=js, ptile=ptile: e.matmul(
                    ptile[:, js * 512:(js + 1) * 512], lhsT=ones1b[:, :],
                    rhs=biasrow[:, js * 512:(js + 1) * 512],
                    start=False, stop=True, skip_group_check=True),
                    writes_cont=[pslot], also=[r_biasrow, r_ones1])
            r_ev = op("act", lambda e, ptile=ptile: e.activation(
                ln_s[:, :], ptile[:, 0:1024], AF.Copy, accum_out=stat4[:, 0:1]),
                reads=[pslot], writes=[sl["ln"], sl["stat"]], hazard=True)
            r2 = op("dve", lambda e: e.tensor_scalar_mul(stat4[:, 0:1], stat4[:, 0:1], 1.0 / D),
                    writes=[sl["stat"]], also=[r_ev])
            r3 = op("act", lambda e: e.activation(
                sq_s[:, :], ln_s[:, :], AF.Square, accum_out=stat4[:, 1:2]),
                reads=[sl["ln"]], writes=[sl["sq"], sl["stat"]], hazard=True)
            r4 = op("dve", lambda e: e.tensor_tensor(stat4[:, 3:4], stat4[:, 0:1], stat4[:, 0:1], ALU.mult),
                    writes=[sl["stat"]], also=[r2])
            r5 = op("dve", lambda e: e.scalar_tensor_tensor(
                stat4[:, 2:3], stat4[:, 1:2], 1.0 / D, stat4[:, 3:4], ALU.mult, ALU.subtract),
                writes=[sl["stat"]], also=[r3, r4])
            r6 = op("act", lambda e: e.activation(stat4[:, 2:3], stat4[:, 2:3], AF.Ln, bias=ccol[:, 2:3]),
                    writes=[sl["stat"]], also=[r5])
            r7 = op("act", lambda e: e.activation(stat4[:, 2:3], stat4[:, 2:3], AF.Exp, scale=-0.5),
                    writes=[sl["stat"]], also=[r6])
            r8 = op("dve", lambda e: e.tensor_scalar(
                ln_s[:, :], ln_s[:, :], stat4[:, 0:1], stat4[:, 2:3], ALU.subtract, ALU.mult),
                reads=[sl["stat"]], writes=[sl["ln"]], also=[r7])
            op("dve", lambda e, t=t: e.tensor_tensor(
                io_stage[:, :], x_bf[:, t * D:(t + 1) * D], ln_s[:, :], ALU.add),
               reads=[sl["x_bf"], sl["ln"]], writes=[sl["io"]])
            op("spd", lambda e, t=t: e.dma_start(d_out[t * 128:(t + 1) * 128, :], io_stage[:, :]),
               reads=[sl["io"]])

        return _finish(nc, P, sems, dbg_outs)


def _finish(nc, P, sems, dbg_outs):
    P.finalize()
    with nc.Block() as block:
        @block.tensor
        def _(eng):
            P.emit("pe", eng, sems)

        @block.scalar
        def _(eng):
            P.emit("act", eng, sems)

        @block.vector
        def _(eng):
            P.emit("dve", eng, sems)

        @block.gpsimd
        def _(eng):
            P.emit("gpc", eng, sems)
            P.emit("gpd", eng, sems)
            if P.ops["gpd"]:
                eng.wait_ge(sems["gpd"], P.val_after["gpd"][-1])

        @block.sync
        def _(eng):
            P.emit("spd", eng, sems)
            if P.ops["spd"]:
                eng.wait_ge(sems["spd"], P.val_after["spd"][-1])

    return nc


def kernel(L0_attn_x, L0_attn_Wq_amp, L0_attn_Wk_amp, L0_attn_Wq_phi,
           L0_attn_Wk_phi, L0_attn_Wv, L0_attn_Wo, L0_attn_score_log_scale,
           L0_ffn_w_freq_real, L0_ffn_w_freq_imag, L0_ffn_bias):
    cst = _host_consts()
    nc = build_program()
    shared = {
        "wqa": np.ascontiguousarray(L0_attn_Wq_amp, dtype=np.float32),
        "wka": np.ascontiguousarray(L0_attn_Wk_amp, dtype=np.float32),
        "wqp": np.ascontiguousarray(L0_attn_Wq_phi, dtype=np.float32),
        "wkp": np.ascontiguousarray(L0_attn_Wk_phi, dtype=np.float32),
        "wv": np.ascontiguousarray(L0_attn_Wv, dtype=np.float32),
        "wo": np.ascontiguousarray(L0_attn_Wo, dtype=np.float32),
        "sls": np.ascontiguousarray(L0_attn_score_log_scale, dtype=np.float32),
        "wfr": np.ascontiguousarray(L0_ffn_w_freq_real, dtype=np.float32),
        "wfi": np.ascontiguousarray(L0_ffn_w_freq_imag, dtype=np.float32),
        "bias": np.ascontiguousarray(L0_ffn_bias, dtype=np.float32),
        "Cm": cst["Cm"], "Sm": cst["Sm"], "identc": cst["ident"],
        "tric": cst["tri"], "e2c": cst["e2"], "e2tc": cst["e2t"],
        "e16c": cst["e16"], "ones1c": cst["ones1"],
    }
    x = np.ascontiguousarray(L0_attn_x, dtype=np.float32)
    in_maps = [dict(shared, x=np.ascontiguousarray(x[b])) for b in range(8)]
    res = run_bass_kernel_spmd(nc, in_maps, core_ids=list(range(8)))
    return np.stack([np.asarray(r["out"], dtype=np.float32) for r in res.results], axis=0)


# version 02
# speedup vs baseline: 1.0461x; 1.0461x over previous
import math
import numpy as np
import ml_dtypes
import concourse.bass as bass
import concourse.mybir as mybir
from concourse.bass_utils import run_bass_kernel_spmd
from contextlib import ExitStack

F32 = mybir.dt.float32
BF16 = mybir.dt.bfloat16
AF = mybir.ActivationFunctionType
ALU = mybir.AluOpType
AX = mybir.AxisListType

D = 1024
S = 2048
NH = 16
NT = 16
ND = 8
NG = 4
GH = 4
LN_EPS = 1e-5
RMS_EPS = 1e-6

_CONSTS = None


def _host_consts():
    global _CONSTS
    if _CONSTS is None:
        n = np.arange(D)
        nk = (np.outer(n, n) % D).astype(np.float64)
        theta = 2.0 * np.pi * nk / D
        Cm = (np.cos(theta) / 32.0).astype(ml_dtypes.bfloat16)
        Sm = (np.sin(theta) / 32.0).astype(ml_dtypes.bfloat16)
        ident = np.eye(128, dtype=ml_dtypes.bfloat16)
        tri = (np.arange(128)[None, :] >= np.arange(128)[:, None]).astype(ml_dtypes.bfloat16)
        e2 = np.zeros((128, 2), dtype=ml_dtypes.bfloat16)
        e2[:64, 0] = 1
        e2[64:, 1] = 1
        e2t = np.ascontiguousarray(e2.T)
        e16 = np.zeros((16, ND * 128), dtype=ml_dtypes.bfloat16)
        for c in range(ND):
            for p in range(128):
                e16[2 * c + p // 64, c * 128 + p] = 1
        ones1 = np.ones((1, 128), dtype=ml_dtypes.bfloat16)
        _CONSTS = dict(Cm=Cm, Sm=Sm, ident=ident, tri=tri, e2=e2, e2t=e2t,
                       e16=e16, ones1=ones1)
    return _CONSTS


DMA_KEYS = ("gpd", "spd")
ALL_KEYS = ("pe", "act", "dve", "gpc", "gpd", "spd")


class Ref:
    __slots__ = ("key", "idx")

    def __init__(self, key, idx):
        self.key = key
        self.idx = idx


class Prog:
    def __init__(self):
        self.ops = {k: [] for k in ALL_KEYS}
        self.needed = {k: set() for k in ALL_KEYS}

    def add(self, key, emit_fn, deps=(), war=()):
        kept = []
        for d in deps:
            if d is None:
                continue
            if d.key == key:
                assert d.idx < len(self.ops[key]), "same-engine forward dep"
            self.needed[d.key].add(d.idx)
            kept.append(d)
        for d in war:
            if d is None:
                continue
            if d.key == key:
                assert d.idx < len(self.ops[key]), "same-engine forward dep"
                continue
            self.needed[d.key].add(d.idx)
            kept.append(d)
        ref = Ref(key, len(self.ops[key]))
        self.ops[key].append((emit_fn, kept))
        return ref

    def finalize(self):
        self.val_after = {}
        for k in ALL_KEYS:
            vals, tot = [], 0
            for i in range(len(self.ops[k])):
                if k in DMA_KEYS:
                    tot += 16
                elif i in self.needed[k]:
                    tot += 1
                vals.append(tot)
            self.val_after[k] = vals

    def emit(self, key, eng, sems):
        last_wait = {k: 0 for k in ALL_KEYS}
        for i, (emit_fn, deps) in enumerate(self.ops[key]):
            for d in deps:
                v = self.val_after[d.key][d.idx]
                if v > last_wait[d.key]:
                    eng.wait_ge(sems[d.key], v)
                    last_wait[d.key] = v
            inst = emit_fn(eng)
            if key in DMA_KEYS:
                inst.then_inc(sems[key], 16)
            elif i in self.needed[key]:
                inst.then_inc(sems[key], 1)

    def last_refs(self):
        return [Ref(k, len(self.ops[k]) - 1) for k in ALL_KEYS if self.ops[k]]


class Slot:

    def __init__(self):
        self.writer = None
        self.readers = []

    def rdeps(self):
        return [self.writer] if self.writer else []

    def wdeps(self):
        return self.readers + ([self.writer] if self.writer else [])

    def read(self, ref):
        self.readers.append(ref)

    def wrote(self, ref):
        self.writer = ref
        self.readers = []

    def wrote_cont(self, ref):
        self.writer = ref


def build_program(taps=(), stop_after=None):
    nc = bass.Bass("TRN2", target_bir_lowering=False)
    from concourse.hw_specs import get_activation_tables
    _tables = list(get_activation_tables(nc.m.arch).keys())
    SET_NLEXP = _tables.index("natural_log_exp_and_others")
    SET_SILU = _tables.index("silu_and_others")

    d_x = nc.dram_tensor("x", [S, D], F32, kind="ExternalInput").ap()
    d_w = {w: nc.dram_tensor(w, [D, D], F32, kind="ExternalInput").ap()
           for w in ("wqa", "wka", "wqp", "wkp", "wv", "wo")}
    d_sls = nc.dram_tensor("sls", [NH], F32, kind="ExternalInput").ap()
    d_wfr = nc.dram_tensor("wfr", [D, D], F32, kind="ExternalInput").ap()
    d_wfi = nc.dram_tensor("wfi", [D, D], F32, kind="ExternalInput").ap()
    d_bias = nc.dram_tensor("bias", [D], F32, kind="ExternalInput").ap()
    d_C = nc.dram_tensor("Cm", [D, D], BF16, kind="ExternalInput").ap()
    d_S = nc.dram_tensor("Sm", [D, D], BF16, kind="ExternalInput").ap()
    d_ident = nc.dram_tensor("identc", [128, 128], BF16, kind="ExternalInput").ap()
    d_tri = nc.dram_tensor("tric", [128, 128], BF16, kind="ExternalInput").ap()
    d_e2 = nc.dram_tensor("e2c", [128, 2], BF16, kind="ExternalInput").ap()
    d_e2t = nc.dram_tensor("e2tc", [2, 128], BF16, kind="ExternalInput").ap()
    d_e16 = nc.dram_tensor("e16c", [16, ND * 128], BF16, kind="ExternalInput").ap()
    d_ones1 = nc.dram_tensor("ones1c", [1, 128], BF16, kind="ExternalInput").ap()
    d_out = nc.dram_tensor("out", [S, D], F32, kind="ExternalOutput").ap()
    d_cpsc = nc.dram_tensor("cpsc", [NH], F32, kind="Internal").ap()

    P = Prog()
    tap_done = set()

    with ExitStack() as ctx:
        sb = lambda n, s, d: ctx.enter_context(nc.sbuf_tensor(n, s, d))
        psa = lambda n, s: ctx.enter_context(nc.psum_tensor(n, s, F32))

        x_bf = sb("x_bf", [128, NT * D], BF16)
        hfm = sb("hfm", [128, ND * S], BF16)
        identb = sb("identb", [128, 128], BF16)
        trib = sb("trib", [128, 128], BF16)
        e2b = sb("e2b", [128, 2], BF16)
        e2tb = sb("e2tb", [2, 128], BF16)
        e16b = sb("e16b", [16, ND * 128], BF16)
        ones1b = sb("ones1b", [1, 128], BF16)
        slsrow = sb("slsrow", [1, NH], F32)
        crow8 = sb("crow8", [1, NH], F32)
        mnegrow = sb("mnegrow", [1, NH], BF16)
        mneg_bc = sb("mneg_bc", [128, NH], F32)
        cpair = sb("cpair", [2, ND], F32)
        ccol = sb("ccol", [128, 4], F32)
        stg = sb("stg", [65, S], BF16)
        io_stage = sb("io_stage", [128, D], F32)
        ln_s = sb("ln_s", [128, D], F32)
        sq_s = sb("sq_s", [128, D], BF16)
        htm_s = sb("htm", [128, 2 * D], BF16)
        stat8 = sb("stat8", [128, 8], F32)
        biasrow = sb("biasrow", [1, D], BF16)

        pA = psa("pA", [128, 2048])
        pB = psa("pB", [128, 1024])
        pC = psa("pC", [128, 1024])

        sems = {k: ctx.enter_context(nc.semaphore("sem_" + k)) for k in ALL_KEYS}

        sl = dict(pA0=Slot(), pA1=Slot(), pA2=Slot(), pA3=Slot(),
                  pB0=Slot(), pB1=Slot(), pC0=Slot(), pC1=Slot(),
                  x_bf=Slot(), hfm=Slot(), stg=Slot(),
                  io=Slot(), ln=Slot(), htm=Slot(), stat=Slot())
        PAall = [sl["pA0"], sl["pA1"], sl["pA2"], sl["pA3"]]
        PBf = [sl["pB0"], sl["pB1"]]
        PCf = [sl["pC0"], sl["pC1"]]

        def op(key, fn, reads=(), writes=(), writes_cont=(), also=()):
            deps = list(also)
            war = []
            for s in reads:
                deps += s.rdeps()
            for s in writes:
                war += s.wdeps()
            ref = P.add(key, fn, deps, war=war)
            for s in reads:
                s.read(ref)
            for s in writes:
                s.wrote(ref)
            for s in writes_cont:
                s.wrote_cont(ref)
            return ref

        def act_load(set_id):
            return P.add("act", lambda e, sid=set_id: e.add_instruction(
                mybir.InstLoadActFuncSet(
                    name=e.bass.get_next_instruction_name(),
                    act_func_set_id=sid, ins=[], outs=[])))

        dbg_outs = {}

        def tap(name, ap, also=()):
            if name not in taps or name in tap_done:
                return
            tap_done.add(name)
            shp = [dd[1] for dd in ap.ap]
            dt = ap.tensor.dtype if hasattr(ap.tensor, "dtype") else F32
            d_dbg = nc.dram_tensor("dbg_" + name, shp, dt, kind="ExternalOutput").ap()
            dbg_outs[name] = d_dbg
            P.add("spd", lambda e, _a=ap, _d=d_dbg: e.dma_start(_d, _a), list(also))

        r_ident = P.add("spd", lambda e: e.dma_start(identb[:, :], d_ident))
        r_tri = P.add("spd", lambda e: e.dma_start(trib[:, :], d_tri))
        r_e2 = P.add("spd", lambda e: e.dma_start(e2b[:, :], d_e2))
        r_e2t = P.add("spd", lambda e: e.dma_start(e2tb[:, :], d_e2t))
        r_e16 = P.add("spd", lambda e: e.dma_start(e16b[:, :], d_e16))
        r_ones1 = P.add("spd", lambda e: e.dma_start(ones1b[:, :], d_ones1))
        r_sls = P.add("spd", lambda e: e.dma_start(slsrow[:, :], d_sls.rearrange("(a b) -> a b", a=1)))
        r_biasrow = P.add("gpd", lambda e: e.dma_start(biasrow[:, :], d_bias.rearrange("(a b) -> a b", a=1)))

        P.add("dve", lambda e: e.memset(ccol[:, 0:1], 1.0))
        P.add("dve", lambda e: e.memset(ccol[:, 1:2], RMS_EPS))
        P.add("dve", lambda e: e.memset(ccol[:, 2:3], LN_EPS))
        r_ccol = P.add("dve", lambda e: e.memset(ccol[:, 3:4], -math.pi / 2))

        act_load(SET_NLEXP)
        r_crow = P.add("act", lambda e: e.activation(crow8[:, :], slsrow[:, :], AF.Exp), [r_sls])
        r_crow = P.add("dve", lambda e: e.tensor_scalar_mul(crow8[:, :], crow8[:, :], 0.125), [r_crow])
        r_mneg = P.add("dve", lambda e: e.tensor_scalar_mul(mnegrow[:, :], crow8[:, :], -64.0), [r_crow])
        r_mm = op("pe", lambda e: e.matmul(pB[:, 0:NH], lhsT=ones1b[:, :], rhs=mnegrow[:, :],
                                           start=True, stop=True),
                  writes=[sl["pB0"]], also=[r_mneg, r_ones1])
        r_mnegbc = op("dve", lambda e: e.tensor_copy(mneg_bc[:, :], pB[:, 0:NH]),
                      reads=[sl["pB0"]])
        r_cpd = P.add("gpd", lambda e: e.dma_start(d_cpsc.rearrange("(a b) -> a b", a=1), crow8[:, :]), [r_crow])
        r_cpair = P.add("gpd", lambda e: e.dma_start(
            cpair[:, :], d_cpsc.rearrange("(cc p) -> p cc", p=2),
            allow_slow_non_contiguous=True), [r_cpd])

        r_xload = [P.add("gpd", lambda e, t=t: e.dma_start(
            x_bf[:, t * D:(t + 1) * D], d_x[t * 128:(t + 1) * 128, :])) for t in range(NT)]

        def emit_ln_stats_apply(src_ap, dst_ap, src_slot, dst_slot, par, extra=()):
            sb_ = 4 * par
            st = lambda i: stat8[:, sb_ + i:sb_ + i + 1]
            r1 = op("dve", lambda e: e.reduce_sum(st(0), src_ap, axis=AX.X),
                    reads=[src_slot], writes=[sl["stat"]], also=list(extra))
            r2 = op("dve", lambda e: e.tensor_scalar_mul(st(0), st(0), 1.0 / D),
                    writes=[sl["stat"]], also=[r1])
            r3 = op("act", lambda e: e.activation(
                sq_s[:, :], src_ap, AF.Square, accum_out=st(1)),
                reads=[src_slot], writes=[sl["stat"]], also=list(extra))
            r4 = op("dve", lambda e: e.tensor_tensor(st(3), st(0), st(0), ALU.mult),
                    writes=[sl["stat"]], also=[r2])
            r5 = op("dve", lambda e: e.scalar_tensor_tensor(
                st(2), st(1), 1.0 / D, st(3), ALU.mult, ALU.subtract),
                writes=[sl["stat"]], also=[r3, r4])
            r6 = op("act", lambda e: e.activation(st(2), st(2), AF.Ln, bias=ccol[:, 2:3]),
                    writes=[sl["stat"]], also=[r5, r_ccol])
            r7 = op("act", lambda e: e.activation(st(2), st(2), AF.Exp, scale=-0.5),
                    writes=[sl["stat"]], also=[r6])
            r8 = op("dve", lambda e: e.tensor_scalar(
                dst_ap, src_ap, st(0), st(2), ALU.subtract, ALU.mult),
                reads=[src_slot, sl["stat"]], writes=[dst_slot], also=[r7])
            return r8

        def emit_transpose(src_tile_fn, dst_tensor, t, src_slot, dst_slot,
                           ptiles=None, extra=()):
            if ptiles is None:
                ptiles = [(pB, sl["pB0"], 0), (pC, sl["pC0"], 0)]
            for half in range(2):
                ptile, pslot, poff = ptiles[half]
                for q in range(4):
                    dc = half * 4 + q
                    op("pe", lambda e, dc=dc, q=q, ptile=ptile, poff=poff: e.matmul(
                        ptile[:, poff + q * 128: poff + (q + 1) * 128],
                        lhsT=src_tile_fn(dc), rhs=identb[:, :], start=True, stop=True),
                        reads=[src_slot],
                        writes=[pslot] if q == 0 else [],
                        writes_cont=[pslot] if q > 0 else [],
                        also=list(extra) + ([r_ident] if q == 0 else []))
                dst = bass.AP(dst_tensor[:, :].tensor, (half * 4) * S + t * 128,
                              [dst_tensor[:, :].ap[0], [S, 4], [1, 128]])
                op("act", lambda e, dst=dst, ptile=ptile, poff=poff: e.activation(
                    dst, ptile[:, poff:poff + 512], AF.Copy),
                    reads=[pslot], writes=[dst_slot])

        for t in range(NT):
            par = t % 2
            hview = htm_s[:, par * D:(par + 1) * D]
            emit_ln_stats_apply(x_bf[:, t * D:(t + 1) * D], hview,
                                sl["x_bf"], sl["htm"], par, extra=[r_xload[t]])
            emit_transpose(lambda dc, par=par: htm_s[:, par * D + dc * 128: par * D + (dc + 1) * 128],
                           hfm, t, sl["htm"], sl["hfm"])

        tap("h0fm", hfm[:, :], also=P.last_refs())
        if stop_after == "phase0":
            return _finish(nc, P, sems, dbg_outs)

        attn_ctx = ctx.enter_context(ExitStack())
        sba = lambda n, s, d: attn_ctx.enter_context(nc.sbuf_tensor(n, s, d))
        o_fm = sba("o_fm", [128, ND * S], BF16)
        Lb = sba("Lb", [NH, S], BF16)
        rlb = sba("rlb", [NH, S], BF16)
        Qcs = sba("Qcs", [128, GH * S], BF16)
        Kcs = sba("Kcs", [128, GH * S], BF16)
        V_aug = sba("V_aug", [128, NT * GH * 65], BF16)
        WTa = sba("WTa", [128, ND * 256], BF16)
        WTb = sba("WTb", [128, ND * 256], BF16)
        WTv = sba("WTv", [128, ND * 256], BF16)
        Wstage = sba("Wstage", [128, 2 * D], BF16)
        qx = sba("qx", [128, S], BF16)
        Pt = sba("Pt", [128, 2048], BF16)
        t14 = sba("t14", [128, S], BF16)
        qa_s = sba("qa_s", [128, 512], BF16)
        q2_s = sba("q2_s", [128, 512], BF16)
        t_s = sba("t_s", [128, 512], BF16)
        tn_s = sba("tn_s", [128, 512], BF16)
        sinv_s = sba("sinv_s", [128, 512], BF16)
        cosn_s = sba("cosn_s", [128, 512], BF16)
        rstd2b_s = sba("rstd2b_s", [2, 512], BF16)
        ln2_s = sba("ln2_s", [128, D], F32)

        for nm in ("Qcs", "Kcs", "V_aug", "Wstage", "qx", "Pt0", "Pt1",
                   "qa_s", "q2_s", "t_s", "tn_s",
                   "sinv_s", "cosn_s", "t14", "rstd2",
                   "WTa", "WTb", "WTv", "o_fm", "Lb", "rlb", "ln2"):
            sl[nm] = Slot()

        va = V_aug[:, :]
        va_ones = bass.AP(va.tensor, 64, [va.ap[0], [260, NT], [65, GH], [0, 1]])
        op("dve", lambda e: e.memset(va_ones, 1.0), writes=[sl["V_aug"]])

        def emit_wslice(g, w, WTdst, dname):
            rl_ = []
            for fc in range(2):
                rl_.append(op("gpd", lambda e, w=w, fc=fc, g=g: e.dma_start(
                    Wstage[:, fc * D:(fc + 1) * D],
                    d_w[w][g * 256 + fc * 128: g * 256 + (fc + 1) * 128, :]),
                    writes=[sl["Wstage"]]))
            for grp4 in range(4):
                ptile, pslot = (pB, sl["pB0"]) if grp4 % 2 == 0 else (pC, sl["pC0"])
                for pos in range(4):
                    idx = grp4 * 4 + pos
                    dc, fc = idx // 2, idx % 2
                    op("pe", lambda e, dc=dc, fc=fc, pos=pos, ptile=ptile: e.matmul(
                        ptile[:, pos * 128:(pos + 1) * 128],
                        lhsT=Wstage[:, fc * D + dc * 128: fc * D + (dc + 1) * 128],
                        rhs=identb[:, :], start=True, stop=True),
                        reads=[sl["Wstage"]],
                        writes=[pslot] if pos == 0 else [],
                        writes_cont=[pslot] if pos > 0 else [],
                        also=rl_ if idx == 0 else [])
                op("dve", lambda e, grp4=grp4, ptile=ptile, WTdst=WTdst: e.tensor_copy(
                    WTdst[:, grp4 * 512:(grp4 + 1) * 512], ptile[:, 0:512]),
                    reads=[pslot], writes=[sl[dname]])

        def emit_proj_pair(g, WTamp, aname, WTphi, pname, dst, dst_name, scale_c):
            for c in range(2):
                act_load(SET_NLEXP)
                for js in range(4):
                    tile, X0, X1 = (pB, sl["pB0"], sl["pB1"]) if js % 2 == 0 else (pC, sl["pC0"], sl["pC1"])
                    for dc in range(ND):
                        op("pe", lambda e, dc=dc, c=c, js=js, tile=tile, WTamp=WTamp: e.matmul(
                            tile[:, 0:512],
                            lhsT=WTamp[:, dc * 256 + c * 128: dc * 256 + (c + 1) * 128],
                            rhs=hfm[:, dc * S + js * 512: dc * S + (js + 1) * 512],
                            start=(dc == 0), stop=(dc == ND - 1)),
                            reads=[sl[aname], sl["hfm"]],
                            writes=[X0] if dc == 0 else [],
                            writes_cont=[X0] if dc > 0 else [])
                    r_sp = op("act", lambda e, tile=tile: e.activation(qa_s[:, :], tile[:, 0:512], AF.Exp),
                       reads=[X0], writes=[sl["qa_s"]])
                    op("act", lambda e: e.activation(qa_s[:, :], qa_s[:, :], AF.Ln, bias=ccol[:, 0:1]),
                       writes=[sl["qa_s"]], also=[r_ccol, r_sp])
                    op("dve", lambda e: e.tensor_mul(q2_s[:, :], qa_s[:, :], qa_s[:, :]),
                       reads=[sl["qa_s"]], writes=[sl["q2_s"]])
                    op("pe", lambda e, tile=tile: e.matmul(
                        tile[0:2, 512:1024], lhsT=e2b[:, :], rhs=q2_s[:, :],
                        start=True, stop=True),
                       reads=[sl["q2_s"]], writes=[X1], also=[r_e2])
                    r_rln = op("act", lambda e, tile=tile: e.activation(
                        rstd2b_s[:, :], tile[0:2, 512:1024], AF.Ln,
                        scale=1.0 / 64.0, bias=ccol[0:2, 1:2]),
                       reads=[X1], writes=[sl["rstd2"]])
                    op("act", lambda e: e.activation(rstd2b_s[:, :], rstd2b_s[:, :], AF.Exp, scale=-0.5),
                       writes=[sl["rstd2"]], also=[r_rln])
                    if scale_c:
                        op("dve", lambda e, g=g, c=c: e.tensor_scalar_mul(
                            rstd2b_s[:, :], rstd2b_s[:, :], cpair[:, g * 2 + c: g * 2 + c + 1]),
                           writes=[sl["rstd2"]], also=[r_cpair])
                    op("pe", lambda e, tile=tile: e.matmul(
                        tile[:, 512:1024], lhsT=e2tb[:, :], rhs=rstd2b_s[:, :],
                        start=True, stop=True),
                       reads=[sl["rstd2"]], writes=[X1], also=[r_e2t])
                    op("dve", lambda e, js=js, tile=tile: e.tensor_tensor(
                        t14[:, js * 512:(js + 1) * 512], qa_s[:, :], tile[:, 512:1024], ALU.mult),
                       reads=[sl["qa_s"], X1], writes=[sl["t14"]])
                act_load(SET_SILU)
                for js in range(4):
                    tile, X0, X1 = (pB, sl["pB0"], sl["pB1"]) if js % 2 == 0 else (pC, sl["pC0"], sl["pC1"])
                    for dc in range(ND):
                        op("pe", lambda e, dc=dc, c=c, js=js, tile=tile, WTphi=WTphi: e.matmul(
                            tile[:, 0:512],
                            lhsT=WTphi[:, dc * 256 + c * 128: dc * 256 + (c + 1) * 128],
                            rhs=hfm[:, dc * S + js * 512: dc * S + (js + 1) * 512],
                            start=(dc == 0), stop=(dc == ND - 1)),
                            reads=[sl[pname], sl["hfm"]],
                            writes=[X0] if dc == 0 else [],
                            writes_cont=[X0] if dc > 0 else [])
                    op("act", lambda e, tile=tile: e.activation(t_s[:, :], tile[:, 0:512], AF.Tanh),
                       reads=[X0], writes=[sl["t_s"]])
                    op("dve", lambda e: e.tensor_scalar_mul(tn_s[:, :], t_s[:, :], -1.0),
                       reads=[sl["t_s"]], writes=[sl["tn_s"]])
                    op("dve", lambda e: e.tensor_max(tn_s[:, :], t_s[:, :], tn_s[:, :]),
                       reads=[sl["t_s"]], writes=[sl["tn_s"]])
                    op("act", lambda e: e.activation(sinv_s[:, :], t_s[:, :], AF.Sin, scale=math.pi),
                       reads=[sl["t_s"]], writes=[sl["sinv_s"]])
                    op("act", lambda e: e.activation(cosn_s[:, :], tn_s[:, :], AF.Sin,
                                                     scale=math.pi, bias=ccol[:, 3:4]),
                       reads=[sl["tn_s"]], writes=[sl["cosn_s"]])
                    lo0 = (2 * c) * S + js * 512
                    lo1 = (2 * c + 1) * S + js * 512
                    t1lo = t14[0:64, js * 512:(js + 1) * 512]
                    t1hi = t14[64:128, js * 512:(js + 1) * 512]
                    op("dve", lambda e, lo0=lo0, t1lo=t1lo: e.scalar_tensor_tensor(
                        dst[0:64, lo0:lo0 + 512], cosn_s[0:64, :], -1.0, t1lo,
                        ALU.mult, ALU.mult),
                       reads=[sl["cosn_s"], sl["t14"]], writes=[sl[dst_name]])
                    op("dve", lambda e, js=js, t1hi=t1hi: e.scalar_tensor_tensor(
                        qx[64:128, js * 512:(js + 1) * 512], cosn_s[64:128, :], -1.0,
                        t1hi, ALU.mult, ALU.mult),
                       reads=[sl["cosn_s"], sl["t14"]], writes=[sl["qx"]])
                    op("dve", lambda e, js=js, t1lo=t1lo: e.tensor_mul(
                        qx[0:64, js * 512:(js + 1) * 512], t1lo, sinv_s[0:64, :]),
                       reads=[sl["sinv_s"], sl["t14"]], writes=[sl["qx"]])
                    op("dve", lambda e, lo1=lo1, t1hi=t1hi: e.tensor_mul(
                        dst[64:128, lo1:lo1 + 512], t1hi, sinv_s[64:128, :]),
                       reads=[sl["sinv_s"], sl["t14"]], writes=[sl[dst_name]])
                op("gpd", lambda e, c=c: e.dma_start(dst[0:64, (2 * c + 1) * S:(2 * c + 2) * S],
                                                     qx[64:128, :]),
                   reads=[sl["qx"]], writes=[sl[dst_name]])
                op("gpd", lambda e, c=c: e.dma_start(dst[64:128, (2 * c) * S:(2 * c) * S + S],
                                                     qx[0:64, :]),
                   reads=[sl["qx"]], writes=[sl[dst_name]])

        def emit_vproj(g):
            for t in range(NT):
                tile, pslot = (pB, sl["pB0"]) if t % 2 == 0 else (pC, sl["pC0"])
                for dc in range(ND):
                    op("pe", lambda e, dc=dc, t=t, tile=tile: e.matmul(
                        tile[:, 0:256], lhsT=hfm[:, dc * S + t * 128: dc * S + (t + 1) * 128],
                        rhs=WTv[:, dc * 256:(dc + 1) * 256],
                        start=(dc == 0), stop=(dc == ND - 1)),
                        reads=[sl["hfm"], sl["WTv"]],
                        writes=[pslot] if dc == 0 else [],
                        writes_cont=[pslot] if dc > 0 else [])
                dstp = bass.AP(va.tensor, t * 260, [va.ap[0], [65, GH], [1, 64]])
                op("dve", lambda e, dstp=dstp, tile=tile: e.tensor_copy(
                    dstp, tile[:, 0:256].rearrange("p (a b) -> p a b", a=GH)),
                   reads=[pslot], writes=[sl["V_aug"]])

        def emit_attention(g):
            act_load(SET_NLEXP)
            for hl in range(GH):
                h = g * GH + hl
                pair_i = 0
                for ki in range(NT):
                    blocks = [(128 * ki, 512 - 128 * (ki % 4))]
                    blocks += [(512 * b, 512) for b in range(ki // 4 + 1, 4)]
                    pairs = [blocks[i:i + 2] for i in range(0, len(blocks), 2)]
                    for pr in pairs:
                        tile, X0, X1 = (pB, sl["pB0"], sl["pB1"]) if pair_i % 2 == 0 else (pC, sl["pC0"], sl["pC1"])
                        ptslot = sl["Pt0"] if pair_i % 2 == 0 else sl["Pt1"]
                        ptbase = 0 if pair_i % 2 == 0 else 1024
                        w0 = pr[0][1]
                        lo = 512 - w0
                        hi = 512 + (512 if len(pr) > 1 else 0)
                        for bi, (qs, w) in enumerate(pr):
                            poff = 512 - w0 if bi == 0 else 512
                            xslot = X0 if bi == 0 else X1
                            op("pe", lambda e, hl=hl, ki=ki, qs=qs, w=w, poff=poff, tile=tile: e.matmul(
                                tile[:, poff:poff + w],
                                lhsT=Kcs[:, hl * S + ki * 128: hl * S + (ki + 1) * 128],
                                rhs=Qcs[:, hl * S + qs: hl * S + qs + w], start=True, stop=True),
                                reads=[sl["Kcs"], sl["Qcs"]], writes=[xslot])
                        r_exp = op("act", lambda e, lo=lo, hi=hi, h=h, ptbase=ptbase, tile=tile: e.activation(
                            Pt[:, ptbase + lo: ptbase + hi], tile[:, lo:hi], AF.Exp,
                            bias=mneg_bc[:, h:h + 1]),
                            reads=[X0] + ([X1] if len(pr) > 1 else []),
                            writes=[ptslot], also=[r_mnegbc])
                        r_last = r_exp
                        if pr[0][0] == 128 * ki:
                            r_last = op("dve", lambda e, lo=lo, ptbase=ptbase: e.tensor_mul(
                                Pt[:, ptbase + lo: ptbase + lo + 128],
                                Pt[:, ptbase + lo: ptbase + lo + 128], trib[:, :]),
                                writes=[ptslot], also=[r_tri])
                        for bi, (qs, w) in enumerate(pr):
                            poff = 512 - w0 if bi == 0 else 512
                            bank = qs // 512
                            bank_last = min(15, 4 * bank + 3)
                            first_write = (ki == 0 and qs == 0)
                            op("pe", lambda e, ki=ki, hl=hl, qs=qs, w=w, poff=poff,
                               ptbase=ptbase, bank_last=bank_last: e.matmul(
                                pA[0:65, qs:qs + w],
                                lhsT=V_aug[:, ki * 260 + hl * 65: ki * 260 + (hl + 1) * 65],
                                rhs=Pt[:, ptbase + poff: ptbase + poff + w],
                                start=(ki == 0), stop=(ki == bank_last), skip_group_check=True),
                                reads=[ptslot, sl["V_aug"]],
                                writes=PAall if first_write else [],
                                writes_cont=PAall if not first_write else [],
                                also=[r_last])
                        pair_i += 1
                r_ev = op("dve", lambda e: e.tensor_copy(stg[:, :], pA[0:65, :]),
                          reads=PAall, writes=[sl["stg"]])
                c, half = h // 2, h % 2
                op("gpd", lambda e, c=c, half=half: e.dma_start(
                    o_fm[64 * half:64 * half + 64, c * S:(c + 1) * S], stg[0:64, :]),
                   reads=[sl["stg"]], writes=[sl["o_fm"]])
                op("gpd", lambda e, h=h: e.dma_start(Lb[h:h + 1, :], stg[64:65, :]),
                   reads=[sl["stg"]], writes=[sl["Lb"]])

        for g in range(NG):
            emit_wslice(g, "wqa", WTa, "WTa")
            emit_wslice(g, "wqp", WTb, "WTb")
            if g == 0:
                emit_wslice(g, "wv", WTv, "WTv")
            emit_proj_pair(g, WTa, "WTa", WTb, "WTb", Qcs, "Qcs", True)
            emit_wslice(g, "wka", WTa, "WTa")
            emit_wslice(g, "wkp", WTb, "WTb")
            emit_proj_pair(g, WTa, "WTa", WTb, "WTb", Kcs, "Kcs", False)
            emit_vproj(g)
            if g < NG - 1:
                emit_wslice(g + 1, "wv", WTv, "WTv")
            if g == 0:
                tap("Qcs0", Qcs[:, :], also=P.last_refs())
                tap("Kcs0", Kcs[:, :], also=P.last_refs())
                tap("Vaug0", V_aug[:, :], also=P.last_refs())
            emit_attention(g)
            if g == 0:
                tap("stg0", stg[:, :], also=P.last_refs())
            if stop_after == "group%d" % g:
                tap("o_fm", o_fm[:, :], also=P.last_refs())
                tap("Lb", Lb[:, :], also=P.last_refs())
                return _finish(nc, P, sems, dbg_outs)

        tap("o_fm", o_fm[:, :], also=P.last_refs())
        tap("Lb", Lb[:, :], also=P.last_refs())

        def _emit_recip(e):
            with nc.allow_low_precision(reason="1/l stored bf16; same scale as bf16 matmul rounding"):
                return e.reciprocal(rlb[:, :], Lb[:, :])
        r_rec = op("dve", _emit_recip, reads=[sl["Lb"]], writes=[sl["rlb"]])
        sl["rlb"].readers.append(r_rec)
        for c in range(ND):
            for js in range(4):
                tile, pslot = (pB, sl["pB0"]) if (c * 4 + js) % 2 == 0 else (pC, sl["pC0"])
                op("pe", lambda e, c=c, js=js, tile=tile: e.matmul(
                    tile[:, 0:512], lhsT=e16b[:, c * 128:(c + 1) * 128],
                    rhs=rlb[:, js * 512:(js + 1) * 512], start=True, stop=True),
                   reads=[sl["rlb"]], writes=[pslot], also=[r_e16])
                op("dve", lambda e, c=c, js=js, tile=tile: e.tensor_tensor(
                    o_fm[:, c * S + js * 512: c * S + (js + 1) * 512],
                    o_fm[:, c * S + js * 512: c * S + (js + 1) * 512],
                    tile[:, 0:512], ALU.mult),
                   reads=[pslot], writes=[sl["o_fm"]])

        tap("o_norm", o_fm[:, :], also=P.last_refs())

        WoT = Qcs
        sl["WoT"] = sl["Qcs"]
        for fcpair in range(4):
            rl_ = []
            for k in range(2):
                foc = fcpair * 2 + k
                rl_.append(op("gpd", lambda e, foc=foc, k=k: e.dma_start(
                    Wstage[:, k * D:(k + 1) * D], d_w["wo"][foc * 128:(foc + 1) * 128, :]),
                    writes=[sl["Wstage"]]))
            for grp4 in range(4):
                ptile, pslot = (pB, sl["pB0"]) if grp4 % 2 == 0 else (pC, sl["pC0"])
                evs = []
                for pos in range(4):
                    idx = grp4 * 4 + pos
                    k, fc = idx // 8, idx % 8
                    op("pe", lambda e, k=k, fc=fc, pos=pos, ptile=ptile: e.matmul(
                        ptile[:, pos * 128:(pos + 1) * 128],
                        lhsT=Wstage[:, k * D + fc * 128: k * D + (fc + 1) * 128],
                        rhs=identb[:, :], start=True, stop=True),
                        reads=[sl["Wstage"]],
                        writes=[pslot] if pos == 0 else [],
                        writes_cont=[pslot] if pos > 0 else [],
                        also=rl_ if idx == 0 else [])
                    evs.append((k, fc, pos))
                for (k, fc, pos) in evs:
                    foc = fcpair * 2 + k
                    op("dve", lambda e, fc=fc, foc=foc, pos=pos, ptile=ptile: e.tensor_copy(
                        WoT[:, fc * D + foc * 128: fc * D + (foc + 1) * 128],
                        ptile[:, pos * 128:(pos + 1) * 128]),
                        reads=[pslot], writes=[sl["WoT"]])

        act_load(SET_NLEXP)

        def emit_postblock(t, ptile, pslots, ln_tile, ln_slot, out_dst, extra_dep=()):
            par = t % 2
            sb_ = 4 * par
            st = lambda i: stat8[:, sb_ + i:sb_ + i + 1]
            r_ev = op("act", lambda e, ptile=ptile, ln_tile=ln_tile: e.activation(
                ln_tile[:, :], ptile[:, 0:1024], AF.Copy, accum_out=st(0)),
                reads=pslots, writes=[ln_slot, sl["stat"]], also=list(extra_dep))
            r2 = op("dve", lambda e: e.tensor_scalar_mul(st(0), st(0), 1.0 / D),
                    writes=[sl["stat"]], also=[r_ev])
            r3 = op("act", lambda e, ln_tile=ln_tile: e.activation(
                sq_s[:, :], ln_tile[:, :], AF.Square, accum_out=st(1)),
                reads=[ln_slot], writes=[sl["stat"]])
            r4 = op("dve", lambda e: e.tensor_tensor(st(3), st(0), st(0), ALU.mult),
                    writes=[sl["stat"]], also=[r2])
            r5 = op("dve", lambda e: e.scalar_tensor_tensor(
                st(2), st(1), 1.0 / D, st(3), ALU.mult, ALU.subtract),
                writes=[sl["stat"]], also=[r3, r4])
            r6 = op("act", lambda e: e.activation(st(2), st(2), AF.Ln, bias=ccol[:, 2:3]),
                    writes=[sl["stat"]], also=[r5])
            r7 = op("act", lambda e: e.activation(st(2), st(2), AF.Exp, scale=-0.5),
                    writes=[sl["stat"]], also=[r6])
            r8 = op("dve", lambda e, ln_tile=ln_tile: e.tensor_scalar(
                ln_tile[:, :], ln_tile[:, :], st(0), st(2), ALU.subtract, ALU.mult),
                reads=[sl["stat"]], writes=[ln_slot], also=[r7])
            return r8

        for t in range(NT):
            k3 = t % 2
            ptile = pB if k3 == 0 else pC
            pslots = PBf if k3 == 0 else PCf
            for js in range(2):
                for c in range(ND):
                    first_op = (js == 0 and c == 0)
                    op("pe", lambda e, c=c, js=js, t=t, ptile=ptile: e.matmul(
                        ptile[:, js * 512:(js + 1) * 512],
                        lhsT=o_fm[:, c * S + t * 128: c * S + (t + 1) * 128],
                        rhs=WoT[:, c * D + js * 512: c * D + (js + 1) * 512],
                        start=(c == 0), stop=(c == ND - 1)),
                        reads=[sl["o_fm"], sl["WoT"]],
                        writes=pslots if first_op else [],
                        writes_cont=pslots if not first_op else [])
            ln_tile = ln_s if t % 2 == 0 else ln2_s
            ln_slot = sl["ln"] if t % 2 == 0 else sl["ln2"]
            r8 = emit_postblock(t, ptile, pslots, ln_tile, ln_slot, None)
            op("dve", lambda e, t=t, ln_tile=ln_tile: e.tensor_tensor(
                x_bf[:, t * D:(t + 1) * D], x_bf[:, t * D:(t + 1) * D], ln_tile[:, :], ALU.add),
               reads=[ln_slot], writes=[sl["x_bf"]], also=[r8])
            par = t % 2
            hview = htm_s[:, par * D:(par + 1) * D]
            emit_ln_stats_apply(x_bf[:, t * D:(t + 1) * D], hview, sl["x_bf"], sl["htm"], par)
            emit_transpose(lambda dc, par=par: htm_s[:, par * D + dc * 128: par * D + (dc + 1) * 128],
                           hfm, t, sl["htm"], sl["hfm"],
                           ptiles=[(pA, sl["pA0"], 0), (pA, sl["pA1"], 512)])

        tap("x1", x_bf[:, :], also=P.last_refs())
        tap("h1fm", hfm[:, :], also=P.last_refs())
        if stop_after == "phaseC":
            return _finish(nc, P, sems, dbg_outs)

        barrier = P.last_refs()
        attn_ctx.close()
        sbm = lambda n, s, d: ctx.enter_context(nc.sbuf_tensor(n, s, d))
        Wr = sbm("Wr", [128, ND * D], BF16)
        Wi = sbm("Wi", [128, ND * D], BF16)
        Csb = sbm("Csb", [128, ND * D], BF16)
        Ssb = sbm("Ssb", [128, ND * D], BF16)
        A_T = sbm("A_T", [128, ND * D], BF16)
        B_Tn = sbm("B_Tn", [128, ND * D], BF16)
        Mb = sbm("Mb", [128, ND * D], BF16)
        lnd_s = sbm("lnd_s", [128, D], F32)
        io2 = sbm("io2", [128, D], F32)
        for nm in ("Wr", "Wi", "Csb", "Ssb", "A_T", "B_Tn", "Mb", "lnd", "io2"):
            sl[nm] = Slot()

        for dc in range(ND):
            r = P.add("gpd", lambda e, dc=dc: e.dma_start(
                Wr[:, dc * D:(dc + 1) * D], d_wfr[dc * 128:(dc + 1) * 128, :]), barrier)
            sl["Wr"].wrote_cont(r) if dc else sl["Wr"].wrote(r)
            r = P.add("gpd", lambda e, dc=dc: e.dma_start(
                Wi[:, dc * D:(dc + 1) * D], d_wfi[dc * 128:(dc + 1) * 128, :]), barrier)
            sl["Wi"].wrote_cont(r) if dc else sl["Wi"].wrote(r)
            r = P.add("spd", lambda e, dc=dc: e.dma_start(
                Csb[:, dc * D:(dc + 1) * D], d_C[dc * 128:(dc + 1) * 128, :]), barrier)
            sl["Csb"].wrote_cont(r) if dc else sl["Csb"].wrote(r)
            r = P.add("spd", lambda e, dc=dc: e.dma_start(
                Ssb[:, dc * D:(dc + 1) * D], d_S[dc * 128:(dc + 1) * 128, :]), barrier)
            sl["Ssb"].wrote_cont(r) if dc else sl["Ssb"].wrote(r)

        def emit_MM_accum(dst, dst_name, lhs_list):
            nterm = len(lhs_list)
            for outc in range(ND):
                tile = pB if outc % 2 == 0 else pC
                Xs = PBf if outc % 2 == 0 else PCf
                for js in range(2):
                    for ti, (lt, lname, rt, rname) in enumerate(lhs_list):
                        for dc in range(ND):
                            first = (ti == 0 and dc == 0)
                            last = (ti == nterm - 1 and dc == ND - 1)
                            is_first_op = (js == 0 and first)
                            op("pe", lambda e, lt=lt, rt=rt, dc=dc, outc=outc, js=js,
                               first=first, last=last, tile=tile: e.matmul(
                                   tile[:, js * 512:(js + 1) * 512],
                                   lhsT=lt[:, dc * D + outc * 128: dc * D + (outc + 1) * 128],
                                   rhs=rt[:, dc * D + js * 512: dc * D + (js + 1) * 512],
                                   start=first, stop=last, skip_group_check=True),
                               reads=[sl[lname], sl[rname]],
                               writes=Xs if is_first_op else [],
                               writes_cont=Xs if not is_first_op else [])
                op("dve", lambda e, outc=outc, tile=tile: e.tensor_copy(
                    dst[:, outc * D:(outc + 1) * D], tile[:, :]),
                    reads=Xs, writes=[sl[dst_name]])

        emit_MM_accum(A_T, "A_T", [(Wr, "Wr", Csb, "Csb"), (Wi, "Wi", Ssb, "Ssb")])
        op("dve", lambda e: e.tensor_scalar_mul(Wi[:, :], Wi[:, :], -1.0), writes=[sl["Wi"]])
        emit_MM_accum(B_Tn, "B_Tn", [(Wr, "Wr", Ssb, "Ssb"), (Wi, "Wi", Csb, "Csb")])
        emit_MM_accum(Mb, "Mb", [(A_T, "A_T", Csb, "Csb"), (B_Tn, "B_Tn", Ssb, "Ssb")])
        tap("Mb", Mb[:, :], also=P.last_refs())

        for t in range(NT):
            k3 = t % 3
            ptile = (pB, pC, pA)[k3]
            pslots = (PBf, PCf, [sl["pA0"], sl["pA1"]])[k3]
            for js in range(2):
                for dc in range(ND):
                    first_op = (js == 0 and dc == 0)
                    op("pe", lambda e, dc=dc, js=js, t=t, ptile=ptile: e.matmul(
                        ptile[:, js * 512:(js + 1) * 512],
                        lhsT=hfm[:, dc * S + t * 128: dc * S + (t + 1) * 128],
                        rhs=Mb[:, dc * D + js * 512: dc * D + (js + 1) * 512],
                        start=(dc == 0), stop=False, skip_group_check=True),
                        reads=[sl["hfm"], sl["Mb"]],
                        writes=pslots if first_op else [],
                        writes_cont=pslots if not first_op else [])
                op("pe", lambda e, js=js, ptile=ptile: e.matmul(
                    ptile[:, js * 512:(js + 1) * 512], lhsT=ones1b[:, :],
                    rhs=biasrow[:, js * 512:(js + 1) * 512],
                    start=False, stop=True, skip_group_check=True),
                    writes_cont=pslots, also=[r_biasrow, r_ones1])
            ln_tile = lnd_s if t % 2 == 0 else ln_s
            ln_slot = sl["lnd"] if t % 2 == 0 else sl["ln"]
            r8 = emit_postblock(t, ptile, pslots, ln_tile, ln_slot, None)
            io_t = io_stage if t % 2 == 0 else io2
            io_slot = sl["io"] if t % 2 == 0 else sl["io2"]
            op("dve", lambda e, t=t, ln_tile=ln_tile, io_t=io_t: e.tensor_tensor(
                io_t[:, :], x_bf[:, t * D:(t + 1) * D], ln_tile[:, :], ALU.add),
               reads=[ln_slot], writes=[io_slot], also=[r8])
            op("spd", lambda e, t=t, io_t=io_t: e.dma_start(
                d_out[t * 128:(t + 1) * 128, :], io_t[:, :]),
               reads=[io_slot])

        return _finish(nc, P, sems, dbg_outs)


def _finish(nc, P, sems, dbg_outs):
    P.finalize()
    with nc.Block() as block:
        @block.tensor
        def _(eng):
            P.emit("pe", eng, sems)

        @block.scalar
        def _(eng):
            P.emit("act", eng, sems)

        @block.vector
        def _(eng):
            P.emit("dve", eng, sems)

        @block.gpsimd
        def _(eng):
            P.emit("gpc", eng, sems)
            P.emit("gpd", eng, sems)
            if P.ops["gpd"]:
                eng.wait_ge(sems["gpd"], P.val_after["gpd"][-1])

        @block.sync
        def _(eng):
            P.emit("spd", eng, sems)
            if P.ops["spd"]:
                eng.wait_ge(sems["spd"], P.val_after["spd"][-1])

    return nc


def kernel(L0_attn_x, L0_attn_Wq_amp, L0_attn_Wk_amp, L0_attn_Wq_phi,
           L0_attn_Wk_phi, L0_attn_Wv, L0_attn_Wo, L0_attn_score_log_scale,
           L0_ffn_w_freq_real, L0_ffn_w_freq_imag, L0_ffn_bias):
    cst = _host_consts()
    nc = build_program()
    shared = {
        "wqa": np.ascontiguousarray(L0_attn_Wq_amp, dtype=np.float32),
        "wka": np.ascontiguousarray(L0_attn_Wk_amp, dtype=np.float32),
        "wqp": np.ascontiguousarray(L0_attn_Wq_phi, dtype=np.float32),
        "wkp": np.ascontiguousarray(L0_attn_Wk_phi, dtype=np.float32),
        "wv": np.ascontiguousarray(L0_attn_Wv, dtype=np.float32),
        "wo": np.ascontiguousarray(L0_attn_Wo, dtype=np.float32),
        "sls": np.ascontiguousarray(L0_attn_score_log_scale, dtype=np.float32),
        "wfr": np.ascontiguousarray(L0_ffn_w_freq_real, dtype=np.float32),
        "wfi": np.ascontiguousarray(L0_ffn_w_freq_imag, dtype=np.float32),
        "bias": np.ascontiguousarray(L0_ffn_bias, dtype=np.float32),
        "Cm": cst["Cm"], "Sm": cst["Sm"], "identc": cst["ident"],
        "tric": cst["tri"], "e2c": cst["e2"], "e2tc": cst["e2t"],
        "e16c": cst["e16"], "ones1c": cst["ones1"],
    }
    x = np.ascontiguousarray(L0_attn_x, dtype=np.float32)
    in_maps = [dict(shared, x=np.ascontiguousarray(x[b])) for b in range(8)]
    res = run_bass_kernel_spmd(nc, in_maps, core_ids=list(range(8)))
    return np.stack([np.asarray(r["out"], dtype=np.float32) for r in res.results], axis=0)


# version 08
# speedup vs baseline: 1.0552x; 1.0086x over previous
import math
import numpy as np
import ml_dtypes
import concourse.bass as bass
import concourse.mybir as mybir
from concourse.bass_utils import run_bass_kernel_spmd
from contextlib import ExitStack

F32 = mybir.dt.float32
BF16 = mybir.dt.bfloat16
AF = mybir.ActivationFunctionType
ALU = mybir.AluOpType
AX = mybir.AxisListType

D = 1024
S = 2048
NH = 16
NT = 16
ND = 8
NG = 4
GH = 4
LN_EPS = 1e-5
RMS_EPS = 1e-6

_CONSTS = None


def _host_consts():
    global _CONSTS
    if _CONSTS is None:
        n = np.arange(D)
        nk = (np.outer(n, n) % D).astype(np.float64)
        theta = 2.0 * np.pi * nk / D
        Cm = (np.cos(theta) / 32.0).astype(ml_dtypes.bfloat16)
        Sm = (np.sin(theta) / 32.0).astype(ml_dtypes.bfloat16)
        ident = np.eye(128, dtype=ml_dtypes.bfloat16)
        tri = (np.arange(128)[None, :] >= np.arange(128)[:, None]).astype(ml_dtypes.bfloat16)
        e2 = np.zeros((128, 2), dtype=ml_dtypes.bfloat16)
        e2[:64, 0] = 1
        e2[64:, 1] = 1
        e2t = np.ascontiguousarray(e2.T)
        e16 = np.zeros((16, ND * 128), dtype=ml_dtypes.bfloat16)
        for c in range(ND):
            for p in range(128):
                e16[2 * c + p // 64, c * 128 + p] = 1
        ones1 = np.ones((1, 128), dtype=ml_dtypes.bfloat16)
        _CONSTS = dict(Cm=Cm, Sm=Sm, ident=ident, tri=tri, e2=e2, e2t=e2t,
                       e16=e16, ones1=ones1)
    return _CONSTS


DMA_KEYS = ("gpd", "spd")
ALL_KEYS = ("pe", "act", "dve", "gpc", "gpd", "spd")


class Ref:
    __slots__ = ("key", "idx")

    def __init__(self, key, idx):
        self.key = key
        self.idx = idx


class Prog:
    def __init__(self):
        self.ops = {k: [] for k in ALL_KEYS}
        self.needed = {k: set() for k in ALL_KEYS}

    def add(self, key, emit_fn, deps=(), war=()):
        kept = []
        for d in deps:
            if d is None:
                continue
            if d.key == key:
                assert d.idx < len(self.ops[key]), "same-engine forward dep"
            self.needed[d.key].add(d.idx)
            kept.append(d)
        for d in war:
            if d is None:
                continue
            if d.key == key:
                assert d.idx < len(self.ops[key]), "same-engine forward dep"
                continue
            self.needed[d.key].add(d.idx)
            kept.append(d)
        ref = Ref(key, len(self.ops[key]))
        self.ops[key].append((emit_fn, kept))
        return ref

    def finalize(self):
        self.val_after = {}
        for k in ALL_KEYS:
            vals, tot = [], 0
            for i in range(len(self.ops[k])):
                if k in DMA_KEYS:
                    tot += 16
                elif i in self.needed[k]:
                    tot += 1
                vals.append(tot)
            self.val_after[k] = vals

    def emit(self, key, eng, sems):
        last_wait = {k: 0 for k in ALL_KEYS}
        for i, (emit_fn, deps) in enumerate(self.ops[key]):
            for d in deps:
                v = self.val_after[d.key][d.idx]
                if v > last_wait[d.key]:
                    eng.wait_ge(sems[d.key], v)
                    last_wait[d.key] = v
            inst = emit_fn(eng)
            if key in DMA_KEYS:
                inst.then_inc(sems[key], 16)
            elif i in self.needed[key]:
                inst.then_inc(sems[key], 1)

    def last_refs(self):
        return [Ref(k, len(self.ops[k]) - 1) for k in ALL_KEYS if self.ops[k]]


class Slot:

    def __init__(self):
        self.writer = None
        self.readers = []

    def rdeps(self):
        return [self.writer] if self.writer else []

    def wdeps(self):
        return self.readers + ([self.writer] if self.writer else [])

    def read(self, ref):
        self.readers.append(ref)

    def wrote(self, ref):
        self.writer = ref
        self.readers = []

    def wrote_cont(self, ref):
        self.writer = ref


def build_program(taps=(), stop_after=None):
    nc = bass.Bass("TRN2", target_bir_lowering=False)
    from concourse.hw_specs import get_activation_tables
    _tables = list(get_activation_tables(nc.m.arch).keys())
    SET_NLEXP = _tables.index("natural_log_exp_and_others")
    SET_SILU = _tables.index("silu_and_others")

    d_x = nc.dram_tensor("x", [S, D], F32, kind="ExternalInput").ap()
    d_w = {w: nc.dram_tensor(w, [D, D], F32, kind="ExternalInput").ap()
           for w in ("wqa", "wka", "wqp", "wkp", "wv", "wo")}
    d_sls = nc.dram_tensor("sls", [NH], F32, kind="ExternalInput").ap()
    d_wfr = nc.dram_tensor("wfr", [D, D], F32, kind="ExternalInput").ap()
    d_wfi = nc.dram_tensor("wfi", [D, D], F32, kind="ExternalInput").ap()
    d_bias = nc.dram_tensor("bias", [D], F32, kind="ExternalInput").ap()
    d_C = nc.dram_tensor("Cm", [D, D], BF16, kind="ExternalInput").ap()
    d_S = nc.dram_tensor("Sm", [D, D], BF16, kind="ExternalInput").ap()
    d_ident = nc.dram_tensor("identc", [128, 128], BF16, kind="ExternalInput").ap()
    d_tri = nc.dram_tensor("tric", [128, 128], BF16, kind="ExternalInput").ap()
    d_e2 = nc.dram_tensor("e2c", [128, 2], BF16, kind="ExternalInput").ap()
    d_e2t = nc.dram_tensor("e2tc", [2, 128], BF16, kind="ExternalInput").ap()
    d_e16 = nc.dram_tensor("e16c", [16, ND * 128], BF16, kind="ExternalInput").ap()
    d_ones1 = nc.dram_tensor("ones1c", [1, 128], BF16, kind="ExternalInput").ap()
    d_out = nc.dram_tensor("out", [S, D], F32, kind="ExternalOutput").ap()
    d_cpsc = nc.dram_tensor("cpsc", [NH], F32, kind="Internal").ap()

    P = Prog()
    tap_done = set()

    with ExitStack() as ctx:
        sb = lambda n, s, d: ctx.enter_context(nc.sbuf_tensor(n, s, d))
        psa = lambda n, s: ctx.enter_context(nc.psum_tensor(n, s, F32))

        x_bf = sb("x_bf", [128, NT * D], BF16)
        hfm = sb("hfm", [128, ND * S], BF16)
        identb = sb("identb", [128, 128], BF16)
        trib = sb("trib", [128, 128], BF16)
        e2b = sb("e2b", [128, 2], BF16)
        e2tb = sb("e2tb", [2, 128], BF16)
        e16b = sb("e16b", [16, ND * 128], BF16)
        ones1b = sb("ones1b", [1, 128], BF16)
        slsrow = sb("slsrow", [1, NH], F32)
        crow8 = sb("crow8", [1, NH], F32)
        mnegrow = sb("mnegrow", [1, NH], BF16)
        mneg_bc = sb("mneg_bc", [128, NH], F32)
        cpair = sb("cpair", [2, ND], F32)
        ccol = sb("ccol", [128, 4], F32)
        stg = sb("stg", [65, 1024], BF16)
        io_stage = sb("io_stage", [128, D], F32)
        ln_s = sb("ln_s", [128, D], F32)
        sq_s = sb("sq_s", [128, D], BF16)
        htm_s = sb("htm", [128, 2 * D], BF16)
        stat8 = sb("stat8", [128, 8], F32)
        biasrow = sb("biasrow", [1, D], BF16)

        pA = psa("pA", [128, 2048])
        pB = psa("pB", [128, 1024])
        pC = psa("pC", [128, 1024])

        sems = {k: ctx.enter_context(nc.semaphore("sem_" + k)) for k in ALL_KEYS}

        sl = dict(pA0=Slot(), pA1=Slot(), pA2=Slot(), pA3=Slot(),
                  pB0=Slot(), pB1=Slot(), pC0=Slot(), pC1=Slot(),
                  x_bf=Slot(), hfm=Slot(), stg=Slot(),
                  io=Slot(), ln=Slot(), htm=Slot(), stat=Slot())
        PAall = [sl["pA0"], sl["pA1"], sl["pA2"], sl["pA3"]]
        PBf = [sl["pB0"], sl["pB1"]]
        PCf = [sl["pC0"], sl["pC1"]]

        def op(key, fn, reads=(), writes=(), writes_cont=(), also=()):
            deps = list(also)
            war = []
            for s in reads:
                deps += s.rdeps()
            for s in writes:
                war += s.wdeps()
            ref = P.add(key, fn, deps, war=war)
            for s in reads:
                s.read(ref)
            for s in writes:
                s.wrote(ref)
            for s in writes_cont:
                s.wrote_cont(ref)
            return ref

        def act_load(set_id):
            return P.add("act", lambda e, sid=set_id: e.add_instruction(
                mybir.InstLoadActFuncSet(
                    name=e.bass.get_next_instruction_name(),
                    act_func_set_id=sid, ins=[], outs=[])))

        dbg_outs = {}

        def tap(name, ap, also=()):
            if name not in taps or name in tap_done:
                return
            tap_done.add(name)
            shp = [dd[1] for dd in ap.ap]
            dt = ap.tensor.dtype if hasattr(ap.tensor, "dtype") else F32
            d_dbg = nc.dram_tensor("dbg_" + name, shp, dt, kind="ExternalOutput").ap()
            dbg_outs[name] = d_dbg
            P.add("spd", lambda e, _a=ap, _d=d_dbg: e.dma_start(_d, _a), list(also))

        r_ident = P.add("spd", lambda e: e.dma_start(identb[:, :], d_ident))
        r_tri = P.add("spd", lambda e: e.dma_start(trib[:, :], d_tri))
        r_e2 = P.add("spd", lambda e: e.dma_start(e2b[:, :], d_e2))
        r_e2t = P.add("spd", lambda e: e.dma_start(e2tb[:, :], d_e2t))
        r_e16 = P.add("spd", lambda e: e.dma_start(e16b[:, :], d_e16))
        r_ones1 = P.add("spd", lambda e: e.dma_start(ones1b[:, :], d_ones1))
        r_sls = P.add("spd", lambda e: e.dma_start(slsrow[:, :], d_sls.rearrange("(a b) -> a b", a=1)))
        r_biasrow = P.add("gpd", lambda e: e.dma_start(biasrow[:, :], d_bias.rearrange("(a b) -> a b", a=1)))

        P.add("dve", lambda e: e.memset(ccol[:, 0:1], 1.0))
        P.add("dve", lambda e: e.memset(ccol[:, 1:2], RMS_EPS))
        P.add("dve", lambda e: e.memset(ccol[:, 2:3], LN_EPS))
        r_ccol = P.add("dve", lambda e: e.memset(ccol[:, 3:4], -math.pi / 2))

        act_load(SET_NLEXP)
        r_crow = P.add("act", lambda e: e.activation(crow8[:, :], slsrow[:, :], AF.Exp), [r_sls])
        r_crow = P.add("dve", lambda e: e.tensor_scalar_mul(crow8[:, :], crow8[:, :], 0.125), [r_crow])
        r_mneg = P.add("dve", lambda e: e.tensor_scalar_mul(mnegrow[:, :], crow8[:, :], -64.0), [r_crow])
        r_mm = op("pe", lambda e: e.matmul(pB[:, 0:NH], lhsT=ones1b[:, :], rhs=mnegrow[:, :],
                                           start=True, stop=True),
                  writes=[sl["pB0"]], also=[r_mneg, r_ones1])
        r_mnegbc = op("dve", lambda e: e.tensor_copy(mneg_bc[:, :], pB[:, 0:NH]),
                      reads=[sl["pB0"]])
        r_cpd = P.add("gpd", lambda e: e.dma_start(d_cpsc.rearrange("(a b) -> a b", a=1), crow8[:, :]), [r_crow])
        r_cpair = P.add("gpd", lambda e: e.dma_start(
            cpair[:, :], d_cpsc.rearrange("(cc p) -> p cc", p=2),
            allow_slow_non_contiguous=True), [r_cpd])

        r_xload = [P.add("gpd", lambda e, t=t: e.dma_start(
            x_bf[:, t * D:(t + 1) * D], d_x[t * 128:(t + 1) * 128, :])) for t in range(NT)]

        def emit_ln_stats_apply(src_ap, dst_ap, src_slot, dst_slot, par, extra=()):
            sb_ = 4 * par
            st = lambda i: stat8[:, sb_ + i:sb_ + i + 1]
            r1 = op("dve", lambda e: e.reduce_sum(st(0), src_ap, axis=AX.X),
                    reads=[src_slot], writes=[sl["stat"]], also=list(extra))
            r2 = op("dve", lambda e: e.tensor_scalar_mul(st(0), st(0), 1.0 / D),
                    writes=[sl["stat"]], also=[r1])
            r3 = op("act", lambda e: e.activation(
                sq_s[:, :], src_ap, AF.Square, accum_out=st(1)),
                reads=[src_slot], writes=[sl["stat"]], also=list(extra))
            r4 = op("dve", lambda e: e.tensor_tensor(st(3), st(0), st(0), ALU.mult),
                    writes=[sl["stat"]], also=[r2])
            r5 = op("dve", lambda e: e.scalar_tensor_tensor(
                st(2), st(1), 1.0 / D, st(3), ALU.mult, ALU.subtract),
                writes=[sl["stat"]], also=[r3, r4])
            r6 = op("act", lambda e: e.activation(st(2), st(2), AF.Ln, bias=ccol[:, 2:3]),
                    writes=[sl["stat"]], also=[r5, r_ccol])
            r7 = op("act", lambda e: e.activation(st(2), st(2), AF.Exp, scale=-0.5),
                    writes=[sl["stat"]], also=[r6])
            r8 = op("dve", lambda e: e.tensor_scalar(
                dst_ap, src_ap, st(0), st(2), ALU.subtract, ALU.mult),
                reads=[src_slot, sl["stat"]], writes=[dst_slot], also=[r7])
            return r8

        def emit_transpose(src_tile_fn, dst_tensor, t, src_slot, dst_slot,
                           ptiles=None, extra=()):
            if ptiles is None:
                ptiles = [(pB, sl["pB0"], 0), (pC, sl["pC0"], 0)]
            for half in range(2):
                ptile, pslot, poff = ptiles[half]
                for q in range(4):
                    dc = half * 4 + q
                    op("pe", lambda e, dc=dc, q=q, ptile=ptile, poff=poff: e.matmul(
                        ptile[:, poff + q * 128: poff + (q + 1) * 128],
                        lhsT=src_tile_fn(dc), rhs=identb[:, :], start=True, stop=True),
                        reads=[src_slot],
                        writes=[pslot] if q == 0 else [],
                        writes_cont=[pslot] if q > 0 else [],
                        also=list(extra) + ([r_ident] if q == 0 else []))
                dst = bass.AP(dst_tensor[:, :].tensor, (half * 4) * S + t * 128,
                              [dst_tensor[:, :].ap[0], [S, 4], [1, 128]])
                op("act", lambda e, dst=dst, ptile=ptile, poff=poff: e.activation(
                    dst, ptile[:, poff:poff + 512], AF.Copy),
                    reads=[pslot], writes=[dst_slot])

        for t in range(NT):
            par = t % 2
            hview = htm_s[:, par * D:(par + 1) * D]
            emit_ln_stats_apply(x_bf[:, t * D:(t + 1) * D], hview,
                                sl["x_bf"], sl["htm"], par, extra=[r_xload[t]])
            emit_transpose(lambda dc, par=par: htm_s[:, par * D + dc * 128: par * D + (dc + 1) * 128],
                           hfm, t, sl["htm"], sl["hfm"])

        tap("h0fm", hfm[:, :], also=P.last_refs())
        if stop_after == "phase0":
            return _finish(nc, P, sems, dbg_outs)

        attn_ctx = ctx.enter_context(ExitStack())
        sba = lambda n, s, d: attn_ctx.enter_context(nc.sbuf_tensor(n, s, d))
        o_fm = sba("o_fm", [128, ND * S], BF16)
        Lb = sba("Lb", [NH, S], BF16)
        rlb = sba("rlb", [NH, S], BF16)
        Qcs = sba("Qcs", [128, GH * S], BF16)
        Kcs = sba("Kcs", [128, GH * S], BF16)
        V_aug = sba("V_aug", [128, NT * GH * 65], BF16)
        WTa = sba("WTa", [128, ND * 256], BF16)
        WTb = sba("WTb", [128, ND * 256], BF16)
        WTv = sba("WTv", [128, ND * 256], BF16)
        Wstage = sba("Wstage", [128, 2 * D], BF16)
        qx = sba("qx", [128, S], BF16)
        Pt = sba("Pt", [128, 3072], BF16)
        t14 = sba("t14", [128, S], BF16)
        qa_s = sba("qa_s", [128, 512], BF16)
        q2_s = sba("q2_s", [128, 512], BF16)
        t_s = sba("t_s", [128, 512], BF16)
        tn_s = sba("tn_s", [128, 512], BF16)
        sinv_s = sba("sinv_s", [128, 512], BF16)
        cosn_s = sba("cosn_s", [128, 512], BF16)
        rstd2b_s = sba("rstd2b_s", [2, 512], BF16)
        ln2_s = sba("ln2_s", [128, D], F32)

        for nm in ("Qcs", "Kcs", "V_aug", "Wstage", "qx", "Pt0", "Pt1", "Pt2",
                   "qa_s", "q2_s", "t_s", "tn_s",
                   "sinv_s", "cosn_s", "t14", "rstd2",
                   "WTa", "WTb", "WTv", "o_fm", "Lb", "rlb", "ln2"):
            sl[nm] = Slot()

        va = V_aug[:, :]
        va_ones = bass.AP(va.tensor, 64, [va.ap[0], [260, NT], [65, GH], [0, 1]])
        op("dve", lambda e: e.memset(va_ones, 1.0), writes=[sl["V_aug"]])

        def emit_wslice(g, w, WTdst, dname):
            rl_ = []
            for fc in range(2):
                rl_.append(op("gpd", lambda e, w=w, fc=fc, g=g: e.dma_start(
                    Wstage[:, fc * D:(fc + 1) * D],
                    d_w[w][g * 256 + fc * 128: g * 256 + (fc + 1) * 128, :]),
                    writes=[sl["Wstage"]]))
            for grp4 in range(4):
                ptile, pslot = (pB, sl["pB0"]) if grp4 % 2 == 0 else (pC, sl["pC0"])
                for pos in range(4):
                    idx = grp4 * 4 + pos
                    dc, fc = idx // 2, idx % 2
                    op("pe", lambda e, dc=dc, fc=fc, pos=pos, ptile=ptile: e.matmul(
                        ptile[:, pos * 128:(pos + 1) * 128],
                        lhsT=Wstage[:, fc * D + dc * 128: fc * D + (dc + 1) * 128],
                        rhs=identb[:, :], start=True, stop=True),
                        reads=[sl["Wstage"]],
                        writes=[pslot] if pos == 0 else [],
                        writes_cont=[pslot] if pos > 0 else [],
                        also=rl_ if idx == 0 else [])
                op("dve", lambda e, grp4=grp4, ptile=ptile, WTdst=WTdst: e.tensor_copy(
                    WTdst[:, grp4 * 512:(grp4 + 1) * 512], ptile[:, 0:512]),
                    reads=[pslot], writes=[sl[dname]])

        def emit_proj_pair(g, WTamp, aname, WTphi, pname, dst, dst_name, scale_c):
            for c in range(2):
                act_load(SET_NLEXP)
                for js in range(4):
                    tile, X0, X1 = (pB, sl["pB0"], sl["pB1"]) if js % 2 == 0 else (pC, sl["pC0"], sl["pC1"])
                    for dc in range(ND):
                        op("pe", lambda e, dc=dc, c=c, js=js, tile=tile, WTamp=WTamp: e.matmul(
                            tile[:, 0:512],
                            lhsT=WTamp[:, dc * 256 + c * 128: dc * 256 + (c + 1) * 128],
                            rhs=hfm[:, dc * S + js * 512: dc * S + (js + 1) * 512],
                            start=(dc == 0), stop=(dc == ND - 1)),
                            reads=[sl[aname], sl["hfm"]],
                            writes=[X0] if dc == 0 else [],
                            writes_cont=[X0] if dc > 0 else [])
                    r_sp = op("act", lambda e, tile=tile: e.activation(qa_s[:, :], tile[:, 0:512], AF.Exp),
                       reads=[X0], writes=[sl["qa_s"]])
                    op("act", lambda e: e.activation(qa_s[:, :], qa_s[:, :], AF.Ln, bias=ccol[:, 0:1]),
                       writes=[sl["qa_s"]], also=[r_ccol, r_sp])
                    op("dve", lambda e: e.tensor_mul(q2_s[:, :], qa_s[:, :], qa_s[:, :]),
                       reads=[sl["qa_s"]], writes=[sl["q2_s"]])
                    op("pe", lambda e, tile=tile: e.matmul(
                        tile[0:2, 512:1024], lhsT=e2b[:, :], rhs=q2_s[:, :],
                        start=True, stop=True),
                       reads=[sl["q2_s"]], writes=[X1], also=[r_e2])
                    r_rln = op("act", lambda e, tile=tile: e.activation(
                        rstd2b_s[:, :], tile[0:2, 512:1024], AF.Ln,
                        scale=1.0 / 64.0, bias=ccol[0:2, 1:2]),
                       reads=[X1], writes=[sl["rstd2"]])
                    op("act", lambda e: e.activation(rstd2b_s[:, :], rstd2b_s[:, :], AF.Exp, scale=-0.5),
                       writes=[sl["rstd2"]], also=[r_rln])
                    if scale_c:
                        op("dve", lambda e, g=g, c=c: e.tensor_scalar_mul(
                            rstd2b_s[:, :], rstd2b_s[:, :], cpair[:, g * 2 + c: g * 2 + c + 1]),
                           writes=[sl["rstd2"]], also=[r_cpair])
                    op("pe", lambda e, tile=tile: e.matmul(
                        tile[:, 512:1024], lhsT=e2tb[:, :], rhs=rstd2b_s[:, :],
                        start=True, stop=True),
                       reads=[sl["rstd2"]], writes=[X1], also=[r_e2t])
                    op("dve", lambda e, js=js, tile=tile: e.tensor_tensor(
                        t14[:, js * 512:(js + 1) * 512], qa_s[:, :], tile[:, 512:1024], ALU.mult),
                       reads=[sl["qa_s"], X1], writes=[sl["t14"]])
                act_load(SET_SILU)
                for js in range(4):
                    tile, X0, X1 = (pB, sl["pB0"], sl["pB1"]) if js % 2 == 0 else (pC, sl["pC0"], sl["pC1"])
                    for dc in range(ND):
                        op("pe", lambda e, dc=dc, c=c, js=js, tile=tile, WTphi=WTphi: e.matmul(
                            tile[:, 0:512],
                            lhsT=WTphi[:, dc * 256 + c * 128: dc * 256 + (c + 1) * 128],
                            rhs=hfm[:, dc * S + js * 512: dc * S + (js + 1) * 512],
                            start=(dc == 0), stop=(dc == ND - 1)),
                            reads=[sl[pname], sl["hfm"]],
                            writes=[X0] if dc == 0 else [],
                            writes_cont=[X0] if dc > 0 else [])
                    op("act", lambda e, tile=tile: e.activation(t_s[:, :], tile[:, 0:512], AF.Tanh),
                       reads=[X0], writes=[sl["t_s"]])
                    op("dve", lambda e: e.tensor_scalar_mul(tn_s[:, :], t_s[:, :], -1.0),
                       reads=[sl["t_s"]], writes=[sl["tn_s"]])
                    op("dve", lambda e: e.tensor_max(tn_s[:, :], t_s[:, :], tn_s[:, :]),
                       reads=[sl["t_s"]], writes=[sl["tn_s"]])
                    op("act", lambda e: e.activation(sinv_s[:, :], t_s[:, :], AF.Sin, scale=math.pi),
                       reads=[sl["t_s"]], writes=[sl["sinv_s"]])
                    op("act", lambda e: e.activation(cosn_s[:, :], tn_s[:, :], AF.Sin,
                                                     scale=math.pi, bias=ccol[:, 3:4]),
                       reads=[sl["tn_s"]], writes=[sl["cosn_s"]])
                    lo0 = (2 * c) * S + js * 512
                    lo1 = (2 * c + 1) * S + js * 512
                    t1lo = t14[0:64, js * 512:(js + 1) * 512]
                    t1hi = t14[64:128, js * 512:(js + 1) * 512]
                    op("dve", lambda e, lo0=lo0, t1lo=t1lo: e.scalar_tensor_tensor(
                        dst[0:64, lo0:lo0 + 512], cosn_s[0:64, :], -1.0, t1lo,
                        ALU.mult, ALU.mult),
                       reads=[sl["cosn_s"], sl["t14"]], writes=[sl[dst_name]])
                    op("dve", lambda e, js=js, t1hi=t1hi: e.scalar_tensor_tensor(
                        qx[64:128, js * 512:(js + 1) * 512], cosn_s[64:128, :], -1.0,
                        t1hi, ALU.mult, ALU.mult),
                       reads=[sl["cosn_s"], sl["t14"]], writes=[sl["qx"]])
                    op("dve", lambda e, js=js, t1lo=t1lo: e.tensor_mul(
                        qx[0:64, js * 512:(js + 1) * 512], t1lo, sinv_s[0:64, :]),
                       reads=[sl["sinv_s"], sl["t14"]], writes=[sl["qx"]])
                    op("dve", lambda e, lo1=lo1, t1hi=t1hi: e.tensor_mul(
                        dst[64:128, lo1:lo1 + 512], t1hi, sinv_s[64:128, :]),
                       reads=[sl["sinv_s"], sl["t14"]], writes=[sl[dst_name]])
                op("gpd", lambda e, c=c: e.dma_start(dst[0:64, (2 * c + 1) * S:(2 * c + 2) * S],
                                                     qx[64:128, :]),
                   reads=[sl["qx"]], writes=[sl[dst_name]])
                op("gpd", lambda e, c=c: e.dma_start(dst[64:128, (2 * c) * S:(2 * c) * S + S],
                                                     qx[0:64, :]),
                   reads=[sl["qx"]], writes=[sl[dst_name]])

        def emit_vproj(g):
            for t in range(NT):
                tile, pslot = (pB, sl["pB0"]) if t % 2 == 0 else (pC, sl["pC0"])
                for dc in range(ND):
                    op("pe", lambda e, dc=dc, t=t, tile=tile: e.matmul(
                        tile[:, 0:256], lhsT=hfm[:, dc * S + t * 128: dc * S + (t + 1) * 128],
                        rhs=WTv[:, dc * 256:(dc + 1) * 256],
                        start=(dc == 0), stop=(dc == ND - 1)),
                        reads=[sl["hfm"], sl["WTv"]],
                        writes=[pslot] if dc == 0 else [],
                        writes_cont=[pslot] if dc > 0 else [])
                dstp = bass.AP(va.tensor, t * 260, [va.ap[0], [65, GH], [1, 64]])
                op("dve", lambda e, dstp=dstp, tile=tile: e.tensor_copy(
                    dstp, tile[:, 0:256].rearrange("p (a b) -> p a b", a=GH)),
                   reads=[pslot], writes=[sl["V_aug"]])

        def emit_attention(g):
            act_load(SET_NLEXP)
            PAo = [sl["pA0"], sl["pA1"]]
            stiles = [(pB, sl["pB0"], sl["pB1"], 0),
                      (pC, sl["pC0"], sl["pC1"], 0),
                      (pA, sl["pA2"], sl["pA3"], 1024)]
            ptslots = (sl["Pt0"], sl["Pt1"], sl["Pt2"])
            pair_i = 0
            for hl in range(GH):
                h = g * GH + hl
                for qh in range(2):
                    end = 1024 * (qh + 1)
                    for ki in range(NT):
                        if 128 * ki >= end:
                            break
                        qs0 = max(128 * ki, 1024 * qh)
                        blocks = [(qs0, min(end, (qs0 // 512 + 1) * 512) - qs0)]
                        if qs0 + blocks[0][1] < end:
                            blocks.append((qs0 + blocks[0][1], 512))
                        si = pair_i % 3
                        tile, X0, X1, tbase = stiles[si]
                        ptslot = ptslots[si]
                        ptbase = 1024 * si
                        w0 = blocks[0][1]
                        lo = 512 - w0
                        hi = 512 + (512 if len(blocks) > 1 else 0)
                        for bi, (qs, w) in enumerate(blocks):
                            poff = 512 - w0 if bi == 0 else 512
                            xslot = X0 if bi == 0 else X1
                            op("pe", lambda e, hl=hl, ki=ki, qs=qs, w=w, poff=poff,
                               tile=tile, tbase=tbase: e.matmul(
                                tile[:, tbase + poff: tbase + poff + w],
                                lhsT=Kcs[:, hl * S + ki * 128: hl * S + (ki + 1) * 128],
                                rhs=Qcs[:, hl * S + qs: hl * S + qs + w], start=True, stop=True),
                                reads=[sl["Kcs"], sl["Qcs"]], writes=[xslot])
                        r_exp = op("act", lambda e, lo=lo, hi=hi, h=h, ptbase=ptbase,
                                   tile=tile, tbase=tbase: e.activation(
                            Pt[:, ptbase + lo: ptbase + hi], tile[:, tbase + lo: tbase + hi],
                            AF.Exp, bias=mneg_bc[:, h:h + 1]),
                            reads=[X0] + ([X1] if len(blocks) > 1 else []),
                            writes=[ptslot], also=[r_mnegbc])
                        r_last = r_exp
                        if qs0 == 128 * ki:
                            r_last = op("dve", lambda e, lo=lo, ptbase=ptbase: e.tensor_mul(
                                Pt[:, ptbase + lo: ptbase + lo + 128],
                                Pt[:, ptbase + lo: ptbase + lo + 128], trib[:, :]),
                                writes=[ptslot], also=[r_tri])
                        for bi, (qs, w) in enumerate(blocks):
                            poff = 512 - w0 if bi == 0 else 512
                            gb = qs // 512
                            qs_l = qs - 1024 * qh
                            first_write = (ki == 0 and qs_l == 0)
                            op("pe", lambda e, ki=ki, hl=hl, qs_l=qs_l, w=w, poff=poff,
                               ptbase=ptbase, gb=gb: e.matmul(
                                pA[0:65, qs_l:qs_l + w],
                                lhsT=V_aug[:, ki * 260 + hl * 65: ki * 260 + (hl + 1) * 65],
                                rhs=Pt[:, ptbase + poff: ptbase + poff + w],
                                start=(ki == 0), stop=(ki == 4 * gb + 3), skip_group_check=True),
                                reads=[ptslot, sl["V_aug"]],
                                writes=PAo if first_write else [],
                                writes_cont=PAo if not first_write else [],
                                also=[r_last])
                        pair_i += 1
                    r_ev = op("dve", lambda e: e.tensor_copy(stg[:, :], pA[0:65, 0:1024]),
                              reads=PAo, writes=[sl["stg"]])
                    c, half = h // 2, h % 2
                    op("gpd", lambda e, c=c, half=half, qh=qh: e.dma_start(
                        o_fm[64 * half:64 * half + 64,
                             c * S + 1024 * qh: c * S + 1024 * qh + 1024], stg[0:64, :]),
                       reads=[sl["stg"]], writes=[sl["o_fm"]])
                    op("gpd", lambda e, h=h, qh=qh: e.dma_start(
                        Lb[h:h + 1, 1024 * qh: 1024 * qh + 1024], stg[64:65, :]),
                       reads=[sl["stg"]], writes=[sl["Lb"]])

        for g in range(NG):
            emit_wslice(g, "wqa", WTa, "WTa")
            emit_wslice(g, "wqp", WTb, "WTb")
            if g == 0:
                emit_wslice(g, "wv", WTv, "WTv")
            emit_proj_pair(g, WTa, "WTa", WTb, "WTb", Qcs, "Qcs", True)
            emit_wslice(g, "wka", WTa, "WTa")
            emit_wslice(g, "wkp", WTb, "WTb")
            emit_proj_pair(g, WTa, "WTa", WTb, "WTb", Kcs, "Kcs", False)
            emit_vproj(g)
            if g < NG - 1:
                emit_wslice(g + 1, "wv", WTv, "WTv")
            if g == 0:
                tap("Qcs0", Qcs[:, :], also=P.last_refs())
                tap("Kcs0", Kcs[:, :], also=P.last_refs())
                tap("Vaug0", V_aug[:, :], also=P.last_refs())
            emit_attention(g)
            if g == 0:
                tap("stg0", stg[:, :], also=P.last_refs())
            if stop_after == "group%d" % g:
                tap("o_fm", o_fm[:, :], also=P.last_refs())
                tap("Lb", Lb[:, :], also=P.last_refs())
                return _finish(nc, P, sems, dbg_outs)

        tap("o_fm", o_fm[:, :], also=P.last_refs())
        tap("Lb", Lb[:, :], also=P.last_refs())

        def _emit_recip(e):
            with nc.allow_low_precision(reason="1/l stored bf16; same scale as bf16 matmul rounding"):
                return e.reciprocal(rlb[:, :], Lb[:, :])
        r_rec = op("dve", _emit_recip, reads=[sl["Lb"]], writes=[sl["rlb"]])
        sl["rlb"].readers.append(r_rec)
        for c in range(ND):
            for js in range(4):
                tile, pslot = (pB, sl["pB0"]) if (c * 4 + js) % 2 == 0 else (pC, sl["pC0"])
                op("pe", lambda e, c=c, js=js, tile=tile: e.matmul(
                    tile[:, 0:512], lhsT=e16b[:, c * 128:(c + 1) * 128],
                    rhs=rlb[:, js * 512:(js + 1) * 512], start=True, stop=True),
                   reads=[sl["rlb"]], writes=[pslot], also=[r_e16])
                op("dve", lambda e, c=c, js=js, tile=tile: e.tensor_tensor(
                    o_fm[:, c * S + js * 512: c * S + (js + 1) * 512],
                    o_fm[:, c * S + js * 512: c * S + (js + 1) * 512],
                    tile[:, 0:512], ALU.mult),
                   reads=[pslot], writes=[sl["o_fm"]])

        tap("o_norm", o_fm[:, :], also=P.last_refs())

        WoT = Qcs
        sl["WoT"] = sl["Qcs"]
        for fcpair in range(4):
            rl_ = []
            for k in range(2):
                foc = fcpair * 2 + k
                rl_.append(op("gpd", lambda e, foc=foc, k=k: e.dma_start(
                    Wstage[:, k * D:(k + 1) * D], d_w["wo"][foc * 128:(foc + 1) * 128, :]),
                    writes=[sl["Wstage"]]))
            for grp4 in range(4):
                ptile, pslot = (pB, sl["pB0"]) if grp4 % 2 == 0 else (pC, sl["pC0"])
                evs = []
                for pos in range(4):
                    idx = grp4 * 4 + pos
                    k, fc = idx // 8, idx % 8
                    op("pe", lambda e, k=k, fc=fc, pos=pos, ptile=ptile: e.matmul(
                        ptile[:, pos * 128:(pos + 1) * 128],
                        lhsT=Wstage[:, k * D + fc * 128: k * D + (fc + 1) * 128],
                        rhs=identb[:, :], start=True, stop=True),
                        reads=[sl["Wstage"]],
                        writes=[pslot] if pos == 0 else [],
                        writes_cont=[pslot] if pos > 0 else [],
                        also=rl_ if idx == 0 else [])
                    evs.append((k, fc, pos))
                for (k, fc, pos) in evs:
                    foc = fcpair * 2 + k
                    op("dve", lambda e, fc=fc, foc=foc, pos=pos, ptile=ptile: e.tensor_copy(
                        WoT[:, fc * D + foc * 128: fc * D + (foc + 1) * 128],
                        ptile[:, pos * 128:(pos + 1) * 128]),
                        reads=[pslot], writes=[sl["WoT"]])

        act_load(SET_NLEXP)

        def emit_postblock(t, ptile, pslots, ln_tile, ln_slot, out_dst, extra_dep=()):
            par = t % 2
            sb_ = 4 * par
            st = lambda i: stat8[:, sb_ + i:sb_ + i + 1]
            r_ev = op("act", lambda e, ptile=ptile, ln_tile=ln_tile: e.activation(
                ln_tile[:, :], ptile[:, 0:1024], AF.Copy, accum_out=st(0)),
                reads=pslots, writes=[ln_slot, sl["stat"]], also=list(extra_dep))
            r2 = op("dve", lambda e: e.tensor_scalar_mul(st(0), st(0), 1.0 / D),
                    writes=[sl["stat"]], also=[r_ev])
            r3 = op("act", lambda e, ln_tile=ln_tile: e.activation(
                sq_s[:, :], ln_tile[:, :], AF.Square, accum_out=st(1)),
                reads=[ln_slot], writes=[sl["stat"]])
            r4 = op("dve", lambda e: e.tensor_tensor(st(3), st(0), st(0), ALU.mult),
                    writes=[sl["stat"]], also=[r2])
            r5 = op("dve", lambda e: e.scalar_tensor_tensor(
                st(2), st(1), 1.0 / D, st(3), ALU.mult, ALU.subtract),
                writes=[sl["stat"]], also=[r3, r4])
            r6 = op("act", lambda e: e.activation(st(2), st(2), AF.Ln, bias=ccol[:, 2:3]),
                    writes=[sl["stat"]], also=[r5])
            r7 = op("act", lambda e: e.activation(st(2), st(2), AF.Exp, scale=-0.5),
                    writes=[sl["stat"]], also=[r6])
            r8 = op("dve", lambda e, ln_tile=ln_tile: e.tensor_scalar(
                ln_tile[:, :], ln_tile[:, :], st(0), st(2), ALU.subtract, ALU.mult),
                reads=[sl["stat"]], writes=[ln_slot], also=[r7])
            return r8

        for t in range(NT):
            k3 = t % 2
            ptile = pB if k3 == 0 else pC
            pslots = PBf if k3 == 0 else PCf
            for js in range(2):
                for c in range(ND):
                    first_op = (js == 0 and c == 0)
                    op("pe", lambda e, c=c, js=js, t=t, ptile=ptile: e.matmul(
                        ptile[:, js * 512:(js + 1) * 512],
                        lhsT=o_fm[:, c * S + t * 128: c * S + (t + 1) * 128],
                        rhs=WoT[:, c * D + js * 512: c * D + (js + 1) * 512],
                        start=(c == 0), stop=(c == ND - 1)),
                        reads=[sl["o_fm"], sl["WoT"]],
                        writes=pslots if first_op else [],
                        writes_cont=pslots if not first_op else [])
            ln_tile = ln_s if t % 2 == 0 else ln2_s
            ln_slot = sl["ln"] if t % 2 == 0 else sl["ln2"]
            r8 = emit_postblock(t, ptile, pslots, ln_tile, ln_slot, None)
            op("dve", lambda e, t=t, ln_tile=ln_tile: e.tensor_tensor(
                x_bf[:, t * D:(t + 1) * D], x_bf[:, t * D:(t + 1) * D], ln_tile[:, :], ALU.add),
               reads=[ln_slot], writes=[sl["x_bf"]], also=[r8])
            par = t % 2
            hview = htm_s[:, par * D:(par + 1) * D]
            emit_ln_stats_apply(x_bf[:, t * D:(t + 1) * D], hview, sl["x_bf"], sl["htm"], par)
            emit_transpose(lambda dc, par=par: htm_s[:, par * D + dc * 128: par * D + (dc + 1) * 128],
                           hfm, t, sl["htm"], sl["hfm"],
                           ptiles=[(pA, sl["pA0"], 0), (pA, sl["pA1"], 512)])

        tap("x1", x_bf[:, :], also=P.last_refs())
        tap("h1fm", hfm[:, :], also=P.last_refs())
        if stop_after == "phaseC":
            return _finish(nc, P, sems, dbg_outs)

        barrier = P.last_refs()
        attn_ctx.close()
        sbm = lambda n, s, d: ctx.enter_context(nc.sbuf_tensor(n, s, d))
        Wr = sbm("Wr", [128, ND * D], BF16)
        Wi = sbm("Wi", [128, ND * D], BF16)
        Csb = sbm("Csb", [128, ND * D], BF16)
        Ssb = sbm("Ssb", [128, ND * D], BF16)
        A_T = sbm("A_T", [128, ND * D], BF16)
        B_Tn = sbm("B_Tn", [128, ND * D], BF16)
        Mb = sbm("Mb", [128, ND * D], BF16)
        lnd_s = sbm("lnd_s", [128, D], F32)
        io2 = sbm("io2", [128, D], F32)
        for nm in ("Wr", "Wi", "Csb", "Ssb", "A_T", "B_Tn", "Mb", "lnd", "io2"):
            sl[nm] = Slot()

        for dc in range(ND):
            r = P.add("gpd", lambda e, dc=dc: e.dma_start(
                Wr[:, dc * D:(dc + 1) * D], d_wfr[dc * 128:(dc + 1) * 128, :]), barrier)
            sl["Wr"].wrote_cont(r) if dc else sl["Wr"].wrote(r)
            r = P.add("gpd", lambda e, dc=dc: e.dma_start(
                Wi[:, dc * D:(dc + 1) * D], d_wfi[dc * 128:(dc + 1) * 128, :]), barrier)
            sl["Wi"].wrote_cont(r) if dc else sl["Wi"].wrote(r)
            r = P.add("spd", lambda e, dc=dc: e.dma_start(
                Csb[:, dc * D:(dc + 1) * D], d_C[dc * 128:(dc + 1) * 128, :]), barrier)
            sl["Csb"].wrote_cont(r) if dc else sl["Csb"].wrote(r)
            r = P.add("spd", lambda e, dc=dc: e.dma_start(
                Ssb[:, dc * D:(dc + 1) * D], d_S[dc * 128:(dc + 1) * 128, :]), barrier)
            sl["Ssb"].wrote_cont(r) if dc else sl["Ssb"].wrote(r)

        def emit_MM_accum(dst, dst_name, lhs_list):
            nterm = len(lhs_list)
            for outc in range(ND):
                tile = pB if outc % 2 == 0 else pC
                Xs = PBf if outc % 2 == 0 else PCf
                for js in range(2):
                    for ti, (lt, lname, rt, rname) in enumerate(lhs_list):
                        for dc in range(ND):
                            first = (ti == 0 and dc == 0)
                            last = (ti == nterm - 1 and dc == ND - 1)
                            is_first_op = (js == 0 and first)
                            op("pe", lambda e, lt=lt, rt=rt, dc=dc, outc=outc, js=js,
                               first=first, last=last, tile=tile: e.matmul(
                                   tile[:, js * 512:(js + 1) * 512],
                                   lhsT=lt[:, dc * D + outc * 128: dc * D + (outc + 1) * 128],
                                   rhs=rt[:, dc * D + js * 512: dc * D + (js + 1) * 512],
                                   start=first, stop=last, skip_group_check=True),
                               reads=[sl[lname], sl[rname]],
                               writes=Xs if is_first_op else [],
                               writes_cont=Xs if not is_first_op else [])
                op("dve", lambda e, outc=outc, tile=tile: e.tensor_copy(
                    dst[:, outc * D:(outc + 1) * D], tile[:, :]),
                    reads=Xs, writes=[sl[dst_name]])

        emit_MM_accum(A_T, "A_T", [(Wr, "Wr", Csb, "Csb"), (Wi, "Wi", Ssb, "Ssb")])
        op("dve", lambda e: e.tensor_scalar_mul(Wi[:, :], Wi[:, :], -1.0), writes=[sl["Wi"]])
        emit_MM_accum(B_Tn, "B_Tn", [(Wr, "Wr", Ssb, "Ssb"), (Wi, "Wi", Csb, "Csb")])
        emit_MM_accum(Mb, "Mb", [(A_T, "A_T", Csb, "Csb"), (B_Tn, "B_Tn", Ssb, "Ssb")])
        tap("Mb", Mb[:, :], also=P.last_refs())

        for t in range(NT):
            k3 = t % 3
            ptile = (pB, pC, pA)[k3]
            pslots = (PBf, PCf, [sl["pA0"], sl["pA1"]])[k3]
            for js in range(2):
                for dc in range(ND):
                    first_op = (js == 0 and dc == 0)
                    op("pe", lambda e, dc=dc, js=js, t=t, ptile=ptile: e.matmul(
                        ptile[:, js * 512:(js + 1) * 512],
                        lhsT=hfm[:, dc * S + t * 128: dc * S + (t + 1) * 128],
                        rhs=Mb[:, dc * D + js * 512: dc * D + (js + 1) * 512],
                        start=(dc == 0), stop=False, skip_group_check=True),
                        reads=[sl["hfm"], sl["Mb"]],
                        writes=pslots if first_op else [],
                        writes_cont=pslots if not first_op else [])
                op("pe", lambda e, js=js, ptile=ptile: e.matmul(
                    ptile[:, js * 512:(js + 1) * 512], lhsT=ones1b[:, :],
                    rhs=biasrow[:, js * 512:(js + 1) * 512],
                    start=False, stop=True, skip_group_check=True),
                    writes_cont=pslots, also=[r_biasrow, r_ones1])
            ln_tile = lnd_s if t % 2 == 0 else ln_s
            ln_slot = sl["lnd"] if t % 2 == 0 else sl["ln"]
            r8 = emit_postblock(t, ptile, pslots, ln_tile, ln_slot, None)
            io_t = io_stage if t % 2 == 0 else io2
            io_slot = sl["io"] if t % 2 == 0 else sl["io2"]
            op("dve", lambda e, t=t, ln_tile=ln_tile, io_t=io_t: e.tensor_tensor(
                io_t[:, :], x_bf[:, t * D:(t + 1) * D], ln_tile[:, :], ALU.add),
               reads=[ln_slot], writes=[io_slot], also=[r8])
            op("spd", lambda e, t=t, io_t=io_t: e.dma_start(
                d_out[t * 128:(t + 1) * 128, :], io_t[:, :]),
               reads=[io_slot])

        return _finish(nc, P, sems, dbg_outs)


def _finish(nc, P, sems, dbg_outs):
    P.finalize()
    with nc.Block() as block:
        @block.tensor
        def _(eng):
            P.emit("pe", eng, sems)

        @block.scalar
        def _(eng):
            P.emit("act", eng, sems)

        @block.vector
        def _(eng):
            P.emit("dve", eng, sems)

        @block.gpsimd
        def _(eng):
            P.emit("gpc", eng, sems)
            P.emit("gpd", eng, sems)
            if P.ops["gpd"]:
                eng.wait_ge(sems["gpd"], P.val_after["gpd"][-1])

        @block.sync
        def _(eng):
            P.emit("spd", eng, sems)
            if P.ops["spd"]:
                eng.wait_ge(sems["spd"], P.val_after["spd"][-1])

    return nc


def kernel(L0_attn_x, L0_attn_Wq_amp, L0_attn_Wk_amp, L0_attn_Wq_phi,
           L0_attn_Wk_phi, L0_attn_Wv, L0_attn_Wo, L0_attn_score_log_scale,
           L0_ffn_w_freq_real, L0_ffn_w_freq_imag, L0_ffn_bias):
    cst = _host_consts()
    nc = build_program()
    shared = {
        "wqa": np.ascontiguousarray(L0_attn_Wq_amp, dtype=np.float32),
        "wka": np.ascontiguousarray(L0_attn_Wk_amp, dtype=np.float32),
        "wqp": np.ascontiguousarray(L0_attn_Wq_phi, dtype=np.float32),
        "wkp": np.ascontiguousarray(L0_attn_Wk_phi, dtype=np.float32),
        "wv": np.ascontiguousarray(L0_attn_Wv, dtype=np.float32),
        "wo": np.ascontiguousarray(L0_attn_Wo, dtype=np.float32),
        "sls": np.ascontiguousarray(L0_attn_score_log_scale, dtype=np.float32),
        "wfr": np.ascontiguousarray(L0_ffn_w_freq_real, dtype=np.float32),
        "wfi": np.ascontiguousarray(L0_ffn_w_freq_imag, dtype=np.float32),
        "bias": np.ascontiguousarray(L0_ffn_bias, dtype=np.float32),
        "Cm": cst["Cm"], "Sm": cst["Sm"], "identc": cst["ident"],
        "tric": cst["tri"], "e2c": cst["e2"], "e2tc": cst["e2t"],
        "e16c": cst["e16"], "ones1c": cst["ones1"],
    }
    x = np.ascontiguousarray(L0_attn_x, dtype=np.float32)
    in_maps = [dict(shared, x=np.ascontiguousarray(x[b])) for b in range(8)]
    res = run_bass_kernel_spmd(nc, in_maps, core_ids=list(range(8)))
    return np.stack([np.asarray(r["out"], dtype=np.float32) for r in res.results], axis=0)


# version 09
# speedup vs baseline: 1.3771x; 1.3051x over previous
import math
import numpy as np
import ml_dtypes
import concourse.bass as bass
import concourse.mybir as mybir
from concourse.bass_utils import run_bass_kernel_spmd
from contextlib import ExitStack

F32 = mybir.dt.float32
BF16 = mybir.dt.bfloat16
AF = mybir.ActivationFunctionType
ALU = mybir.AluOpType
AX = mybir.AxisListType

D = 1024
S = 2048
NH = 16
NT = 16
ND = 8
NG = 4
GH = 4
LN_EPS = 1e-5
RMS_EPS = 1e-6

_CONSTS = None


def _host_consts():
    global _CONSTS
    if _CONSTS is None:
        n = np.arange(D)
        nk = (np.outer(n, n) % D).astype(np.float64)
        theta = 2.0 * np.pi * nk / D
        Cm = (np.cos(theta) / 32.0).astype(ml_dtypes.bfloat16)
        Sm = (np.sin(theta) / 32.0).astype(ml_dtypes.bfloat16)
        ident = np.eye(128, dtype=ml_dtypes.bfloat16)
        tri = (np.arange(128)[None, :] >= np.arange(128)[:, None]).astype(ml_dtypes.bfloat16)
        e2 = np.zeros((128, 2), dtype=ml_dtypes.bfloat16)
        e2[:64, 0] = 1
        e2[64:, 1] = 1
        e2t = np.ascontiguousarray(e2.T)
        e16 = np.zeros((16, ND * 128), dtype=ml_dtypes.bfloat16)
        for c in range(ND):
            for p in range(128):
                e16[2 * c + p // 64, c * 128 + p] = 1
        ones1 = np.ones((1, 128), dtype=ml_dtypes.bfloat16)
        _CONSTS = dict(Cm=Cm, Sm=Sm, ident=ident, tri=tri, e2=e2, e2t=e2t,
                       e16=e16, ones1=ones1)
    return _CONSTS


DMA_KEYS = ("gpd", "spd")
ALL_KEYS = ("pe", "act", "dve", "gpc", "gpd", "spd")


class Ref:
    __slots__ = ("key", "idx")

    def __init__(self, key, idx):
        self.key = key
        self.idx = idx


class Prog:
    def __init__(self):
        self.ops = {k: [] for k in ALL_KEYS}
        self.needed = {k: set() for k in ALL_KEYS}

    def add(self, key, emit_fn, deps=(), war=()):
        kept = []
        for d in deps:
            if d is None:
                continue
            if d.key == key:
                assert d.idx < len(self.ops[key]), "same-engine forward dep"
            self.needed[d.key].add(d.idx)
            kept.append(d)
        for d in war:
            if d is None:
                continue
            if d.key == key:
                assert d.idx < len(self.ops[key]), "same-engine forward dep"
                continue
            self.needed[d.key].add(d.idx)
            kept.append(d)
        ref = Ref(key, len(self.ops[key]))
        self.ops[key].append((emit_fn, kept))
        return ref

    def finalize(self):
        self.val_after = {}
        for k in ALL_KEYS:
            vals, tot = [], 0
            for i in range(len(self.ops[k])):
                if k in DMA_KEYS:
                    tot += 16
                elif i in self.needed[k]:
                    tot += 1
                vals.append(tot)
            self.val_after[k] = vals

    def emit(self, key, eng, sems):
        last_wait = {k: 0 for k in ALL_KEYS}
        for i, (emit_fn, deps) in enumerate(self.ops[key]):
            for d in deps:
                v = self.val_after[d.key][d.idx]
                if v > last_wait[d.key]:
                    eng.wait_ge(sems[d.key], v)
                    last_wait[d.key] = v
            inst = emit_fn(eng)
            if key in DMA_KEYS:
                inst.then_inc(sems[key], 16)
            elif i in self.needed[key]:
                inst.then_inc(sems[key], 1)

    def last_refs(self):
        return [Ref(k, len(self.ops[k]) - 1) for k in ALL_KEYS if self.ops[k]]


class Slot:

    def __init__(self):
        self.writer = None
        self.readers = []

    def rdeps(self):
        return [self.writer] if self.writer else []

    def wdeps(self):
        return self.readers + ([self.writer] if self.writer else [])

    def read(self, ref):
        self.readers.append(ref)

    def wrote(self, ref):
        self.writer = ref
        self.readers = []

    def wrote_cont(self, ref):
        self.writer = ref


def build_program(taps=(), stop_after=None):
    nc = bass.Bass("TRN2", target_bir_lowering=False)
    from concourse.hw_specs import get_activation_tables
    _tables = list(get_activation_tables(nc.m.arch).keys())
    SET_NLEXP = _tables.index("natural_log_exp_and_others")
    SET_SILU = _tables.index("silu_and_others")

    d_x = nc.dram_tensor("x", [S, D], F32, kind="ExternalInput").ap()
    d_w = {w: nc.dram_tensor(w, [D, D], F32, kind="ExternalInput").ap()
           for w in ("wqa", "wka", "wqp", "wkp", "wv", "wo")}
    d_sls = nc.dram_tensor("sls", [NH], F32, kind="ExternalInput").ap()
    d_wfr = nc.dram_tensor("wfr", [D, D], F32, kind="ExternalInput").ap()
    d_wfi = nc.dram_tensor("wfi", [D, D], F32, kind="ExternalInput").ap()
    d_bias = nc.dram_tensor("bias", [D], F32, kind="ExternalInput").ap()
    d_C = nc.dram_tensor("Cm", [D, D], BF16, kind="ExternalInput").ap()
    d_S = nc.dram_tensor("Sm", [D, D], BF16, kind="ExternalInput").ap()
    d_ident = nc.dram_tensor("identc", [128, 128], BF16, kind="ExternalInput").ap()
    d_tri = nc.dram_tensor("tric", [128, 128], BF16, kind="ExternalInput").ap()
    d_e2 = nc.dram_tensor("e2c", [128, 2], BF16, kind="ExternalInput").ap()
    d_e2t = nc.dram_tensor("e2tc", [2, 128], BF16, kind="ExternalInput").ap()
    d_e16 = nc.dram_tensor("e16c", [16, ND * 128], BF16, kind="ExternalInput").ap()
    d_ones1 = nc.dram_tensor("ones1c", [1, 128], BF16, kind="ExternalInput").ap()
    d_out = nc.dram_tensor("out", [S, D], F32, kind="ExternalOutput").ap()
    d_cpsc = nc.dram_tensor("cpsc", [NH], F32, kind="Internal").ap()

    P = Prog()
    tap_done = set()

    with ExitStack() as ctx:
        sb = lambda n, s, d: ctx.enter_context(nc.sbuf_tensor(n, s, d))
        psa = lambda n, s: ctx.enter_context(nc.psum_tensor(n, s, F32))

        x_bf = sb("x_bf", [128, NT * D], BF16)
        hfm = sb("hfm", [128, ND * S], BF16)
        identb = sb("identb", [128, 128], BF16)
        trib = sb("trib", [128, 128], BF16)
        e2b = sb("e2b", [128, 2], BF16)
        e2tb = sb("e2tb", [2, 128], BF16)
        e16b = sb("e16b", [16, ND * 128], BF16)
        ones1b = sb("ones1b", [1, 128], BF16)
        slsrow = sb("slsrow", [1, NH], F32)
        crow8 = sb("crow8", [1, NH], F32)
        mnegrow = sb("mnegrow", [1, NH], BF16)
        mneg_bc = sb("mneg_bc", [128, NH], F32)
        cpair = sb("cpair", [2, ND], F32)
        ccol = sb("ccol", [128, 4], F32)
        stg = sb("stg", [65, 1024], BF16)
        io_stage = sb("io_stage", [128, D], F32)
        ln_s = sb("ln_s", [128, D], F32)
        sq_s = sb("sq_s", [128, D], BF16)
        htm_s = sb("htm", [128, 2 * D], BF16)
        stat8 = sb("stat8", [128, 8], F32)
        biasrow = sb("biasrow", [1, D], BF16)

        pA = psa("pA", [128, 2048])
        pB = psa("pB", [128, 1024])
        pC = psa("pC", [128, 1024])

        sems = {k: ctx.enter_context(nc.semaphore("sem_" + k)) for k in ALL_KEYS}

        sl = dict(pA0=Slot(), pA1=Slot(), pA2=Slot(), pA3=Slot(),
                  pB0=Slot(), pB1=Slot(), pC0=Slot(), pC1=Slot(),
                  x_bf=Slot(), hfm=Slot(), stg=Slot(),
                  io=Slot(), ln=Slot(), htm=Slot(), stat=Slot())
        PAall = [sl["pA0"], sl["pA1"], sl["pA2"], sl["pA3"]]
        PBf = [sl["pB0"], sl["pB1"]]
        PCf = [sl["pC0"], sl["pC1"]]

        def op(key, fn, reads=(), writes=(), writes_cont=(), also=()):
            deps = list(also)
            war = []
            for s in reads:
                deps += s.rdeps()
            for s in writes:
                war += s.wdeps()
            ref = P.add(key, fn, deps, war=war)
            for s in reads:
                s.read(ref)
            for s in writes:
                s.wrote(ref)
            for s in writes_cont:
                s.wrote_cont(ref)
            return ref

        def act_load(set_id):
            return P.add("act", lambda e, sid=set_id: e.add_instruction(
                mybir.InstLoadActFuncSet(
                    name=e.bass.get_next_instruction_name(),
                    act_func_set_id=sid, ins=[], outs=[])))

        dbg_outs = {}

        def tap(name, ap, also=()):
            if name not in taps or name in tap_done:
                return
            tap_done.add(name)
            shp = [dd[1] for dd in ap.ap]
            dt = ap.tensor.dtype if hasattr(ap.tensor, "dtype") else F32
            d_dbg = nc.dram_tensor("dbg_" + name, shp, dt, kind="ExternalOutput").ap()
            dbg_outs[name] = d_dbg
            P.add("spd", lambda e, _a=ap, _d=d_dbg: e.dma_start(_d, _a), list(also))

        r_ident = P.add("spd", lambda e: e.dma_start(identb[:, :], d_ident))
        r_tri = P.add("spd", lambda e: e.dma_start(trib[:, :], d_tri))
        r_e2 = P.add("spd", lambda e: e.dma_start(e2b[:, :], d_e2))
        r_e2t = P.add("spd", lambda e: e.dma_start(e2tb[:, :], d_e2t))
        r_e16 = P.add("spd", lambda e: e.dma_start(e16b[:, :], d_e16))
        r_ones1 = P.add("spd", lambda e: e.dma_start(ones1b[:, :], d_ones1))
        r_sls = P.add("spd", lambda e: e.dma_start(slsrow[:, :], d_sls.rearrange("(a b) -> a b", a=1)))
        r_biasrow = P.add("gpd", lambda e: e.dma_start(biasrow[:, :], d_bias.rearrange("(a b) -> a b", a=1)))

        P.add("dve", lambda e: e.memset(ccol[:, 0:1], 1.0))
        P.add("dve", lambda e: e.memset(ccol[:, 1:2], RMS_EPS))
        P.add("dve", lambda e: e.memset(ccol[:, 2:3], LN_EPS))
        r_ccol = P.add("dve", lambda e: e.memset(ccol[:, 3:4], -math.pi / 2))

        act_load(SET_NLEXP)
        r_crow = P.add("act", lambda e: e.activation(crow8[:, :], slsrow[:, :], AF.Exp), [r_sls])
        r_crow = P.add("dve", lambda e: e.tensor_scalar_mul(crow8[:, :], crow8[:, :], 0.125), [r_crow])
        r_mneg = P.add("dve", lambda e: e.tensor_scalar_mul(mnegrow[:, :], crow8[:, :], -64.0), [r_crow])
        r_mm = op("pe", lambda e: e.matmul(pB[:, 0:NH], lhsT=ones1b[:, :], rhs=mnegrow[:, :],
                                           start=True, stop=True),
                  writes=[sl["pB0"]], also=[r_mneg, r_ones1])
        r_mnegbc = op("dve", lambda e: e.tensor_copy(mneg_bc[:, :], pB[:, 0:NH]),
                      reads=[sl["pB0"]])
        r_cpd = P.add("gpd", lambda e: e.dma_start(d_cpsc.rearrange("(a b) -> a b", a=1), crow8[:, :]), [r_crow])
        r_cpair = P.add("gpd", lambda e: e.dma_start(
            cpair[:, :], d_cpsc.rearrange("(cc p) -> p cc", p=2),
            allow_slow_non_contiguous=True), [r_cpd])

        r_xload = [P.add("gpd", lambda e, t=t: e.dma_start(
            x_bf[:, t * D:(t + 1) * D], d_x[t * 128:(t + 1) * 128, :])) for t in range(NT)]

        def emit_ln_stats_apply(src_ap, dst_ap, src_slot, dst_slot, par, extra=()):
            sb_ = 4 * par
            st = lambda i: stat8[:, sb_ + i:sb_ + i + 1]
            r1 = op("dve", lambda e: e.reduce_sum(st(0), src_ap, axis=AX.X),
                    reads=[src_slot], writes=[sl["stat"]], also=list(extra))
            r2 = op("dve", lambda e: e.tensor_scalar_mul(st(0), st(0), 1.0 / D),
                    writes=[sl["stat"]], also=[r1])
            r3 = op("act", lambda e: e.activation(
                sq_s[:, :], src_ap, AF.Square, accum_out=st(1)),
                reads=[src_slot], writes=[sl["stat"]], also=list(extra))
            r4 = op("dve", lambda e: e.tensor_tensor(st(3), st(0), st(0), ALU.mult),
                    writes=[sl["stat"]], also=[r2])
            r5 = op("dve", lambda e: e.scalar_tensor_tensor(
                st(2), st(1), 1.0 / D, st(3), ALU.mult, ALU.subtract),
                writes=[sl["stat"]], also=[r3, r4])
            r6 = op("act", lambda e: e.activation(st(2), st(2), AF.Ln, bias=ccol[:, 2:3]),
                    writes=[sl["stat"]], also=[r5, r_ccol])
            r7 = op("act", lambda e: e.activation(st(2), st(2), AF.Exp, scale=-0.5),
                    writes=[sl["stat"]], also=[r6])
            r8 = op("dve", lambda e: e.tensor_scalar(
                dst_ap, src_ap, st(0), st(2), ALU.subtract, ALU.mult),
                reads=[src_slot, sl["stat"]], writes=[dst_slot], also=[r7])
            return r8

        def emit_transpose(src_tile_fn, dst_tensor, t, src_slot, dst_slot,
                           ptiles=None, extra=()):
            if ptiles is None:
                ptiles = [(pB, sl["pB0"], 0), (pC, sl["pC0"], 0)]
            for half in range(2):
                ptile, pslot, poff = ptiles[half]
                for q in range(4):
                    dc = half * 4 + q
                    op("pe", lambda e, dc=dc, q=q, ptile=ptile, poff=poff: e.matmul(
                        ptile[:, poff + q * 128: poff + (q + 1) * 128],
                        lhsT=src_tile_fn(dc), rhs=identb[:, :], start=True, stop=True),
                        reads=[src_slot],
                        writes=[pslot] if q == 0 else [],
                        writes_cont=[pslot] if q > 0 else [],
                        also=list(extra) + ([r_ident] if q == 0 else []))
                dst = bass.AP(dst_tensor[:, :].tensor, (half * 4) * S + t * 128,
                              [dst_tensor[:, :].ap[0], [S, 4], [1, 128]])
                op("act", lambda e, dst=dst, ptile=ptile, poff=poff: e.activation(
                    dst, ptile[:, poff:poff + 512], AF.Copy),
                    reads=[pslot], writes=[dst_slot])

        for t in range(NT):
            par = t % 2
            hview = htm_s[:, par * D:(par + 1) * D]
            emit_ln_stats_apply(x_bf[:, t * D:(t + 1) * D], hview,
                                sl["x_bf"], sl["htm"], par, extra=[r_xload[t]])
            emit_transpose(lambda dc, par=par: htm_s[:, par * D + dc * 128: par * D + (dc + 1) * 128],
                           hfm, t, sl["htm"], sl["hfm"])

        tap("h0fm", hfm[:, :], also=P.last_refs())
        if stop_after == "phase0":
            return _finish(nc, P, sems, dbg_outs)

        attn_ctx = ctx.enter_context(ExitStack())
        sba = lambda n, s, d: attn_ctx.enter_context(nc.sbuf_tensor(n, s, d))
        o_fm = sba("o_fm", [128, ND * S], BF16)
        Lb = sba("Lb", [NH, S], BF16)
        rlb = sba("rlb", [NH, S], BF16)
        Qcs = sba("Qcs", [128, GH * S], BF16)
        Kcs = sba("Kcs", [128, GH * S], BF16)
        V_aug = sba("V_aug", [128, NT * GH * 65], BF16)
        WTa = sba("WTa", [128, ND * 256], BF16)
        WTb = sba("WTb", [128, ND * 256], BF16)
        WTv = sba("WTv", [128, ND * 256], BF16)
        Wstage = sba("Wstage", [128, 2 * D], BF16)
        qx = sba("qx", [128, S], BF16)
        Pt = sba("Pt", [128, 3072], BF16)
        t14 = sba("t14", [128, S], BF16)
        qa_s = sba("qa_s", [128, 512], BF16)
        q2_s = sba("q2_s", [128, 512], BF16)
        t_s = sba("t_s", [128, 512], BF16)
        tn_s = sba("tn_s", [128, 512], BF16)
        sinv_s = sba("sinv_s", [128, 512], BF16)
        cosn_s = sba("cosn_s", [128, 512], BF16)
        rstd2b_s = sba("rstd2b_s", [2, 512], BF16)
        ln2_s = sba("ln2_s", [128, D], F32)

        for nm in ("Qcs", "Kcs", "V_aug", "Wstage", "qx", "Pt0", "Pt1", "Pt2",
                   "qa_s", "q2_s", "t_s", "tn_s",
                   "sinv_s", "cosn_s", "t14", "rstd2",
                   "WTa", "WTb", "WTv", "o_fm", "Lb", "rlb", "ln2"):
            sl[nm] = Slot()

        va = V_aug[:, :]
        va_ones = bass.AP(va.tensor, 64, [va.ap[0], [260, NT], [65, GH], [0, 1]])
        op("dve", lambda e: e.memset(va_ones, 1.0), writes=[sl["V_aug"]])

        def emit_wslice(g, w, WTdst, dname):
            rl_ = []
            for fc in range(2):
                rl_.append(op("gpd", lambda e, w=w, fc=fc, g=g: e.dma_start(
                    Wstage[:, fc * D:(fc + 1) * D],
                    d_w[w][g * 256 + fc * 128: g * 256 + (fc + 1) * 128, :]),
                    writes=[sl["Wstage"]]))
            for grp4 in range(4):
                ptile, pslot = (pB, sl["pB0"]) if grp4 % 2 == 0 else (pC, sl["pC0"])
                for pos in range(4):
                    idx = grp4 * 4 + pos
                    dc, fc = idx // 2, idx % 2
                    op("pe", lambda e, dc=dc, fc=fc, pos=pos, ptile=ptile: e.matmul(
                        ptile[:, pos * 128:(pos + 1) * 128],
                        lhsT=Wstage[:, fc * D + dc * 128: fc * D + (dc + 1) * 128],
                        rhs=identb[:, :], start=True, stop=True),
                        reads=[sl["Wstage"]],
                        writes=[pslot] if pos == 0 else [],
                        writes_cont=[pslot] if pos > 0 else [],
                        also=rl_ if idx == 0 else [])
                op("dve", lambda e, grp4=grp4, ptile=ptile, WTdst=WTdst: e.tensor_copy(
                    WTdst[:, grp4 * 512:(grp4 + 1) * 512], ptile[:, 0:512]),
                    reads=[pslot], writes=[sl[dname]])

        def emit_proj_pair(g, WTamp, aname, WTphi, pname, dst, dst_name, scale_c):
            for c in range(2):
                act_load(SET_NLEXP)
                for js in range(4):
                    tile, X0, X1 = (pB, sl["pB0"], sl["pB1"]) if js % 2 == 0 else (pC, sl["pC0"], sl["pC1"])
                    for dc in range(ND):
                        op("pe", lambda e, dc=dc, c=c, js=js, tile=tile, WTamp=WTamp: e.matmul(
                            tile[:, 0:512],
                            lhsT=WTamp[:, dc * 256 + c * 128: dc * 256 + (c + 1) * 128],
                            rhs=hfm[:, dc * S + js * 512: dc * S + (js + 1) * 512],
                            start=(dc == 0), stop=(dc == ND - 1)),
                            reads=[sl[aname], sl["hfm"]],
                            writes=[X0] if dc == 0 else [],
                            writes_cont=[X0] if dc > 0 else [])
                    r_sp = op("act", lambda e, tile=tile: e.activation(qa_s[:, :], tile[:, 0:512], AF.Exp),
                       reads=[X0], writes=[sl["qa_s"]])
                    op("act", lambda e: e.activation(qa_s[:, :], qa_s[:, :], AF.Ln, bias=ccol[:, 0:1]),
                       writes=[sl["qa_s"]], also=[r_ccol, r_sp])
                    op("dve", lambda e: e.tensor_mul(q2_s[:, :], qa_s[:, :], qa_s[:, :]),
                       reads=[sl["qa_s"]], writes=[sl["q2_s"]])
                    op("pe", lambda e, tile=tile: e.matmul(
                        tile[0:2, 512:1024], lhsT=e2b[:, :], rhs=q2_s[:, :],
                        start=True, stop=True),
                       reads=[sl["q2_s"]], writes=[X1], also=[r_e2])
                    r_rln = op("act", lambda e, tile=tile: e.activation(
                        rstd2b_s[:, :], tile[0:2, 512:1024], AF.Ln,
                        scale=1.0 / 64.0, bias=ccol[0:2, 1:2]),
                       reads=[X1], writes=[sl["rstd2"]])
                    op("act", lambda e: e.activation(rstd2b_s[:, :], rstd2b_s[:, :], AF.Exp, scale=-0.5),
                       writes=[sl["rstd2"]], also=[r_rln])
                    if scale_c:
                        op("dve", lambda e, g=g, c=c: e.tensor_scalar_mul(
                            rstd2b_s[:, :], rstd2b_s[:, :], cpair[:, g * 2 + c: g * 2 + c + 1]),
                           writes=[sl["rstd2"]], also=[r_cpair])
                    op("pe", lambda e, tile=tile: e.matmul(
                        tile[:, 512:1024], lhsT=e2tb[:, :], rhs=rstd2b_s[:, :],
                        start=True, stop=True),
                       reads=[sl["rstd2"]], writes=[X1], also=[r_e2t])
                    op("dve", lambda e, js=js, tile=tile: e.tensor_tensor(
                        t14[:, js * 512:(js + 1) * 512], qa_s[:, :], tile[:, 512:1024], ALU.mult),
                       reads=[sl["qa_s"], X1], writes=[sl["t14"]])
                act_load(SET_SILU)
                for js in range(4):
                    tile, X0, X1 = (pB, sl["pB0"], sl["pB1"]) if js % 2 == 0 else (pC, sl["pC0"], sl["pC1"])
                    for dc in range(ND):
                        op("pe", lambda e, dc=dc, c=c, js=js, tile=tile, WTphi=WTphi: e.matmul(
                            tile[:, 0:512],
                            lhsT=WTphi[:, dc * 256 + c * 128: dc * 256 + (c + 1) * 128],
                            rhs=hfm[:, dc * S + js * 512: dc * S + (js + 1) * 512],
                            start=(dc == 0), stop=(dc == ND - 1)),
                            reads=[sl[pname], sl["hfm"]],
                            writes=[X0] if dc == 0 else [],
                            writes_cont=[X0] if dc > 0 else [])
                    op("act", lambda e, tile=tile: e.activation(t_s[:, :], tile[:, 0:512], AF.Tanh),
                       reads=[X0], writes=[sl["t_s"]])
                    op("dve", lambda e: e.tensor_scalar_mul(tn_s[:, :], t_s[:, :], -1.0),
                       reads=[sl["t_s"]], writes=[sl["tn_s"]])
                    op("dve", lambda e: e.tensor_max(tn_s[:, :], t_s[:, :], tn_s[:, :]),
                       reads=[sl["t_s"]], writes=[sl["tn_s"]])
                    op("act", lambda e: e.activation(sinv_s[:, :], t_s[:, :], AF.Sin, scale=math.pi),
                       reads=[sl["t_s"]], writes=[sl["sinv_s"]])
                    op("act", lambda e: e.activation(cosn_s[:, :], tn_s[:, :], AF.Sin,
                                                     scale=math.pi, bias=ccol[:, 3:4]),
                       reads=[sl["tn_s"]], writes=[sl["cosn_s"]])
                    lo0 = (2 * c) * S + js * 512
                    lo1 = (2 * c + 1) * S + js * 512
                    t1lo = t14[0:64, js * 512:(js + 1) * 512]
                    t1hi = t14[64:128, js * 512:(js + 1) * 512]
                    op("dve", lambda e, lo0=lo0, t1lo=t1lo: e.scalar_tensor_tensor(
                        dst[0:64, lo0:lo0 + 512], cosn_s[0:64, :], -1.0, t1lo,
                        ALU.mult, ALU.mult),
                       reads=[sl["cosn_s"], sl["t14"]], writes=[sl[dst_name]])
                    op("dve", lambda e, js=js, t1hi=t1hi: e.scalar_tensor_tensor(
                        qx[64:128, js * 512:(js + 1) * 512], cosn_s[64:128, :], -1.0,
                        t1hi, ALU.mult, ALU.mult),
                       reads=[sl["cosn_s"], sl["t14"]], writes=[sl["qx"]])
                    op("dve", lambda e, js=js, t1lo=t1lo: e.tensor_mul(
                        qx[0:64, js * 512:(js + 1) * 512], t1lo, sinv_s[0:64, :]),
                       reads=[sl["sinv_s"], sl["t14"]], writes=[sl["qx"]])
                    op("dve", lambda e, lo1=lo1, t1hi=t1hi: e.tensor_mul(
                        dst[64:128, lo1:lo1 + 512], t1hi, sinv_s[64:128, :]),
                       reads=[sl["sinv_s"], sl["t14"]], writes=[sl[dst_name]])
                op("gpd", lambda e, c=c: e.dma_start(dst[0:64, (2 * c + 1) * S:(2 * c + 2) * S],
                                                     qx[64:128, :]),
                   reads=[sl["qx"]], writes=[sl[dst_name]])
                op("gpd", lambda e, c=c: e.dma_start(dst[64:128, (2 * c) * S:(2 * c) * S + S],
                                                     qx[0:64, :]),
                   reads=[sl["qx"]], writes=[sl[dst_name]])

        def emit_vproj(g):
            for t in range(NT):
                tile, pslot = (pB, sl["pB0"]) if t % 2 == 0 else (pC, sl["pC0"])
                for dc in range(ND):
                    op("pe", lambda e, dc=dc, t=t, tile=tile: e.matmul(
                        tile[:, 0:256], lhsT=hfm[:, dc * S + t * 128: dc * S + (t + 1) * 128],
                        rhs=WTv[:, dc * 256:(dc + 1) * 256],
                        start=(dc == 0), stop=(dc == ND - 1)),
                        reads=[sl["hfm"], sl["WTv"]],
                        writes=[pslot] if dc == 0 else [],
                        writes_cont=[pslot] if dc > 0 else [])
                dstp = bass.AP(va.tensor, t * 260, [va.ap[0], [65, GH], [1, 64]])
                op("dve", lambda e, dstp=dstp, tile=tile: e.tensor_copy(
                    dstp, tile[:, 0:256].rearrange("p (a b) -> p a b", a=GH)),
                   reads=[pslot], writes=[sl["V_aug"]])

        def emit_attention(g):
            act_load(SET_NLEXP)
            PAo = [sl["pA0"], sl["pA1"]]
            stiles = [(pB, sl["pB0"], sl["pB1"], 0),
                      (pC, sl["pC0"], sl["pC1"], 0),
                      (pA, sl["pA2"], sl["pA3"], 1024)]
            ptslots = (sl["Pt0"], sl["Pt1"], sl["Pt2"])
            pairs = []
            for hl in range(GH):
                for qh in range(2):
                    end = 1024 * (qh + 1)
                    for ki in range(NT):
                        if 128 * ki >= end:
                            break
                        qs0 = max(128 * ki, 1024 * qh)
                        blocks = [(qs0, min(end, (qs0 // 512 + 1) * 512) - qs0)]
                        if qs0 + blocks[0][1] < end:
                            blocks.append((qs0 + blocks[0][1], 512))
                        last = (qh == 0 and ki == 7) or (qh == 1 and ki == NT - 1)
                        pairs.append((hl, qh, ki, blocks, last))
            exp_ref = {}

            def emit_st_exp(p):
                hl, qh, ki, blocks, last = pairs[p]
                h = g * GH + hl
                si = p % 3
                tile, X0, X1, tbase = stiles[si]
                ptbase = 1024 * si
                w0 = blocks[0][1]
                lo = 512 - w0
                hi = 512 + (512 if len(blocks) > 1 else 0)
                for bi, (qs, w) in enumerate(blocks):
                    poff = 512 - w0 if bi == 0 else 512
                    xslot = X0 if bi == 0 else X1
                    op("pe", lambda e, hl=hl, ki=ki, qs=qs, w=w, poff=poff,
                       tile=tile, tbase=tbase: e.matmul(
                        tile[:, tbase + poff: tbase + poff + w],
                        lhsT=Kcs[:, hl * S + ki * 128: hl * S + (ki + 1) * 128],
                        rhs=Qcs[:, hl * S + qs: hl * S + qs + w], start=True, stop=True),
                        reads=[sl["Kcs"], sl["Qcs"]], writes=[xslot])
                r_exp = op("act", lambda e, lo=lo, hi=hi, h=h, ptbase=ptbase,
                           tile=tile, tbase=tbase: e.activation(
                    Pt[:, ptbase + lo: ptbase + hi], tile[:, tbase + lo: tbase + hi],
                    AF.Exp, bias=mneg_bc[:, h:h + 1]),
                    reads=[X0] + ([X1] if len(blocks) > 1 else []),
                    writes=[ptslots[si]], also=[r_mnegbc])
                r_last = r_exp
                if blocks[0][0] == 128 * ki:
                    r_last = op("dve", lambda e, lo=lo, ptbase=ptbase: e.tensor_mul(
                        Pt[:, ptbase + lo: ptbase + lo + 128],
                        Pt[:, ptbase + lo: ptbase + lo + 128], trib[:, :]),
                        writes=[ptslots[si]], also=[r_tri])
                exp_ref[p] = r_last

            def emit_ov(p):
                hl, qh, ki, blocks, last = pairs[p]
                h = g * GH + hl
                si = p % 3
                ptbase = 1024 * si
                w0 = blocks[0][1]
                for bi, (qs, w) in enumerate(blocks):
                    poff = 512 - w0 if bi == 0 else 512
                    gb = qs // 512
                    qs_l = qs - 1024 * qh
                    first_write = (ki == 0 and qs_l == 0)
                    op("pe", lambda e, ki=ki, hl=hl, qs_l=qs_l, w=w, poff=poff,
                       ptbase=ptbase, gb=gb: e.matmul(
                        pA[0:65, qs_l:qs_l + w],
                        lhsT=V_aug[:, ki * 260 + hl * 65: ki * 260 + (hl + 1) * 65],
                        rhs=Pt[:, ptbase + poff: ptbase + poff + w],
                        start=(ki == 0), stop=(ki == 4 * gb + 3), skip_group_check=True),
                        reads=[ptslots[si], sl["V_aug"]],
                        writes=PAo if first_write else [],
                        writes_cont=PAo if not first_write else [],
                        also=[exp_ref[p]])
                if last:
                    r_ev = op("dve", lambda e: e.tensor_copy(stg[:, :], pA[0:65, 0:1024]),
                              reads=PAo, writes=[sl["stg"]])
                    c, half = h // 2, h % 2
                    op("gpd", lambda e, c=c, half=half, qh=qh: e.dma_start(
                        o_fm[64 * half:64 * half + 64,
                             c * S + 1024 * qh: c * S + 1024 * qh + 1024], stg[0:64, :]),
                       reads=[sl["stg"]], writes=[sl["o_fm"]])
                    op("gpd", lambda e, h=h, qh=qh: e.dma_start(
                        Lb[h:h + 1, 1024 * qh: 1024 * qh + 1024], stg[64:65, :]),
                       reads=[sl["stg"]], writes=[sl["Lb"]])

            SKEW = 2
            for p in range(len(pairs) + SKEW):
                if p < len(pairs):
                    emit_st_exp(p)
                if p >= SKEW:
                    emit_ov(p - SKEW)

        for g in range(NG):
            emit_wslice(g, "wqa", WTa, "WTa")
            emit_wslice(g, "wqp", WTb, "WTb")
            if g == 0:
                emit_wslice(g, "wv", WTv, "WTv")
            emit_proj_pair(g, WTa, "WTa", WTb, "WTb", Qcs, "Qcs", True)
            emit_wslice(g, "wka", WTa, "WTa")
            emit_wslice(g, "wkp", WTb, "WTb")
            emit_proj_pair(g, WTa, "WTa", WTb, "WTb", Kcs, "Kcs", False)
            emit_vproj(g)
            if g < NG - 1:
                emit_wslice(g + 1, "wv", WTv, "WTv")
            if g == 0:
                tap("Qcs0", Qcs[:, :], also=P.last_refs())
                tap("Kcs0", Kcs[:, :], also=P.last_refs())
                tap("Vaug0", V_aug[:, :], also=P.last_refs())
            emit_attention(g)
            if g == 0:
                tap("stg0", stg[:, :], also=P.last_refs())
            if stop_after == "group%d" % g:
                tap("o_fm", o_fm[:, :], also=P.last_refs())
                tap("Lb", Lb[:, :], also=P.last_refs())
                return _finish(nc, P, sems, dbg_outs)

        tap("o_fm", o_fm[:, :], also=P.last_refs())
        tap("Lb", Lb[:, :], also=P.last_refs())

        def _emit_recip(e):
            with nc.allow_low_precision(reason="1/l stored bf16; same scale as bf16 matmul rounding"):
                return e.reciprocal(rlb[:, :], Lb[:, :])
        r_rec = op("dve", _emit_recip, reads=[sl["Lb"]], writes=[sl["rlb"]])
        sl["rlb"].readers.append(r_rec)
        for c in range(ND):
            for js in range(4):
                tile, pslot = (pB, sl["pB0"]) if (c * 4 + js) % 2 == 0 else (pC, sl["pC0"])
                op("pe", lambda e, c=c, js=js, tile=tile: e.matmul(
                    tile[:, 0:512], lhsT=e16b[:, c * 128:(c + 1) * 128],
                    rhs=rlb[:, js * 512:(js + 1) * 512], start=True, stop=True),
                   reads=[sl["rlb"]], writes=[pslot], also=[r_e16])
                op("dve", lambda e, c=c, js=js, tile=tile: e.tensor_tensor(
                    o_fm[:, c * S + js * 512: c * S + (js + 1) * 512],
                    o_fm[:, c * S + js * 512: c * S + (js + 1) * 512],
                    tile[:, 0:512], ALU.mult),
                   reads=[pslot], writes=[sl["o_fm"]])

        tap("o_norm", o_fm[:, :], also=P.last_refs())

        WoT = Qcs
        sl["WoT"] = sl["Qcs"]
        for fcpair in range(4):
            rl_ = []
            for k in range(2):
                foc = fcpair * 2 + k
                rl_.append(op("gpd", lambda e, foc=foc, k=k: e.dma_start(
                    Wstage[:, k * D:(k + 1) * D], d_w["wo"][foc * 128:(foc + 1) * 128, :]),
                    writes=[sl["Wstage"]]))
            for grp4 in range(4):
                ptile, pslot = (pB, sl["pB0"]) if grp4 % 2 == 0 else (pC, sl["pC0"])
                evs = []
                for pos in range(4):
                    idx = grp4 * 4 + pos
                    k, fc = idx // 8, idx % 8
                    op("pe", lambda e, k=k, fc=fc, pos=pos, ptile=ptile: e.matmul(
                        ptile[:, pos * 128:(pos + 1) * 128],
                        lhsT=Wstage[:, k * D + fc * 128: k * D + (fc + 1) * 128],
                        rhs=identb[:, :], start=True, stop=True),
                        reads=[sl["Wstage"]],
                        writes=[pslot] if pos == 0 else [],
                        writes_cont=[pslot] if pos > 0 else [],
                        also=rl_ if idx == 0 else [])
                    evs.append((k, fc, pos))
                for (k, fc, pos) in evs:
                    foc = fcpair * 2 + k
                    op("dve", lambda e, fc=fc, foc=foc, pos=pos, ptile=ptile: e.tensor_copy(
                        WoT[:, fc * D + foc * 128: fc * D + (foc + 1) * 128],
                        ptile[:, pos * 128:(pos + 1) * 128]),
                        reads=[pslot], writes=[sl["WoT"]])

        act_load(SET_NLEXP)

        def emit_postblock(t, ptile, pslots, ln_tile, ln_slot, out_dst, extra_dep=()):
            par = t % 2
            sb_ = 4 * par
            st = lambda i: stat8[:, sb_ + i:sb_ + i + 1]
            r_ev = op("act", lambda e, ptile=ptile, ln_tile=ln_tile: e.activation(
                ln_tile[:, :], ptile[:, 0:1024], AF.Copy, accum_out=st(0)),
                reads=pslots, writes=[ln_slot, sl["stat"]], also=list(extra_dep))
            r2 = op("dve", lambda e: e.tensor_scalar_mul(st(0), st(0), 1.0 / D),
                    writes=[sl["stat"]], also=[r_ev])
            r3 = op("act", lambda e, ln_tile=ln_tile: e.activation(
                sq_s[:, :], ln_tile[:, :], AF.Square, accum_out=st(1)),
                reads=[ln_slot], writes=[sl["stat"]])
            r4 = op("dve", lambda e: e.tensor_tensor(st(3), st(0), st(0), ALU.mult),
                    writes=[sl["stat"]], also=[r2])
            r5 = op("dve", lambda e: e.scalar_tensor_tensor(
                st(2), st(1), 1.0 / D, st(3), ALU.mult, ALU.subtract),
                writes=[sl["stat"]], also=[r3, r4])
            r6 = op("act", lambda e: e.activation(st(2), st(2), AF.Ln, bias=ccol[:, 2:3]),
                    writes=[sl["stat"]], also=[r5])
            r7 = op("act", lambda e: e.activation(st(2), st(2), AF.Exp, scale=-0.5),
                    writes=[sl["stat"]], also=[r6])
            r8 = op("dve", lambda e, ln_tile=ln_tile: e.tensor_scalar(
                ln_tile[:, :], ln_tile[:, :], st(0), st(2), ALU.subtract, ALU.mult),
                reads=[sl["stat"]], writes=[ln_slot], also=[r7])
            return r8

        for t in range(NT):
            k3 = t % 2
            ptile = pB if k3 == 0 else pC
            pslots = PBf if k3 == 0 else PCf
            for js in range(2):
                for c in range(ND):
                    first_op = (js == 0 and c == 0)
                    op("pe", lambda e, c=c, js=js, t=t, ptile=ptile: e.matmul(
                        ptile[:, js * 512:(js + 1) * 512],
                        lhsT=o_fm[:, c * S + t * 128: c * S + (t + 1) * 128],
                        rhs=WoT[:, c * D + js * 512: c * D + (js + 1) * 512],
                        start=(c == 0), stop=(c == ND - 1)),
                        reads=[sl["o_fm"], sl["WoT"]],
                        writes=pslots if first_op else [],
                        writes_cont=pslots if not first_op else [])
            ln_tile = ln_s if t % 2 == 0 else ln2_s
            ln_slot = sl["ln"] if t % 2 == 0 else sl["ln2"]
            r8 = emit_postblock(t, ptile, pslots, ln_tile, ln_slot, None)
            op("dve", lambda e, t=t, ln_tile=ln_tile: e.tensor_tensor(
                x_bf[:, t * D:(t + 1) * D], x_bf[:, t * D:(t + 1) * D], ln_tile[:, :], ALU.add),
               reads=[ln_slot], writes=[sl["x_bf"]], also=[r8])
            par = t % 2
            hview = htm_s[:, par * D:(par + 1) * D]
            emit_ln_stats_apply(x_bf[:, t * D:(t + 1) * D], hview, sl["x_bf"], sl["htm"], par)
            emit_transpose(lambda dc, par=par: htm_s[:, par * D + dc * 128: par * D + (dc + 1) * 128],
                           hfm, t, sl["htm"], sl["hfm"],
                           ptiles=[(pA, sl["pA0"], 0), (pA, sl["pA1"], 512)])

        tap("x1", x_bf[:, :], also=P.last_refs())
        tap("h1fm", hfm[:, :], also=P.last_refs())
        if stop_after == "phaseC":
            return _finish(nc, P, sems, dbg_outs)

        barrier = P.last_refs()
        attn_ctx.close()
        sbm = lambda n, s, d: ctx.enter_context(nc.sbuf_tensor(n, s, d))
        Wr = sbm("Wr", [128, ND * D], BF16)
        Wi = sbm("Wi", [128, ND * D], BF16)
        Csb = sbm("Csb", [128, ND * D], BF16)
        Ssb = sbm("Ssb", [128, ND * D], BF16)
        A_T = sbm("A_T", [128, ND * D], BF16)
        B_Tn = sbm("B_Tn", [128, ND * D], BF16)
        Mb = sbm("Mb", [128, ND * D], BF16)
        lnd_s = sbm("lnd_s", [128, D], F32)
        io2 = sbm("io2", [128, D], F32)
        for nm in ("Wr", "Wi", "Csb", "Ssb", "A_T", "B_Tn", "Mb", "lnd", "io2"):
            sl[nm] = Slot()

        for dc in range(ND):
            r = P.add("gpd", lambda e, dc=dc: e.dma_start(
                Wr[:, dc * D:(dc + 1) * D], d_wfr[dc * 128:(dc + 1) * 128, :]), barrier)
            sl["Wr"].wrote_cont(r) if dc else sl["Wr"].wrote(r)
            r = P.add("gpd", lambda e, dc=dc: e.dma_start(
                Wi[:, dc * D:(dc + 1) * D], d_wfi[dc * 128:(dc + 1) * 128, :]), barrier)
            sl["Wi"].wrote_cont(r) if dc else sl["Wi"].wrote(r)
            r = P.add("spd", lambda e, dc=dc: e.dma_start(
                Csb[:, dc * D:(dc + 1) * D], d_C[dc * 128:(dc + 1) * 128, :]), barrier)
            sl["Csb"].wrote_cont(r) if dc else sl["Csb"].wrote(r)
            r = P.add("spd", lambda e, dc=dc: e.dma_start(
                Ssb[:, dc * D:(dc + 1) * D], d_S[dc * 128:(dc + 1) * 128, :]), barrier)
            sl["Ssb"].wrote_cont(r) if dc else sl["Ssb"].wrote(r)

        def emit_MM_accum(dst, dst_name, lhs_list):
            nterm = len(lhs_list)
            for outc in range(ND):
                tile = pB if outc % 2 == 0 else pC
                Xs = PBf if outc % 2 == 0 else PCf
                for js in range(2):
                    for ti, (lt, lname, rt, rname) in enumerate(lhs_list):
                        for dc in range(ND):
                            first = (ti == 0 and dc == 0)
                            last = (ti == nterm - 1 and dc == ND - 1)
                            is_first_op = (js == 0 and first)
                            op("pe", lambda e, lt=lt, rt=rt, dc=dc, outc=outc, js=js,
                               first=first, last=last, tile=tile: e.matmul(
                                   tile[:, js * 512:(js + 1) * 512],
                                   lhsT=lt[:, dc * D + outc * 128: dc * D + (outc + 1) * 128],
                                   rhs=rt[:, dc * D + js * 512: dc * D + (js + 1) * 512],
                                   start=first, stop=last, skip_group_check=True),
                               reads=[sl[lname], sl[rname]],
                               writes=Xs if is_first_op else [],
                               writes_cont=Xs if not is_first_op else [])
                op("dve", lambda e, outc=outc, tile=tile: e.tensor_copy(
                    dst[:, outc * D:(outc + 1) * D], tile[:, :]),
                    reads=Xs, writes=[sl[dst_name]])

        emit_MM_accum(A_T, "A_T", [(Wr, "Wr", Csb, "Csb"), (Wi, "Wi", Ssb, "Ssb")])
        op("dve", lambda e: e.tensor_scalar_mul(Wi[:, :], Wi[:, :], -1.0), writes=[sl["Wi"]])
        emit_MM_accum(B_Tn, "B_Tn", [(Wr, "Wr", Ssb, "Ssb"), (Wi, "Wi", Csb, "Csb")])
        emit_MM_accum(Mb, "Mb", [(A_T, "A_T", Csb, "Csb"), (B_Tn, "B_Tn", Ssb, "Ssb")])
        tap("Mb", Mb[:, :], also=P.last_refs())

        for t in range(NT):
            k3 = t % 3
            ptile = (pB, pC, pA)[k3]
            pslots = (PBf, PCf, [sl["pA0"], sl["pA1"]])[k3]
            for js in range(2):
                for dc in range(ND):
                    first_op = (js == 0 and dc == 0)
                    op("pe", lambda e, dc=dc, js=js, t=t, ptile=ptile: e.matmul(
                        ptile[:, js * 512:(js + 1) * 512],
                        lhsT=hfm[:, dc * S + t * 128: dc * S + (t + 1) * 128],
                        rhs=Mb[:, dc * D + js * 512: dc * D + (js + 1) * 512],
                        start=(dc == 0), stop=False, skip_group_check=True),
                        reads=[sl["hfm"], sl["Mb"]],
                        writes=pslots if first_op else [],
                        writes_cont=pslots if not first_op else [])
                op("pe", lambda e, js=js, ptile=ptile: e.matmul(
                    ptile[:, js * 512:(js + 1) * 512], lhsT=ones1b[:, :],
                    rhs=biasrow[:, js * 512:(js + 1) * 512],
                    start=False, stop=True, skip_group_check=True),
                    writes_cont=pslots, also=[r_biasrow, r_ones1])
            ln_tile = lnd_s if t % 2 == 0 else ln_s
            ln_slot = sl["lnd"] if t % 2 == 0 else sl["ln"]
            r8 = emit_postblock(t, ptile, pslots, ln_tile, ln_slot, None)
            io_t = io_stage if t % 2 == 0 else io2
            io_slot = sl["io"] if t % 2 == 0 else sl["io2"]
            op("dve", lambda e, t=t, ln_tile=ln_tile, io_t=io_t: e.tensor_tensor(
                io_t[:, :], x_bf[:, t * D:(t + 1) * D], ln_tile[:, :], ALU.add),
               reads=[ln_slot], writes=[io_slot], also=[r8])
            op("spd", lambda e, t=t, io_t=io_t: e.dma_start(
                d_out[t * 128:(t + 1) * 128, :], io_t[:, :]),
               reads=[io_slot])

        return _finish(nc, P, sems, dbg_outs)


def _finish(nc, P, sems, dbg_outs):
    P.finalize()
    with nc.Block() as block:
        @block.tensor
        def _(eng):
            P.emit("pe", eng, sems)

        @block.scalar
        def _(eng):
            P.emit("act", eng, sems)

        @block.vector
        def _(eng):
            P.emit("dve", eng, sems)

        @block.gpsimd
        def _(eng):
            P.emit("gpc", eng, sems)
            P.emit("gpd", eng, sems)
            if P.ops["gpd"]:
                eng.wait_ge(sems["gpd"], P.val_after["gpd"][-1])

        @block.sync
        def _(eng):
            P.emit("spd", eng, sems)
            if P.ops["spd"]:
                eng.wait_ge(sems["spd"], P.val_after["spd"][-1])

    return nc


def kernel(L0_attn_x, L0_attn_Wq_amp, L0_attn_Wk_amp, L0_attn_Wq_phi,
           L0_attn_Wk_phi, L0_attn_Wv, L0_attn_Wo, L0_attn_score_log_scale,
           L0_ffn_w_freq_real, L0_ffn_w_freq_imag, L0_ffn_bias):
    cst = _host_consts()
    nc = build_program()
    shared = {
        "wqa": np.ascontiguousarray(L0_attn_Wq_amp, dtype=np.float32),
        "wka": np.ascontiguousarray(L0_attn_Wk_amp, dtype=np.float32),
        "wqp": np.ascontiguousarray(L0_attn_Wq_phi, dtype=np.float32),
        "wkp": np.ascontiguousarray(L0_attn_Wk_phi, dtype=np.float32),
        "wv": np.ascontiguousarray(L0_attn_Wv, dtype=np.float32),
        "wo": np.ascontiguousarray(L0_attn_Wo, dtype=np.float32),
        "sls": np.ascontiguousarray(L0_attn_score_log_scale, dtype=np.float32),
        "wfr": np.ascontiguousarray(L0_ffn_w_freq_real, dtype=np.float32),
        "wfi": np.ascontiguousarray(L0_ffn_w_freq_imag, dtype=np.float32),
        "bias": np.ascontiguousarray(L0_ffn_bias, dtype=np.float32),
        "Cm": cst["Cm"], "Sm": cst["Sm"], "identc": cst["ident"],
        "tric": cst["tri"], "e2c": cst["e2"], "e2tc": cst["e2t"],
        "e16c": cst["e16"], "ones1c": cst["ones1"],
    }
    x = np.ascontiguousarray(L0_attn_x, dtype=np.float32)
    in_maps = [dict(shared, x=np.ascontiguousarray(x[b])) for b in range(8)]
    res = run_bass_kernel_spmd(nc, in_maps, core_ids=list(range(8)))
    return np.stack([np.asarray(r["out"], dtype=np.float32) for r in res.results], axis=0)


# version 13 (final)
# speedup vs baseline: 1.5214x; 1.1048x over previous
import math
import numpy as np
import ml_dtypes
import concourse.bass as bass
import concourse.mybir as mybir
from concourse.bass_utils import run_bass_kernel_spmd
from contextlib import ExitStack

F32 = mybir.dt.float32
BF16 = mybir.dt.bfloat16
AF = mybir.ActivationFunctionType
ALU = mybir.AluOpType
AX = mybir.AxisListType

D = 1024
S = 2048
NH = 16
NT = 16
ND = 8
NG = 4
GH = 4
LN_EPS = 1e-5
RMS_EPS = 1e-6

_CONSTS = None


def _host_consts():
    global _CONSTS
    if _CONSTS is None:
        n = np.arange(D)
        nk = (np.outer(n, n) % D).astype(np.float64)
        theta = 2.0 * np.pi * nk / D
        Cm = (np.cos(theta) / 32.0).astype(ml_dtypes.bfloat16)
        Sm = (np.sin(theta) / 32.0).astype(ml_dtypes.bfloat16)
        ident = np.eye(128, dtype=ml_dtypes.bfloat16)
        tri = (np.arange(128)[None, :] >= np.arange(128)[:, None]).astype(ml_dtypes.bfloat16)
        e2 = np.zeros((128, 2), dtype=ml_dtypes.bfloat16)
        e2[:64, 0] = 1
        e2[64:, 1] = 1
        e2t = np.ascontiguousarray(e2.T)
        e16 = np.zeros((16, ND * 128), dtype=ml_dtypes.bfloat16)
        for c in range(ND):
            for p in range(128):
                e16[2 * c + p // 64, c * 128 + p] = 1
        ones1 = np.ones((1, 128), dtype=ml_dtypes.bfloat16)
        _CONSTS = dict(Cm=Cm, Sm=Sm, ident=ident, tri=tri, e2=e2, e2t=e2t,
                       e16=e16, ones1=ones1)
    return _CONSTS


DMA_KEYS = ("gpd", "spd")
ALL_KEYS = ("pe", "act", "dve", "gpc", "gpd", "spd")


class Ref:
    __slots__ = ("key", "idx")

    def __init__(self, key, idx):
        self.key = key
        self.idx = idx


class Prog:
    def __init__(self):
        self.ops = {k: [] for k in ALL_KEYS}
        self.needed = {k: set() for k in ALL_KEYS}

    def add(self, key, emit_fn, deps=(), war=()):
        kept = []
        for d in deps:
            if d is None:
                continue
            if d.key == key:
                assert d.idx < len(self.ops[key]), "same-engine forward dep"
            self.needed[d.key].add(d.idx)
            kept.append(d)
        for d in war:
            if d is None:
                continue
            if d.key == key:
                assert d.idx < len(self.ops[key]), "same-engine forward dep"
                continue
            self.needed[d.key].add(d.idx)
            kept.append(d)
        ref = Ref(key, len(self.ops[key]))
        self.ops[key].append((emit_fn, kept))
        return ref

    def finalize(self):
        self.val_after = {}
        for k in ALL_KEYS:
            vals, tot = [], 0
            for i in range(len(self.ops[k])):
                if k in DMA_KEYS:
                    tot += 16
                elif i in self.needed[k]:
                    tot += 1
                vals.append(tot)
            self.val_after[k] = vals

    def emit(self, key, eng, sems):
        last_wait = {k: 0 for k in ALL_KEYS}
        for i, (emit_fn, deps) in enumerate(self.ops[key]):
            for d in deps:
                v = self.val_after[d.key][d.idx]
                if v > last_wait[d.key]:
                    eng.wait_ge(sems[d.key], v)
                    last_wait[d.key] = v
            inst = emit_fn(eng)
            if key in DMA_KEYS:
                inst.then_inc(sems[key], 16)
            elif i in self.needed[key]:
                inst.then_inc(sems[key], 1)

    def last_refs(self):
        return [Ref(k, len(self.ops[k]) - 1) for k in ALL_KEYS if self.ops[k]]


class Slot:

    def __init__(self):
        self.writer = None
        self.readers = []

    def rdeps(self):
        return [self.writer] if self.writer else []

    def wdeps(self):
        return self.readers + ([self.writer] if self.writer else [])

    def read(self, ref):
        self.readers.append(ref)

    def wrote(self, ref):
        self.writer = ref
        self.readers = []

    def wrote_cont(self, ref):
        self.writer = ref


def build_program(taps=(), stop_after=None):
    nc = bass.Bass("TRN2", target_bir_lowering=False)
    from concourse.hw_specs import get_activation_tables
    _tables = list(get_activation_tables(nc.m.arch).keys())
    SET_NLEXP = _tables.index("natural_log_exp_and_others")
    SET_SILU = _tables.index("silu_and_others")

    d_x = nc.dram_tensor("x", [S, D], F32, kind="ExternalInput").ap()
    d_w = {w: nc.dram_tensor(w, [D, D], F32, kind="ExternalInput").ap()
           for w in ("wqa", "wka", "wqp", "wkp", "wv", "wo")}
    d_sls = nc.dram_tensor("sls", [NH], F32, kind="ExternalInput").ap()
    d_wfr = nc.dram_tensor("wfr", [D, D], F32, kind="ExternalInput").ap()
    d_wfi = nc.dram_tensor("wfi", [D, D], F32, kind="ExternalInput").ap()
    d_bias = nc.dram_tensor("bias", [D], F32, kind="ExternalInput").ap()
    d_C = nc.dram_tensor("Cm", [D, D], BF16, kind="ExternalInput").ap()
    d_S = nc.dram_tensor("Sm", [D, D], BF16, kind="ExternalInput").ap()
    d_ident = nc.dram_tensor("identc", [128, 128], BF16, kind="ExternalInput").ap()
    d_tri = nc.dram_tensor("tric", [128, 128], BF16, kind="ExternalInput").ap()
    d_e2 = nc.dram_tensor("e2c", [128, 2], BF16, kind="ExternalInput").ap()
    d_e2t = nc.dram_tensor("e2tc", [2, 128], BF16, kind="ExternalInput").ap()
    d_e16 = nc.dram_tensor("e16c", [16, ND * 128], BF16, kind="ExternalInput").ap()
    d_ones1 = nc.dram_tensor("ones1c", [1, 128], BF16, kind="ExternalInput").ap()
    d_out = nc.dram_tensor("out", [S, D], F32, kind="ExternalOutput").ap()
    d_cpsc = nc.dram_tensor("cpsc", [NH], F32, kind="Internal").ap()

    P = Prog()
    tap_done = set()

    with ExitStack() as ctx:
        sb = lambda n, s, d: ctx.enter_context(nc.sbuf_tensor(n, s, d))
        psa = lambda n, s: ctx.enter_context(nc.psum_tensor(n, s, F32))

        x_bf = sb("x_bf", [128, NT * D], BF16)
        hfm = sb("hfm", [128, ND * S], BF16)
        identb = sb("identb", [128, 128], BF16)
        trib = sb("trib", [128, 128], BF16)
        e2b = sb("e2b", [128, 2], BF16)
        e2tb = sb("e2tb", [2, 128], BF16)
        e16b = sb("e16b", [16, ND * 128], BF16)
        ones1b = sb("ones1b", [1, 128], BF16)
        slsrow = sb("slsrow", [1, NH], F32)
        crow8 = sb("crow8", [1, NH], F32)
        mnegrow = sb("mnegrow", [1, NH], BF16)
        mneg_bc = sb("mneg_bc", [128, NH], F32)
        cpair = sb("cpair", [2, ND], F32)
        ccol = sb("ccol", [128, 4], F32)
        stg = sb("stg", [65, 1024], BF16)
        ln_s = sb("ln_s", [128, D], F32)
        sq_s = sb("sq_s", [128, D], BF16)
        htm_s = sb("htm", [128, 2 * D], BF16)
        stat8 = sb("stat8", [128, 8], F32)

        pAo = psa("pAo", [128, 1024])
        pAs = psa("pAs", [128, 1024])
        pB = psa("pB", [128, 1024])
        pC = psa("pC", [128, 1024])

        sems = {k: ctx.enter_context(nc.semaphore("sem_" + k)) for k in ALL_KEYS}

        sl = dict(pA0=Slot(), pA1=Slot(), pA2=Slot(), pA3=Slot(),
                  pB0=Slot(), pB1=Slot(), pC0=Slot(), pC1=Slot(),
                  x_bf=Slot(), hfm=Slot(), stg=Slot(),
                  io=Slot(), ln=Slot(), htm=Slot(), stat=Slot())
        PAall = [sl["pA0"], sl["pA1"], sl["pA2"], sl["pA3"]]
        PBf = [sl["pB0"], sl["pB1"]]
        PCf = [sl["pC0"], sl["pC1"]]

        def op(key, fn, reads=(), writes=(), writes_cont=(), also=()):
            deps = list(also)
            war = []
            for s in reads:
                deps += s.rdeps()
            for s in writes:
                war += s.wdeps()
            ref = P.add(key, fn, deps, war=war)
            for s in reads:
                s.read(ref)
            for s in writes:
                s.wrote(ref)
            for s in writes_cont:
                s.wrote_cont(ref)
            return ref

        def act_load(set_id):
            return P.add("act", lambda e, sid=set_id: e.add_instruction(
                mybir.InstLoadActFuncSet(
                    name=e.bass.get_next_instruction_name(),
                    act_func_set_id=sid, ins=[], outs=[])))

        dbg_outs = {}

        def tap(name, ap, also=()):
            if name not in taps or name in tap_done:
                return
            tap_done.add(name)
            shp = [dd[1] for dd in ap.ap]
            dt = ap.tensor.dtype if hasattr(ap.tensor, "dtype") else F32
            d_dbg = nc.dram_tensor("dbg_" + name, shp, dt, kind="ExternalOutput").ap()
            dbg_outs[name] = d_dbg
            P.add("spd", lambda e, _a=ap, _d=d_dbg: e.dma_start(_d, _a), list(also))

        r_ident = P.add("spd", lambda e: e.dma_start(identb[:, :], d_ident))
        r_tri = P.add("spd", lambda e: e.dma_start(trib[:, :], d_tri))
        r_e2 = P.add("spd", lambda e: e.dma_start(e2b[:, :], d_e2))
        r_e2t = P.add("spd", lambda e: e.dma_start(e2tb[:, :], d_e2t))
        r_e16 = P.add("spd", lambda e: e.dma_start(e16b[:, :], d_e16))
        r_ones1 = P.add("spd", lambda e: e.dma_start(ones1b[:, :], d_ones1))
        r_sls = P.add("spd", lambda e: e.dma_start(slsrow[:, :], d_sls.rearrange("(a b) -> a b", a=1)))

        P.add("dve", lambda e: e.memset(ccol[:, 0:1], 1.0))
        P.add("dve", lambda e: e.memset(ccol[:, 1:2], RMS_EPS))
        P.add("dve", lambda e: e.memset(ccol[:, 2:3], LN_EPS))
        r_ccol = P.add("dve", lambda e: e.memset(ccol[:, 3:4], -math.pi / 2))

        act_load(SET_NLEXP)
        r_crow = P.add("act", lambda e: e.activation(crow8[:, :], slsrow[:, :], AF.Exp), [r_sls])
        r_crow = P.add("dve", lambda e: e.tensor_scalar_mul(crow8[:, :], crow8[:, :], 0.125), [r_crow])
        r_mneg = P.add("dve", lambda e: e.tensor_scalar_mul(mnegrow[:, :], crow8[:, :], -64.0), [r_crow])
        r_mm = op("pe", lambda e: e.matmul(pB[:, 0:NH], lhsT=ones1b[:, :], rhs=mnegrow[:, :],
                                           start=True, stop=True),
                  writes=[sl["pB0"]], also=[r_mneg, r_ones1])
        r_mnegbc = op("dve", lambda e: e.tensor_copy(mneg_bc[:, :], pB[:, 0:NH]),
                      reads=[sl["pB0"]])
        r_cpd = P.add("gpd", lambda e: e.dma_start(d_cpsc.rearrange("(a b) -> a b", a=1), crow8[:, :]), [r_crow])
        r_cpair = P.add("gpd", lambda e: e.dma_start(
            cpair[:, :], d_cpsc.rearrange("(cc p) -> p cc", p=2),
            allow_slow_non_contiguous=True), [r_cpd])

        r_xload = [P.add("gpd", lambda e, t=t: e.dma_start(
            x_bf[:, t * D:(t + 1) * D], d_x[t * 128:(t + 1) * 128, :])) for t in range(NT)]

        def emit_ln_stats_apply(src_ap, dst_ap, src_slot, dst_slot, par, extra=()):
            sb_ = 4 * par
            st = lambda i: stat8[:, sb_ + i:sb_ + i + 1]
            r1 = op("dve", lambda e: e.reduce_sum(st(0), src_ap, axis=AX.X),
                    reads=[src_slot], writes=[sl["stat"]], also=list(extra))
            r2 = op("dve", lambda e: e.tensor_scalar_mul(st(0), st(0), 1.0 / D),
                    writes=[sl["stat"]], also=[r1])
            r3 = op("act", lambda e: e.activation(
                sq_s[:, :], src_ap, AF.Square, accum_out=st(1)),
                reads=[src_slot], writes=[sl["stat"]], also=list(extra))
            r4 = op("dve", lambda e: e.tensor_tensor(st(3), st(0), st(0), ALU.mult),
                    writes=[sl["stat"]], also=[r2])
            r5 = op("dve", lambda e: e.scalar_tensor_tensor(
                st(2), st(1), 1.0 / D, st(3), ALU.mult, ALU.subtract),
                writes=[sl["stat"]], also=[r3, r4])
            r6 = op("act", lambda e: e.activation(st(2), st(2), AF.Ln, bias=ccol[:, 2:3]),
                    writes=[sl["stat"]], also=[r5, r_ccol])
            r7 = op("act", lambda e: e.activation(st(2), st(2), AF.Exp, scale=-0.5),
                    writes=[sl["stat"]], also=[r6])
            r8 = op("dve", lambda e: e.tensor_scalar(
                dst_ap, src_ap, st(0), st(2), ALU.subtract, ALU.mult),
                reads=[src_slot, sl["stat"]], writes=[dst_slot], also=[r7])
            return r8

        def emit_transpose(src_tile_fn, dst_tensor, t, src_slot, dst_slot,
                           ptiles=None, extra=()):
            if ptiles is None:
                ptiles = [(pB, sl["pB0"], 0), (pC, sl["pC0"], 0)]
            for half in range(2):
                ptile, pslot, poff = ptiles[half]
                for q in range(4):
                    dc = half * 4 + q
                    op("pe", lambda e, dc=dc, q=q, ptile=ptile, poff=poff: e.matmul(
                        ptile[:, poff + q * 128: poff + (q + 1) * 128],
                        lhsT=src_tile_fn(dc), rhs=identb[:, :], start=True, stop=True),
                        reads=[src_slot],
                        writes=[pslot] if q == 0 else [],
                        writes_cont=[pslot] if q > 0 else [],
                        also=list(extra) + ([r_ident] if q == 0 else []))
                dst = bass.AP(dst_tensor[:, :].tensor, (half * 4) * S + t * 128,
                              [dst_tensor[:, :].ap[0], [S, 4], [1, 128]])
                op("act", lambda e, dst=dst, ptile=ptile, poff=poff: e.activation(
                    dst, ptile[:, poff:poff + 512], AF.Copy),
                    reads=[pslot], writes=[dst_slot])

        def s00(t):
            par = t % 2
            hview = htm_s[:, par * D:(par + 1) * D]
            emit_ln_stats_apply(x_bf[:, t * D:(t + 1) * D], hview,
                                sl["x_bf"], sl["htm"], par, extra=[r_xload[t]])

        def s01(t):
            par = t % 2
            pt = [(pB, sl["pB0"], 0), (pB, sl["pB1"], 512)] if par == 0 else \
                 [(pC, sl["pC0"], 0), (pC, sl["pC1"], 512)]
            emit_transpose(lambda dc, par=par: htm_s[:, par * D + dc * 128: par * D + (dc + 1) * 128],
                           hfm, t, sl["htm"], sl["hfm"], ptiles=pt)

        for t in range(NT):
            s00(t)
            if t >= 1:
                s01(t - 1)
        s01(NT - 1)

        tap("h0fm", hfm[:, :], also=P.last_refs())
        if stop_after == "phase0":
            return _finish(nc, P, sems, dbg_outs)

        attn_ctx = ctx.enter_context(ExitStack())
        sba = lambda n, s, d: attn_ctx.enter_context(nc.sbuf_tensor(n, s, d))
        o_fm = sba("o_fm", [128, ND * S], BF16)
        Lb = sba("Lb", [NH, S], BF16)
        rlb = sba("rlb", [NH, S], BF16)
        Qcs = sba("Qcs", [128, GH * S], BF16)
        Kcs = sba("Kcs", [128, GH * S], BF16)
        V_aug = sba("V_aug", [128, NT * GH * 65], BF16)
        WTa = sba("WTa", [128, ND * 256], BF16)
        WTb = sba("WTb", [128, ND * 256], BF16)
        WTv = sba("WTv", [128, ND * 256], BF16)
        Wstage = sba("Wstage", [128, 2 * D], BF16)
        qx = sba("qx", [128, 1024], BF16)
        Pt = sba("Pt", [128, 3072], BF16)
        t14 = sba("t14", [128, S], BF16)
        qa_s = sba("qa_s", [128, 1536], BF16)
        q2_s = sba("q2_s", [128, 1024], BF16)
        t_s = sba("t_s", [128, 1024], BF16)
        tn_s = sba("tn_s", [128, 1024], BF16)
        sinv_s = sba("sinv_s", [128, 1024], BF16)
        cosn_s = sba("cosn_s", [128, 1024], BF16)
        rstd2b_s = sba("rstd2b_s", [2, 1024], BF16)
        ln2_s = sba("ln2_s", [128, D], F32)

        for nm in ("Qcs", "Kcs", "V_aug", "Wstage", "qx0", "qx1", "Pt0", "Pt1", "Pt2",
                   "qa_s0", "qa_s1", "qa_s2", "q2_s0", "q2_s1", "t_s0", "t_s1",
                   "tn_s0", "tn_s1", "sinv_s0", "sinv_s1", "cosn_s0", "cosn_s1",
                   "t14", "rstd20", "rstd21",
                   "WTa", "WTb", "WTv", "o_fm", "Lb", "rlb", "ln2"):
            sl[nm] = Slot()

        va = V_aug[:, :]
        va_ones = bass.AP(va.tensor, 64, [va.ap[0], [260, NT], [65, GH], [0, 1]])
        op("dve", lambda e: e.memset(va_ones, 1.0), writes=[sl["V_aug"]])

        def emit_wslice(g, w, WTdst, dname):
            rl_ = []
            for fc in range(2):
                rl_.append(op("gpd", lambda e, w=w, fc=fc, g=g: e.dma_start(
                    Wstage[:, fc * D:(fc + 1) * D],
                    d_w[w][g * 256 + fc * 128: g * 256 + (fc + 1) * 128, :]),
                    writes=[sl["Wstage"]]))
            for grp4 in range(4):
                ptile, pslot = (pB, sl["pB0"]) if grp4 % 2 == 0 else (pC, sl["pC0"])
                for pos in range(4):
                    idx = grp4 * 4 + pos
                    dc, fc = idx // 2, idx % 2
                    op("pe", lambda e, dc=dc, fc=fc, pos=pos, ptile=ptile: e.matmul(
                        ptile[:, pos * 128:(pos + 1) * 128],
                        lhsT=Wstage[:, fc * D + dc * 128: fc * D + (dc + 1) * 128],
                        rhs=identb[:, :], start=True, stop=True),
                        reads=[sl["Wstage"]],
                        writes=[pslot] if pos == 0 else [],
                        writes_cont=[pslot] if pos > 0 else [],
                        also=rl_ if idx == 0 else [])
                op("dve", lambda e, grp4=grp4, ptile=ptile, WTdst=WTdst: e.tensor_copy(
                    WTdst[:, grp4 * 512:(grp4 + 1) * 512], ptile[:, 0:512]),
                    reads=[pslot], writes=[sl[dname]])

        def pipe3(s0, s1, s2, nj=4):
            for j in range(nj + 2):
                if j < nj:
                    s0(j)
                if 1 <= j <= nj:
                    s1(j - 1)
                if j >= 2:
                    s2(j - 2)

        def emit_proj_pair(g, WTamp, aname, WTphi, pname, dst, dst_name, scale_c):
            for c in range(2):
                act_load(SET_NLEXP)

                def tileX(js):
                    return (pB, sl["pB0"], sl["pB1"]) if js % 2 == 0 else (pC, sl["pC0"], sl["pC1"])

                def aview(js):
                    p3 = js % 3
                    return qa_s[:, p3 * 512:(p3 + 1) * 512], sl["qa_s%d" % p3]

                def q2view(js):
                    p2 = js % 2
                    return q2_s[:, p2 * 512:(p2 + 1) * 512], sl["q2_s%d" % p2]

                def rbview(js):
                    p2 = js % 2
                    return rstd2b_s[:, p2 * 512:(p2 + 1) * 512], sl["rstd2%d" % p2]

                def sA0(js):
                    tile, X0, X1 = tileX(js)
                    qa, qsl = aview(js)
                    q2, q2sl = q2view(js)
                    for dc in range(ND):
                        op("pe", lambda e, dc=dc, c=c, js=js, tile=tile, WTamp=WTamp: e.matmul(
                            tile[:, 0:512],
                            lhsT=WTamp[:, dc * 256 + c * 128: dc * 256 + (c + 1) * 128],
                            rhs=hfm[:, dc * S + js * 512: dc * S + (js + 1) * 512],
                            start=(dc == 0), stop=(dc == ND - 1)),
                            reads=[sl[aname], sl["hfm"]],
                            writes=[X0] if dc == 0 else [],
                            writes_cont=[X0] if dc > 0 else [])
                    r_sp = op("act", lambda e, tile=tile, qa=qa: e.activation(
                        qa, tile[:, 0:512], AF.Exp),
                       reads=[X0], writes=[qsl])
                    op("act", lambda e, qa=qa: e.activation(qa, qa, AF.Ln, bias=ccol[:, 0:1]),
                       writes=[qsl], also=[r_ccol, r_sp])
                    op("dve", lambda e, qa=qa, q2=q2: e.tensor_mul(q2, qa, qa),
                       reads=[qsl], writes=[q2sl])

                def sA1(js):
                    tile, X0, X1 = tileX(js)
                    q2, q2sl = q2view(js)
                    rb, rsl = rbview(js)
                    op("pe", lambda e, tile=tile, q2=q2: e.matmul(
                        tile[0:2, 512:1024], lhsT=e2b[:, :], rhs=q2,
                        start=True, stop=True),
                       reads=[q2sl], writes=[X1], also=[r_e2])
                    r_rln = op("act", lambda e, tile=tile, rb=rb: e.activation(
                        rb, tile[0:2, 512:1024], AF.Ln,
                        scale=1.0 / 64.0, bias=ccol[0:2, 1:2]),
                       reads=[X1], writes=[rsl])
                    op("act", lambda e, rb=rb: e.activation(rb, rb, AF.Exp, scale=-0.5),
                       writes=[rsl], also=[r_rln])
                    if scale_c:
                        op("dve", lambda e, g=g, c=c, rb=rb: e.tensor_scalar_mul(
                            rb, rb, cpair[:, g * 2 + c: g * 2 + c + 1]),
                           writes=[rsl], also=[r_cpair])

                def sA2(js):
                    tile, X0, X1 = tileX(js)
                    qa, qsl = aview(js)
                    rb, rsl = rbview(js)
                    op("pe", lambda e, tile=tile, rb=rb: e.matmul(
                        tile[:, 512:1024], lhsT=e2tb[:, :], rhs=rb,
                        start=True, stop=True),
                       reads=[rsl], writes=[X1], also=[r_e2t])
                    op("dve", lambda e, js=js, tile=tile, qa=qa: e.tensor_tensor(
                        t14[:, js * 512:(js + 1) * 512], qa, tile[:, 512:1024], ALU.mult),
                       reads=[qsl, X1], writes=[sl["t14"]])

                pipe3(sA0, sA1, sA2)

                act_load(SET_SILU)

                def tview(js):
                    p2 = js % 2
                    return t_s[:, p2 * 512:(p2 + 1) * 512], sl["t_s%d" % p2]

                def tnview(js):
                    p2 = js % 2
                    return tn_s[:, p2 * 512:(p2 + 1) * 512], sl["tn_s%d" % p2]

                def svview(js):
                    p2 = js % 2
                    return (sinv_s[:, p2 * 512:(p2 + 1) * 512], sl["sinv_s%d" % p2],
                            cosn_s[:, p2 * 512:(p2 + 1) * 512], sl["cosn_s%d" % p2])

                def sB0(js):
                    tile, X0, X1 = tileX(js)
                    ts, tsl = tview(js)
                    tn, tnsl = tnview(js)
                    for dc in range(ND):
                        op("pe", lambda e, dc=dc, c=c, js=js, tile=tile, WTphi=WTphi: e.matmul(
                            tile[:, 0:512],
                            lhsT=WTphi[:, dc * 256 + c * 128: dc * 256 + (c + 1) * 128],
                            rhs=hfm[:, dc * S + js * 512: dc * S + (js + 1) * 512],
                            start=(dc == 0), stop=(dc == ND - 1)),
                            reads=[sl[pname], sl["hfm"]],
                            writes=[X0] if dc == 0 else [],
                            writes_cont=[X0] if dc > 0 else [])
                    op("act", lambda e, tile=tile, ts=ts: e.activation(ts, tile[:, 0:512], AF.Tanh),
                       reads=[X0], writes=[tsl])
                    op("dve", lambda e, ts=ts, tn=tn: e.tensor_scalar_mul(tn, ts, -1.0),
                       reads=[tsl], writes=[tnsl])
                    op("dve", lambda e, ts=ts, tn=tn: e.tensor_max(tn, ts, tn),
                       reads=[tsl], writes=[tnsl])

                def sB1(js):
                    ts, tsl = tview(js)
                    tn, tnsl = tnview(js)
                    sv, svsl, cn, cnsl = svview(js)
                    op("act", lambda e, ts=ts, sv=sv: e.activation(sv, ts, AF.Sin, scale=math.pi),
                       reads=[tsl], writes=[svsl])
                    op("act", lambda e, tn=tn, cn=cn: e.activation(cn, tn, AF.Sin,
                                                                   scale=math.pi, bias=ccol[:, 3:4]),
                       reads=[tnsl], writes=[cnsl])

                def sB2(js):
                    sv, svsl, cn, cnsl = svview(js)
                    p2 = js % 2
                    qxv = qx[:, p2 * 512:(p2 + 1) * 512]
                    qxsl = sl["qx%d" % p2]
                    lo0 = (2 * c) * S + js * 512
                    lo1 = (2 * c + 1) * S + js * 512
                    t1lo = t14[0:64, js * 512:(js + 1) * 512]
                    t1hi = t14[64:128, js * 512:(js + 1) * 512]
                    op("dve", lambda e, lo0=lo0, t1lo=t1lo, cn=cn: e.scalar_tensor_tensor(
                        dst[0:64, lo0:lo0 + 512], cn[0:64, :], -1.0, t1lo,
                        ALU.mult, ALU.mult),
                       reads=[cnsl, sl["t14"]], writes=[sl[dst_name]])
                    op("dve", lambda e, qxv=qxv, t1hi=t1hi, cn=cn: e.scalar_tensor_tensor(
                        qxv[64:128, :], cn[64:128, :], -1.0,
                        t1hi, ALU.mult, ALU.mult),
                       reads=[cnsl, sl["t14"]], writes=[qxsl])
                    op("dve", lambda e, qxv=qxv, t1lo=t1lo, sv=sv: e.tensor_mul(
                        qxv[0:64, :], t1lo, sv[0:64, :]),
                       reads=[svsl, sl["t14"]], writes=[qxsl])
                    op("dve", lambda e, lo1=lo1, t1hi=t1hi, sv=sv: e.tensor_mul(
                        dst[64:128, lo1:lo1 + 512], t1hi, sv[64:128, :]),
                       reads=[svsl, sl["t14"]], writes=[sl[dst_name]])
                    op("gpd", lambda e, c=c, js=js, qxv=qxv: e.dma_start(
                        dst[0:64, (2 * c + 1) * S + js * 512: (2 * c + 1) * S + (js + 1) * 512],
                        qxv[64:128, :]),
                       reads=[qxsl], writes=[sl[dst_name]])
                    op("gpd", lambda e, c=c, js=js, qxv=qxv: e.dma_start(
                        dst[64:128, (2 * c) * S + js * 512: (2 * c) * S + (js + 1) * 512],
                        qxv[0:64, :]),
                       reads=[qxsl], writes=[sl[dst_name]])

                pipe3(sB0, sB1, sB2)

        def emit_vproj(g):
            for t in range(NT):
                tile, pslot = (pB, sl["pB0"]) if t % 2 == 0 else (pC, sl["pC0"])
                for dc in range(ND):
                    op("pe", lambda e, dc=dc, t=t, tile=tile: e.matmul(
                        tile[:, 0:256], lhsT=hfm[:, dc * S + t * 128: dc * S + (t + 1) * 128],
                        rhs=WTv[:, dc * 256:(dc + 1) * 256],
                        start=(dc == 0), stop=(dc == ND - 1)),
                        reads=[sl["hfm"], sl["WTv"]],
                        writes=[pslot] if dc == 0 else [],
                        writes_cont=[pslot] if dc > 0 else [])
                dstp = bass.AP(va.tensor, t * 260, [va.ap[0], [65, GH], [1, 64]])
                op("dve", lambda e, dstp=dstp, tile=tile: e.tensor_copy(
                    dstp, tile[:, 0:256].rearrange("p (a b) -> p a b", a=GH)),
                   reads=[pslot], writes=[sl["V_aug"]])

        def emit_attention(g):
            act_load(SET_NLEXP)
            PAo = [sl["pA0"], sl["pA1"]]
            stiles = [(pB, sl["pB0"], sl["pB1"], 0),
                      (pC, sl["pC0"], sl["pC1"], 0),
                      (pAs, sl["pA2"], sl["pA3"], 0)]
            ptslots = (sl["Pt0"], sl["Pt1"], sl["Pt2"])
            pairs = []
            for hl in range(GH):
                for qh in range(2):
                    end = 1024 * (qh + 1)
                    for ki in range(NT):
                        if 128 * ki >= end:
                            break
                        qs0 = max(128 * ki, 1024 * qh)
                        blocks = [(qs0, min(end, (qs0 // 512 + 1) * 512) - qs0)]
                        if qs0 + blocks[0][1] < end:
                            blocks.append((qs0 + blocks[0][1], 512))
                        last = (qh == 0 and ki == 7) or (qh == 1 and ki == NT - 1)
                        pairs.append((hl, qh, ki, blocks, last))
            exp_ref = {}

            def emit_st_exp(p):
                hl, qh, ki, blocks, last = pairs[p]
                h = g * GH + hl
                si = p % 3
                tile, X0, X1, tbase = stiles[si]
                ptbase = 1024 * si
                w0 = blocks[0][1]
                lo = 512 - w0
                hi = 512 + (512 if len(blocks) > 1 else 0)
                for bi, (qs, w) in enumerate(blocks):
                    poff = 512 - w0 if bi == 0 else 512
                    xslot = X0 if bi == 0 else X1
                    op("pe", lambda e, hl=hl, ki=ki, qs=qs, w=w, poff=poff,
                       tile=tile, tbase=tbase: e.matmul(
                        tile[:, tbase + poff: tbase + poff + w],
                        lhsT=Kcs[:, hl * S + ki * 128: hl * S + (ki + 1) * 128],
                        rhs=Qcs[:, hl * S + qs: hl * S + qs + w], start=True, stop=True),
                        reads=[sl["Kcs"], sl["Qcs"]], writes=[xslot])
                r_exp = op("act", lambda e, lo=lo, hi=hi, h=h, ptbase=ptbase,
                           tile=tile, tbase=tbase: e.activation(
                    Pt[:, ptbase + lo: ptbase + hi], tile[:, tbase + lo: tbase + hi],
                    AF.Exp, bias=mneg_bc[:, h:h + 1]),
                    reads=[X0] + ([X1] if len(blocks) > 1 else []),
                    writes=[ptslots[si]], also=[r_mnegbc])
                r_last = r_exp
                if blocks[0][0] == 128 * ki:
                    r_last = op("dve", lambda e, lo=lo, ptbase=ptbase: e.tensor_mul(
                        Pt[:, ptbase + lo: ptbase + lo + 128],
                        Pt[:, ptbase + lo: ptbase + lo + 128], trib[:, :]),
                        writes=[ptslots[si]], also=[r_tri])
                exp_ref[p] = r_last

            def emit_ov(p):
                hl, qh, ki, blocks, last = pairs[p]
                h = g * GH + hl
                si = p % 3
                ptbase = 1024 * si
                w0 = blocks[0][1]
                for bi, (qs, w) in enumerate(blocks):
                    poff = 512 - w0 if bi == 0 else 512
                    gb = qs // 512
                    qs_l = qs - 1024 * qh
                    first_write = (ki == 0 and qs_l == 0)
                    op("pe", lambda e, ki=ki, hl=hl, qs_l=qs_l, w=w, poff=poff,
                       ptbase=ptbase, gb=gb: e.matmul(
                        pAo[0:65, qs_l:qs_l + w],
                        lhsT=V_aug[:, ki * 260 + hl * 65: ki * 260 + (hl + 1) * 65],
                        rhs=Pt[:, ptbase + poff: ptbase + poff + w],
                        start=(ki == 0), stop=(ki == 4 * gb + 3), skip_group_check=True),
                        reads=[ptslots[si], sl["V_aug"]],
                        writes=PAo if first_write else [],
                        writes_cont=PAo if not first_write else [],
                        also=[exp_ref[p]])
                if last:
                    r_ev = op("dve", lambda e: e.tensor_copy(stg[:, :], pAo[0:65, 0:1024]),
                              reads=PAo, writes=[sl["stg"]])
                    c, half = h // 2, h % 2
                    op("gpd", lambda e, c=c, half=half, qh=qh: e.dma_start(
                        o_fm[64 * half:64 * half + 64,
                             c * S + 1024 * qh: c * S + 1024 * qh + 1024], stg[0:64, :]),
                       reads=[sl["stg"]], writes=[sl["o_fm"]])
                    op("gpd", lambda e, h=h, qh=qh: e.dma_start(
                        Lb[h:h + 1, 1024 * qh: 1024 * qh + 1024], stg[64:65, :]),
                       reads=[sl["stg"]], writes=[sl["Lb"]])

            SKEW = 2
            for p in range(len(pairs) + SKEW):
                if p < len(pairs):
                    emit_st_exp(p)
                if p >= SKEW:
                    emit_ov(p - SKEW)

        for g in range(NG):
            emit_wslice(g, "wqa", WTa, "WTa")
            emit_wslice(g, "wqp", WTb, "WTb")
            if g == 0:
                emit_wslice(g, "wv", WTv, "WTv")
            emit_proj_pair(g, WTa, "WTa", WTb, "WTb", Qcs, "Qcs", True)
            emit_wslice(g, "wka", WTa, "WTa")
            emit_wslice(g, "wkp", WTb, "WTb")
            emit_proj_pair(g, WTa, "WTa", WTb, "WTb", Kcs, "Kcs", False)
            emit_vproj(g)
            if g < NG - 1:
                emit_wslice(g + 1, "wv", WTv, "WTv")
            if g == 0:
                tap("Qcs0", Qcs[:, :], also=P.last_refs())
                tap("Kcs0", Kcs[:, :], also=P.last_refs())
                tap("Vaug0", V_aug[:, :], also=P.last_refs())
            emit_attention(g)
            if g == 0:
                tap("stg0", stg[:, :], also=P.last_refs())
            if stop_after == "group%d" % g:
                tap("o_fm", o_fm[:, :], also=P.last_refs())
                tap("Lb", Lb[:, :], also=P.last_refs())
                return _finish(nc, P, sems, dbg_outs)

        tap("o_fm", o_fm[:, :], also=P.last_refs())
        tap("Lb", Lb[:, :], also=P.last_refs())

        def _emit_recip(e):
            with nc.allow_low_precision(reason="1/l stored bf16; same scale as bf16 matmul rounding"):
                return e.reciprocal(rlb[:, :], Lb[:, :])
        r_rec = op("dve", _emit_recip, reads=[sl["Lb"]], writes=[sl["rlb"]])
        sl["rlb"].readers.append(r_rec)
        for c in range(ND):
            for js in range(4):
                tile, pslot = (pB, sl["pB0"]) if (c * 4 + js) % 2 == 0 else (pC, sl["pC0"])
                op("pe", lambda e, c=c, js=js, tile=tile: e.matmul(
                    tile[:, 0:512], lhsT=e16b[:, c * 128:(c + 1) * 128],
                    rhs=rlb[:, js * 512:(js + 1) * 512], start=True, stop=True),
                   reads=[sl["rlb"]], writes=[pslot], also=[r_e16])
                op("dve", lambda e, c=c, js=js, tile=tile: e.tensor_tensor(
                    o_fm[:, c * S + js * 512: c * S + (js + 1) * 512],
                    o_fm[:, c * S + js * 512: c * S + (js + 1) * 512],
                    tile[:, 0:512], ALU.mult),
                   reads=[pslot], writes=[sl["o_fm"]])

        tap("o_norm", o_fm[:, :], also=P.last_refs())

        WoT = Qcs
        sl["WoT"] = sl["Qcs"]
        for fcpair in range(4):
            rl_ = []
            for k in range(2):
                foc = fcpair * 2 + k
                rl_.append(op("gpd", lambda e, foc=foc, k=k: e.dma_start(
                    Wstage[:, k * D:(k + 1) * D], d_w["wo"][foc * 128:(foc + 1) * 128, :]),
                    writes=[sl["Wstage"]]))
            for grp4 in range(4):
                ptile, pslot = (pB, sl["pB0"]) if grp4 % 2 == 0 else (pC, sl["pC0"])
                evs = []
                for pos in range(4):
                    idx = grp4 * 4 + pos
                    k, fc = idx // 8, idx % 8
                    op("pe", lambda e, k=k, fc=fc, pos=pos, ptile=ptile: e.matmul(
                        ptile[:, pos * 128:(pos + 1) * 128],
                        lhsT=Wstage[:, k * D + fc * 128: k * D + (fc + 1) * 128],
                        rhs=identb[:, :], start=True, stop=True),
                        reads=[sl["Wstage"]],
                        writes=[pslot] if pos == 0 else [],
                        writes_cont=[pslot] if pos > 0 else [],
                        also=rl_ if idx == 0 else [])
                    evs.append((k, fc, pos))
                for (k, fc, pos) in evs:
                    foc = fcpair * 2 + k
                    op("dve", lambda e, fc=fc, foc=foc, pos=pos, ptile=ptile: e.tensor_copy(
                        WoT[:, fc * D + foc * 128: fc * D + (foc + 1) * 128],
                        ptile[:, pos * 128:(pos + 1) * 128]),
                        reads=[pslot], writes=[sl["WoT"]])

        act_load(SET_NLEXP)

        def emit_postblock(t, ptile, pslots, ln_tile, ln_slot, out_dst, extra_dep=(), poff=0):
            par = t % 2
            sb_ = 4 * par
            st = lambda i: stat8[:, sb_ + i:sb_ + i + 1]
            r_ev = op("act", lambda e, ptile=ptile, ln_tile=ln_tile, poff=poff: e.activation(
                ln_tile[:, :], ptile[:, poff:poff + 1024], AF.Copy, accum_out=st(0)),
                reads=pslots, writes=[ln_slot, sl["stat"]], also=list(extra_dep))
            r2 = op("dve", lambda e: e.tensor_scalar_mul(st(0), st(0), 1.0 / D),
                    writes=[sl["stat"]], also=[r_ev])
            r3 = op("act", lambda e, ln_tile=ln_tile: e.activation(
                sq_s[:, :], ln_tile[:, :], AF.Square, accum_out=st(1)),
                reads=[ln_slot], writes=[sl["stat"]])
            r4 = op("dve", lambda e: e.tensor_tensor(st(3), st(0), st(0), ALU.mult),
                    writes=[sl["stat"]], also=[r2])
            r5 = op("dve", lambda e: e.scalar_tensor_tensor(
                st(2), st(1), 1.0 / D, st(3), ALU.mult, ALU.subtract),
                writes=[sl["stat"]], also=[r3, r4])
            r6 = op("act", lambda e: e.activation(st(2), st(2), AF.Ln, bias=ccol[:, 2:3]),
                    writes=[sl["stat"]], also=[r5])
            r7 = op("act", lambda e: e.activation(st(2), st(2), AF.Exp, scale=-0.5),
                    writes=[sl["stat"]], also=[r6])
            r8 = op("dve", lambda e, ln_tile=ln_tile: e.tensor_scalar(
                ln_tile[:, :], ln_tile[:, :], st(0), st(2), ALU.subtract, ALU.mult),
                reads=[sl["stat"]], writes=[ln_slot], also=[r7])
            return r8

        def ctile(t):
            k3 = t % 3
            if k3 == 0:
                return pB, PBf, 0
            if k3 == 1:
                return pC, PCf, 0
            return pAs, [sl["pA2"], sl["pA3"]], 0

        def sC0(t):
            ptile, pslots, poff = ctile(t)
            for js in range(2):
                for c in range(ND):
                    first_op = (js == 0 and c == 0)
                    op("pe", lambda e, c=c, js=js, t=t, ptile=ptile, poff=poff: e.matmul(
                        ptile[:, poff + js * 512: poff + (js + 1) * 512],
                        lhsT=o_fm[:, c * S + t * 128: c * S + (t + 1) * 128],
                        rhs=WoT[:, c * D + js * 512: c * D + (js + 1) * 512],
                        start=(c == 0), stop=(c == ND - 1)),
                        reads=[sl["o_fm"], sl["WoT"]],
                        writes=pslots if first_op else [],
                        writes_cont=pslots if not first_op else [])

        def sC1(t):
            ptile, pslots, poff = ctile(t)
            ln_tile = ln_s if t % 2 == 0 else ln2_s
            ln_slot = sl["ln"] if t % 2 == 0 else sl["ln2"]
            r8 = emit_postblock(t, ptile, pslots, ln_tile, ln_slot, None, poff=poff)
            op("dve", lambda e, t=t, ln_tile=ln_tile: e.tensor_tensor(
                x_bf[:, t * D:(t + 1) * D], x_bf[:, t * D:(t + 1) * D], ln_tile[:, :], ALU.add),
               reads=[ln_slot], writes=[sl["x_bf"]], also=[r8])
            par = t % 2
            hview = htm_s[:, par * D:(par + 1) * D]
            emit_ln_stats_apply(x_bf[:, t * D:(t + 1) * D], hview, sl["x_bf"], sl["htm"], par)

        def sC2(t):
            par = t % 2
            emit_transpose(lambda dc, par=par: htm_s[:, par * D + dc * 128: par * D + (dc + 1) * 128],
                           hfm, t, sl["htm"], sl["hfm"],
                           ptiles=[(pAo, sl["pA0"], 0), (pAo, sl["pA1"], 512)])

        pipe3(sC0, sC1, sC2, nj=NT)

        tap("x1", x_bf[:, :], also=P.last_refs())
        tap("h1fm", hfm[:, :], also=P.last_refs())
        if stop_after == "phaseC":
            return _finish(nc, P, sems, dbg_outs)

        barrier = P.last_refs()
        attn_ctx.close()
        sbm = lambda n, s, d: ctx.enter_context(nc.sbuf_tensor(n, s, d))
        Wr = sbm("Wr", [128, ND * D], BF16)
        Wi = sbm("Wi", [128, ND * D], BF16)
        Csb = sbm("Csb", [128, ND * D], BF16)
        Ssb = sbm("Ssb", [128, ND * D], BF16)
        A_T = sbm("A_T", [128, ND * D], BF16)
        B_Tn = sbm("B_Tn", [128, ND * D], BF16)
        Mb = sbm("Mb", [128, ND * D], BF16)
        lnd_s = sbm("lnd_s", [128, D], F32)
        io_stage = sbm("io_stage", [128, D], F32)
        io2 = sbm("io2", [128, D], F32)
        biasrow = sbm("biasrow", [1, D], BF16)
        r_biasrow = P.add("gpd", lambda e: e.dma_start(
            biasrow[:, :], d_bias.rearrange("(a b) -> a b", a=1)), barrier)
        for nm in ("Wr", "Wi", "Csb", "Ssb", "A_T", "B_Tn", "Mb", "lnd", "io2"):
            sl[nm] = Slot()

        for dc in range(ND):
            r = P.add("gpd", lambda e, dc=dc: e.dma_start(
                Wr[:, dc * D:(dc + 1) * D], d_wfr[dc * 128:(dc + 1) * 128, :]), barrier)
            sl["Wr"].wrote_cont(r) if dc else sl["Wr"].wrote(r)
            r = P.add("gpd", lambda e, dc=dc: e.dma_start(
                Wi[:, dc * D:(dc + 1) * D], d_wfi[dc * 128:(dc + 1) * 128, :]), barrier)
            sl["Wi"].wrote_cont(r) if dc else sl["Wi"].wrote(r)
            r = P.add("spd", lambda e, dc=dc: e.dma_start(
                Csb[:, dc * D:(dc + 1) * D], d_C[dc * 128:(dc + 1) * 128, :]), barrier)
            sl["Csb"].wrote_cont(r) if dc else sl["Csb"].wrote(r)
            r = P.add("spd", lambda e, dc=dc: e.dma_start(
                Ssb[:, dc * D:(dc + 1) * D], d_S[dc * 128:(dc + 1) * 128, :]), barrier)
            sl["Ssb"].wrote_cont(r) if dc else sl["Ssb"].wrote(r)

        def emit_MM_accum(dst, dst_name, lhs_list):
            nterm = len(lhs_list)
            for outc in range(ND):
                tile = pB if outc % 2 == 0 else pC
                Xs = PBf if outc % 2 == 0 else PCf
                for js in range(2):
                    for ti, (lt, lname, rt, rname) in enumerate(lhs_list):
                        for dc in range(ND):
                            first = (ti == 0 and dc == 0)
                            last = (ti == nterm - 1 and dc == ND - 1)
                            is_first_op = (js == 0 and first)
                            op("pe", lambda e, lt=lt, rt=rt, dc=dc, outc=outc, js=js,
                               first=first, last=last, tile=tile: e.matmul(
                                   tile[:, js * 512:(js + 1) * 512],
                                   lhsT=lt[:, dc * D + outc * 128: dc * D + (outc + 1) * 128],
                                   rhs=rt[:, dc * D + js * 512: dc * D + (js + 1) * 512],
                                   start=first, stop=last, skip_group_check=True),
                               reads=[sl[lname], sl[rname]],
                               writes=Xs if is_first_op else [],
                               writes_cont=Xs if not is_first_op else [])
                op("dve", lambda e, outc=outc, tile=tile: e.tensor_copy(
                    dst[:, outc * D:(outc + 1) * D], tile[:, :]),
                    reads=Xs, writes=[sl[dst_name]])

        emit_MM_accum(A_T, "A_T", [(Wr, "Wr", Csb, "Csb"), (Wi, "Wi", Ssb, "Ssb")])
        op("dve", lambda e: e.tensor_scalar_mul(Wi[:, :], Wi[:, :], -1.0), writes=[sl["Wi"]])
        emit_MM_accum(B_Tn, "B_Tn", [(Wr, "Wr", Ssb, "Ssb"), (Wi, "Wi", Csb, "Csb")])
        emit_MM_accum(Mb, "Mb", [(A_T, "A_T", Csb, "Csb"), (B_Tn, "B_Tn", Ssb, "Ssb")])
        tap("Mb", Mb[:, :], also=P.last_refs())

        for t in range(NT):
            k3 = t % 3
            ptile = (pB, pC, pAo)[k3]
            pslots = (PBf, PCf, [sl["pA0"], sl["pA1"]])[k3]
            for js in range(2):
                for dc in range(ND):
                    first_op = (js == 0 and dc == 0)
                    op("pe", lambda e, dc=dc, js=js, t=t, ptile=ptile: e.matmul(
                        ptile[:, js * 512:(js + 1) * 512],
                        lhsT=hfm[:, dc * S + t * 128: dc * S + (t + 1) * 128],
                        rhs=Mb[:, dc * D + js * 512: dc * D + (js + 1) * 512],
                        start=(dc == 0), stop=False, skip_group_check=True),
                        reads=[sl["hfm"], sl["Mb"]],
                        writes=pslots if first_op else [],
                        writes_cont=pslots if not first_op else [])
                op("pe", lambda e, js=js, ptile=ptile: e.matmul(
                    ptile[:, js * 512:(js + 1) * 512], lhsT=ones1b[:, :],
                    rhs=biasrow[:, js * 512:(js + 1) * 512],
                    start=False, stop=True, skip_group_check=True),
                    writes_cont=pslots, also=[r_biasrow, r_ones1])
            ln_tile = lnd_s if t % 2 == 0 else ln_s
            ln_slot = sl["lnd"] if t % 2 == 0 else sl["ln"]
            r8 = emit_postblock(t, ptile, pslots, ln_tile, ln_slot, None)
            io_t = io_stage if t % 2 == 0 else io2
            io_slot = sl["io"] if t % 2 == 0 else sl["io2"]
            op("dve", lambda e, t=t, ln_tile=ln_tile, io_t=io_t: e.tensor_tensor(
                io_t[:, :], x_bf[:, t * D:(t + 1) * D], ln_tile[:, :], ALU.add),
               reads=[ln_slot], writes=[io_slot], also=[r8])
            op("spd", lambda e, t=t, io_t=io_t: e.dma_start(
                d_out[t * 128:(t + 1) * 128, :], io_t[:, :]),
               reads=[io_slot])

        return _finish(nc, P, sems, dbg_outs)


def _finish(nc, P, sems, dbg_outs):
    P.finalize()
    with nc.Block() as block:
        @block.tensor
        def _(eng):
            P.emit("pe", eng, sems)

        @block.scalar
        def _(eng):
            P.emit("act", eng, sems)

        @block.vector
        def _(eng):
            P.emit("dve", eng, sems)

        @block.gpsimd
        def _(eng):
            P.emit("gpc", eng, sems)
            P.emit("gpd", eng, sems)
            if P.ops["gpd"]:
                eng.wait_ge(sems["gpd"], P.val_after["gpd"][-1])

        @block.sync
        def _(eng):
            P.emit("spd", eng, sems)
            if P.ops["spd"]:
                eng.wait_ge(sems["spd"], P.val_after["spd"][-1])

    return nc


def kernel(L0_attn_x, L0_attn_Wq_amp, L0_attn_Wk_amp, L0_attn_Wq_phi,
           L0_attn_Wk_phi, L0_attn_Wv, L0_attn_Wo, L0_attn_score_log_scale,
           L0_ffn_w_freq_real, L0_ffn_w_freq_imag, L0_ffn_bias):
    cst = _host_consts()
    nc = build_program()
    shared = {
        "wqa": np.ascontiguousarray(L0_attn_Wq_amp, dtype=np.float32),
        "wka": np.ascontiguousarray(L0_attn_Wk_amp, dtype=np.float32),
        "wqp": np.ascontiguousarray(L0_attn_Wq_phi, dtype=np.float32),
        "wkp": np.ascontiguousarray(L0_attn_Wk_phi, dtype=np.float32),
        "wv": np.ascontiguousarray(L0_attn_Wv, dtype=np.float32),
        "wo": np.ascontiguousarray(L0_attn_Wo, dtype=np.float32),
        "sls": np.ascontiguousarray(L0_attn_score_log_scale, dtype=np.float32),
        "wfr": np.ascontiguousarray(L0_ffn_w_freq_real, dtype=np.float32),
        "wfi": np.ascontiguousarray(L0_ffn_w_freq_imag, dtype=np.float32),
        "bias": np.ascontiguousarray(L0_ffn_bias, dtype=np.float32),
        "Cm": cst["Cm"], "Sm": cst["Sm"], "identc": cst["ident"],
        "tric": cst["tri"], "e2c": cst["e2"], "e2tc": cst["e2t"],
        "e16c": cst["e16"], "ones1c": cst["ones1"],
    }
    x = np.ascontiguousarray(L0_attn_x, dtype=np.float32)
    in_maps = [dict(shared, x=np.ascontiguousarray(x[b])) for b in range(8)]
    res = run_bass_kernel_spmd(nc, in_maps, core_ids=list(range(8)))
    return np.stack([np.asarray(r["out"], dtype=np.float32) for r in res.results], axis=0)
